# version 16
# baseline (speedup 1.0000x reference)
import numpy as np
import ml_dtypes
import concourse.bass as bass
import concourse.mybir as mybir
from concourse.bass_utils import run_bass_kernel_spmd

F32 = mybir.dt.float32
BF16 = mybir.dt.bfloat16
AF = mybir.ActivationFunctionType
ALU = mybir.AluOpType

D = 1024
S = 2048
NCHUNK = 4
CH = S // NCHUNK
NH = 16
HD = 64
NPAIR = 8
NKT = 16
CSHIFT = 6.0
EPS = 1e-5
MAGIC = 12582912.0
INTER = 4096
RANK = 16


class Step:
    __slots__ = ("eng", "sem", "inc", "fn", "deps", "need", "val", "tag")

    def __init__(self, eng, sem, inc, fn, deps, tag):
        self.eng, self.sem, self.inc, self.fn = eng, sem, inc, fn
        self.deps = [d for d in deps if d is not None]
        self.need = False
        self.val = None
        self.tag = tag


class Prog:
    QENG = {"wq": "sync", "sq": "scalar", "gq": "gpsimd"}

    def __init__(self):
        self.steps = []

    def _add(self, eng, sem, inc, fn, deps, tag):
        st = Step(eng, sem, inc, fn, deps, tag)
        for d in st.deps:
            d.need = True
        self.steps.append(st)
        return st

    def dma(self, q, fn, deps=(), tag=""):
        st = self._add(self.QENG[q], q, 16, fn, deps, tag)
        st.need = True
        return st

    def te(self, fn, deps=(), tag=""):
        return self._add("tensor", "te", 1, fn, deps, tag)

    def ve(self, fn, deps=(), tag=""):
        return self._add("vector", "ve", 1, fn, deps, tag)

    def ac(self, fn, deps=(), tag=""):
        return self._add("scalar", "ac", 1, fn, deps, tag)

    def po(self, fn, deps=(), tag=""):
        return self._add("gpsimd", "po", 1, fn, deps, tag)

    def finalize(self):
        cnt = {}
        for st in self.steps:
            if st.need:
                cnt[st.sem] = cnt.get(st.sem, 0) + st.inc
                st.val = cnt[st.sem]

    def replay(self, nc, sems):
        with nc.Block() as blk:
            for engname in ("sync", "vector", "scalar", "tensor", "gpsimd"):
                mysteps = [s for s in self.steps if s.eng == engname]

                def body(eng, mysteps=mysteps):
                    hwm = {}
                    for st in mysteps:
                        for d in st.deps:
                            if d.val is not None and hwm.get(d.sem, 0) < d.val:
                                eng.wait_ge(sems[d.sem], d.val)
                                hwm[d.sem] = d.val
                        inst = st.fn(eng)
                        if st.need:
                            inst.then_inc(sems[st.sem], st.inc)

                getattr(blk, engname)(body)


class Buf:

    def __init__(self):
        self.users = []
        self.writer = None

    def wdeps(self, extra=()):
        d = list(extra) + self.users
        self.users = []
        return d

    def wrote(self, step):
        self.writer = step
        return step

    def read(self, step):
        self.users.append(step)
        return step

    def peek(self):
        return list(self.users)


def _rope_tables():
    inv_freq = 1.0 / 10000.0 ** (np.arange(0, HD, 2, dtype=np.float32) / HD)
    t = np.arange(S, dtype=np.float32)
    fr = np.outer(t, inv_freq)
    cosT = np.cos(fr).T.astype(np.float32)
    sinT = np.sin(fr).T.astype(np.float32)
    T1 = np.concatenate([cosT, sinT, cosT, sinT], 0)
    T2 = np.concatenate([sinT, cosT, sinT, cosT], 0)
    return np.ascontiguousarray(T1), np.ascontiguousarray(T2)


def _rot_mats():
    R1 = np.zeros((128, 128), np.float32)
    R2 = np.zeros((128, 128), np.float32)
    for base in (0, 64):
        for r in range(32):
            R1[base + r, base + r] = 1.0
            R1[base + r, base + r + 32] = -1.0
            R2[base + 32 + r, base + r] = 1.0
            R2[base + 32 + r, base + 32 + r] = 1.0
    return np.ascontiguousarray(R1.T), np.ascontiguousarray(R2.T)


def _parity_perm():
    p = np.zeros(D, np.int64)
    for h in range(NH):
        b = h * HD
        p[b:b + 32] = b + np.arange(0, HD, 2)
        p[b + 32:b + HD] = b + np.arange(1, HD, 2)
    return p


def _indmat():
    ind = np.zeros((NH, D), np.float32)
    for h in range(NH):
        ind[h, h * HD:(h + 1) * HD] = 1.0
    return ind


def build_nc(n_layers=2, taps=()):
    nc = bass.Bass("TRN2", target_bir_lowering=False)

    def ein(name, shape):
        return nc.dram_tensor(name, list(shape), F32, kind="ExternalInput").ap()

    xin = ein("x", [128, NCHUNK * 4096])
    y = nc.dram_tensor("y", [128, NCHUNK * 4096], F32, kind="ExternalOutput").ap()

    wqT0 = ein("wqT0", [D, D]); wkT0 = ein("wkT0", [D, D]); wvT0 = ein("wvT0", [D, D])
    w1 = ein("w1", [D, INTER]); b1 = ein("b1", [128, 32])
    w2 = ein("w2", [INTER, D]); b2 = ein("b2", [128, 8])
    wqkvT = ein("wqkvT", [D, 3 * D])
    woutT = ein("woutT", [D, D]); bout = ein("bout", [128, 8])
    wfc = ein("wfc", [INTER, D]); wfcT = ein("wfcT", [D, INTER])
    wproj = ein("wproj", [D, INTER]); wprojT = ein("wprojT", [INTER, D])
    uT = ein("uT", [RANK, INTER]); vT = ein("vT", [D, RANK])
    fbT = ein("fbT", [RANK, RANK]); gate = ein("gate", [1])
    t1c = ein("t1c", [128, S]); t2c = ein("t2c", [128, S])
    r1T = ein("r1T", [128, 128]); r2T = ein("r2T", [128, 128])
    indmat = ein("indm", [NH, D])

    def scr(name, shape, dtype):
        return nc.dram_tensor(name, list(shape), dtype, kind="Internal").ap()

    x_scr = scr("x_scr", [128, NCHUNK * 4096], F32)
    xn_scr = scr("xn_scr", [128, NCHUNK * 4096], BF16)
    k_scr = scr("k_scr", [D, S], BF16)
    vp_scr = scr("vp_scr", [128, NPAIR * NKT * 132], BF16)
    den_scr = scr("den_scr", [NH, CH], F32)
    w1s = [scr("w1bf", [128, 8 * 4096], BF16), scr("tfcT", [128, 8 * 4096], BF16)]
    w2s = [scr("w2bf", [128, 8 * 4096], BF16), scr("tpjT", [128, 8 * 4096], BF16)]
    s1r_scr = scr("s1r_scr", [1, INTER], F32)
    qrow_scr = scr("qrow_scr", [NH, CH], F32)
    uts_scr = scr("uts_scr", [RANK, INTER], BF16)
    s2r_scr = scr("s2r_scr", [1, D], F32)

    tap_aps = {}
    for name, shape in taps:
        tap_aps[name] = nc.dram_tensor("tap_" + name, list(shape), F32,
                                       kind="ExternalOutput").ap()

    with nc.cleanup_on_exit():
        A = nc.alloc_sbuf_tensor
        xnb = [A("xnb0", [128, 8 * CH], BF16), A("xnb1", [128, 8 * CH], BF16)]
        xcb = A("xcb", [128, 8 * CH], F32)
        khaug = [A("kha0", [65, S], BF16), A("kha1", [65, S], BF16)]
        vhb = [A("vh0", [128, NKT * 132], BF16), A("vh1", [128, NKT * 132], BF16)]
        qc = A("qc", [128, 8 * CH], BF16)
        qaug = A("qaug", [65, 16 * CH], BF16)
        eabb = [A("eab0", [128, 1024], BF16), A("eab1", [128, 1024], BF16)]
        sc = A("sc", [128, 8 * CH], F32)
        hc = A("hc", [128, 32 * 512], BF16)
        w1cb = [A("w1c0", [128, 4096], BF16), A("w1c1", [128, 4096], BF16)]
        w2cb = [A("w2c0", [128, 4096], BF16), A("w2c1", [128, 4096], BF16)]
        stgb = [A("stg0", [128, 512], BF16), A("stg1", [128, 512], BF16)]
        vstg = A("vstg", [128, 528], BF16)
        m1t = A("m1t", [128, 512], BF16)
        m2t = A("m2t", [128, 512], BF16)
        zt = A("zt", [128, 512], F32)
        ut = A("ut", [128, 512], F32)
        ttt = A("ttt", [128, 512], F32)
        t1b = A("t1b", [128, S], BF16)
        t2b = A("t2b", [128, S], BF16)
        r1b = A("r1b", [128, 128], BF16)
        r2b = A("r2b", [128, 128], BF16)
        indb = A("indb", [NH, D], F32)
        macc256 = A("macc256", [128, 256], F32)
        mcols = A("mcols", [128, 64], F32)
        negLB = [A("negLB0", [128, 1], F32), A("negLB1", [128, 1], F32)]
        zero_col = A("zero_col", [128, 1], F32)
        tiny_col = A("tiny_col", [128, 1], F32)
        ones_col_bf = A("ones_col_bf", [128, 1], BF16)
        ones_row_f = A("ones_row_f", [1, 128], F32)
        one_col = A("one_col", [128, 1], F32)
        negC_col = A("negC_col", [128, 1], F32)
        eps1 = A("eps1", [1, 1], F32)
        b1c = A("b1c", [128, 32], F32)
        b2c = A("b2c", [128, 8], F32)
        boutc = A("boutc", [128, 8], F32)
        mrow = A("mrow", [1, CH], F32)
        msqrow = A("msqrow", [1, CH], F32)
        rstdrow = A("rstdrow", [1, CH], F32)
        mrrow = A("mrrow", [1, CH], F32)
        den16 = A("den16", [NH, CH], F32)
        uTsb = [A("uTs0", [RANK, 512], BF16), A("uTs1", [RANK, 512], BF16)]
        uTf = A("uTf", [RANK, 512], BF16)
        vTb = A("vTb", [128, 8 * RANK], BF16)
        fbTb = A("fbTb", [RANK, RANK], BF16)
        gcol = A("gcol", [RANK, 1], F32)
        graw = A("graw", [1, 1], F32)
        z1b = A("z1b", [RANK, CH], BF16)
        z2g = A("z2g", [RANK, CH], BF16)
        sclf_sq = A("sclf_sq", [128, 32], F32)
        sclp = A("sclp", [128, 8], F32)
        rcpf = A("rcpf", [128, 32], F32)
        rcpp = A("rcpp", [128, 8], F32)
        s1row = mrow
        s2row = msqrow
        rb512 = A("rb512", [128, 512], F32)
        rb5122 = A("rb5122", [128, 512], F32)
        pwb = [A("pw0", [128, 512], F32), A("pw1", [128, 512], F32)]

        psA = nc.alloc_psum_tensor("psA", [128, 1024], F32)
        psB = nc.alloc_psum_tensor("psB", [128, 1024], F32)
        poA = nc.alloc_psum_tensor("poA", [128, 1024], F32)
        poB = nc.alloc_psum_tensor("poB", [128, 1024], F32)
        PS = {"A": psA, "B": psB, "OA": poA, "OB": poB}
        khp = [hc.ap()[:, 8192:10240], hc.ap()[:, 10240:12288]]

        sems = {k: nc.alloc_semaphore(k) for k in
                ("wq", "sq", "gq", "te", "ve", "ac", "po")}

        p = Prog()
        B = {}

        def buf(name, i=0):
            key = (name, i)
            if key not in B:
                B[key] = Buf()
            return B[key]

        def reg(name, half):
            return buf("ps" + name, half)

        def psap(name, half, p0=0, p1=128):
            return PS[name][p0:p1, half * 512:half * 512 + 512]

        def mm_group(rname, half, mms, deps=(), tag="", p0=0, p1=128):
            r = reg(rname, half)
            war = r.wdeps()
            last = None
            n = len(mms)
            for i, (lf, rf, tp) in enumerate(mms):
                dd = (list(deps) + war) if i == 0 else []
                last = p.te(lambda e, lf=lf, rf=rf, tp=tp, st=(i == 0),
                            sp=(i == n - 1), p0=p0, p1=p1, half=half, rname=rname:
                            e.matmul(psap(rname, half, p0, p1), lf(), rf(),
                                     start=st, stop=sp, tile_position=tp),
                            deps=dd, tag=tag)
            r.wrote(last)
            return last

        INIT = []
        for fn in (lambda e: e.memset(ones_col_bf[:], 1.0),
                   lambda e: e.memset(ones_row_f[:], 1.0),
                   lambda e: e.memset(one_col[:], 1.0),
                   lambda e: e.memset(negC_col[:], -CSHIFT),
                   lambda e: e.memset(eps1[:], EPS)):
            INIT.append(p.ve(fn))
        INIT.append(p.dma("wq", lambda e: e.dma_start(out=indb[:], in_=indmat)))
        LB = [(150.0, 87.0 / 300.0, 1.0 / 32.0), (300.0, 87.0 / 600.0, 1.0 / 8.0)]
        for Li in (0, 1):
            Bc, lam, scl_ = LB[Li]
            INIT.append(p.ve(lambda e, Li=Li, v=-LB[Li][1] * LB[Li][0]:
                             e.memset(negLB[Li][:], v)))
        INIT.append(p.ve(lambda e: e.memset(zero_col[:], 0.0)))
        INIT.append(p.ve(lambda e: e.memset(tiny_col[:], 1e-38)))
        for b_ in range(2):
            INIT.append(p.ve(lambda e, b_=b_: e.memset(khaug[b_][64:65, :], 1.0)))
        INIT.append(p.ve(lambda e: e.memset(qaug[64:65, :], -640.0)))
        for dst, src in ((t1b, t1c), (t2b, t2c)):
            INIT.append(p.dma("gq", lambda e, d=dst, s=src: e.dma_start(out=d[:], in_=s)))
        for dst, src in ((r1b, r1T), (r2b, r2T), (fbTb, fbT)):
            INIT.append(p.dma("gq", lambda e, d=dst, s=src: e.dma_start(out=d[:], in_=s)))
        INIT.append(p.dma("gq", lambda e: e.dma_start(out=b1c[:], in_=b1)))
        INIT.append(p.dma("gq", lambda e: e.dma_start(out=b2c[:], in_=b2)))
        INIT.append(p.dma("gq", lambda e: e.dma_start(out=boutc[:], in_=bout)))
        INIT.append(p.dma("gq", lambda e: e.dma_start(
            out=vTb.ap().rearrange("p (t r) -> p t r", t=8),
            in_=vT.rearrange("(t p) r -> p t r", p=128))))
        INIT.append(p.dma("gq", lambda e: e.dma_start(
            out=graw[:], in_=gate.unsqueeze(0))))
        VPPAD = []
        INIT.append(p.ve(lambda e: e.memset(
            vstg.ap().rearrange("p (h z) -> p h z", h=8)[:, :, 64:66], 1.0)))

        L0W_TOKS = []
        for ic in range(8):
            L0W_TOKS.append(p.dma("gq", lambda e, ic=ic: e.dma_start(
                out=w1s[0][:, ic * 4096:(ic + 1) * 4096].rearrange(
                    "p (dt t) -> p dt t", dt=8),
                in_=w1.rearrange("(dt p) i -> p dt i", p=128)[:, :, ic * 512:(ic + 1) * 512]),
                tag=f"w1cast{ic}"))
        for ft in range(8):
            L0W_TOKS.append(p.dma("gq", lambda e, ft=ft: e.dma_start(
                out=w2s[0][:, ft * 4096:(ft + 1) * 4096].rearrange(
                    "p (it f) -> p it f", it=32),
                in_=w2.rearrange("(it p) f -> p it f", p=128)[:, :, ft * 128:(ft + 1) * 128]),
                tag=f"w2cast{ft}"))

        def load_wT_fchunk(whbm, col0, bufname, bi, deps=(), tag=""):
            bb = buf(bufname, bi)
            dst = (w1cb if bufname == "w1c" else w2cb)[bi]
            t = p.dma("gq", lambda e: e.dma_start(
                out=dst.ap().rearrange("p (dt t) -> p dt t", dt=8),
                in_=whbm[:, col0:col0 + 512].rearrange("(dt p) t -> p dt t", p=128)),
                deps=bb.wdeps(deps), tag=tag)
            bb.wrote(t)
            return t

        def load_x_chunk(xsrc, c, tag):
            return p.dma("gq", lambda e: e.dma_start(
                out=xcb[:], in_=xsrc[:, c * 4096:(c + 1) * 4096]),
                deps=buf("xcb").wdeps(), tag=tag)

        def phase_ln(L, c, xsrc, extra_deps=()):
            par = c % 2
            t_x = load_x_chunk(xsrc, c, f"L{L}c{c}:x1")
            for d in extra_deps:
                t_x.deps.append(d); d.need = True
            bx = buf("xcb"); bx.wrote(t_x)
            bq = buf("qcln")
            t_cast = bx.read(p.ve(lambda e: e.tensor_copy(out=qc[:], in_=xcb[:]),
                                  deps=bq.wdeps([t_x])))
            bq.wrote(t_cast)
            bh = buf("hcln")
            t_sq = bq.read(p.ve(lambda e: e.tensor_mul(out=hc[:, 0:4096], in0=qc[:], in1=qc[:]),
                                deps=bh.wdeps([t_cast])))
            bh.wrote(t_sq)
            t_s1 = mm_group("A", 0, [
                (lambda: ones_col_bf[:, 0:1],
                 lambda dt=dt: qc[:, dt * CH:(dt + 1) * CH], None) for dt in range(8)],
                deps=[t_cast] + INIT, tag=f"L{L}c{c}:lnsum", p0=0, p1=1)
            bq.read(t_s1)
            t_s2 = mm_group("A", 1, [
                (lambda: ones_col_bf[:, 0:1],
                 lambda dt=dt: hc[:, dt * 512:(dt + 1) * 512], None) for dt in range(8)],
                deps=[t_sq], tag=f"L{L}c{c}:lnsq", p0=0, p1=1)
            bh.read(t_s2)
            t_m = reg("A", 0).read(p.ve(lambda e: e.tensor_scalar_mul(
                out=mrow[:], in0=psA[0:1, 0:512], scalar1=1.0 / D), deps=[t_s1]))
            t_mq = reg("A", 1).read(p.ve(lambda e: e.tensor_scalar_mul(
                out=msqrow[:], in0=psA[0:1, 512:1024], scalar1=1.0 / D), deps=[t_s2]))
            t_v1 = p.ve(lambda e: e.tensor_mul(out=rstdrow[:], in0=mrow[:], in1=mrow[:]),
                        deps=[t_m])
            t_v2 = p.ve(lambda e: e.tensor_sub(out=msqrow[:], in0=msqrow[:], in1=rstdrow[:]),
                        deps=[t_v1, t_mq])
            t_l = p.ac(lambda e: e.activation(out=rstdrow[:], in_=msqrow[:], func=AF.Ln,
                                              bias=eps1[:, 0:1]), deps=[t_v2])
            t_r = p.ac(lambda e: e.activation(out=rstdrow[:], in_=rstdrow[:], func=AF.Exp,
                                              scale=-0.5), deps=[t_l])
            t_mr = p.ve(lambda e: e.tensor_mul(out=mrrow[:], in0=mrow[:], in1=rstdrow[:]),
                        deps=[t_r, t_m])
            t_b1 = mm_group("B", 0, [(lambda: ones_row_f[:], lambda: rstdrow[:], None)],
                            deps=[t_r], tag="lnb1")
            t_b2 = mm_group("B", 1, [(lambda: ones_row_f[:], lambda: mrrow[:], None)],
                            deps=[t_mr], tag="lnb2")
            bsc = buf("sc")
            tt1 = []
            for dt in range(8):
                dd = [t_b1, t_x] + (bsc.wdeps() if dt == 0 else [])
                tt1.append(reg("B", 0).read(bx.read(p.ve(lambda e, dt=dt: e.tensor_tensor(
                    out=sc[:, dt * CH:(dt + 1) * CH], in0=xcb[:, dt * CH:(dt + 1) * CH],
                    in1=psB[:, 0:512], op=ALU.mult), deps=dd))))
            bsc.wrote(tt1[-1])
            bxn = buf("xnb", par)
            tt2 = []
            for dt in range(8):
                dd = [t_b2, tt1[dt]] + (bxn.wdeps() if dt == 0 else [])
                tt2.append(bsc.read(reg("B", 1).read(p.ve(lambda e, dt=dt: e.tensor_tensor(
                    out=xnb[par][:, dt * CH:(dt + 1) * CH],
                    in0=sc[:, dt * CH:(dt + 1) * CH],
                    in1=psB[:, 512:1024], op=ALU.subtract), deps=dd))))
            bxn.wrote(tt2[-1])
            t_st = p.dma("wq", lambda e: e.dma_start(
                out=xn_scr[:, c * 4096:(c + 1) * 4096], in_=xnb[par][:]),
                deps=tt2, tag=f"L{L}c{c}:xnst")
            bxn.read(t_st)
            return t_st

        def phase_kv(L, c, xn_st):
            par = c % 2
            bxn = buf("xnb", par)
            t_xn = p.dma("wq", lambda e: e.dma_start(
                out=xnb[par][:], in_=xn_scr[:, c * 4096:(c + 1) * 4096]),
                deps=bxn.wdeps([xn_st]), tag=f"L{L}c{c}:xn2")
            bxn.wrote(t_xn)
            wk = wkT0 if L == 0 else wqkvT[:, D:2 * D]
            wv = wvT0 if L == 0 else wqkvT[:, 2 * D:3 * D]
            stores = []
            for fc in range(2):
                bi = fc % 2
                t_w = load_wT_fchunk(wk, fc * 512, "w1c", bi, tag=f"L{L}c{c}:wk{fc}")
                for fi in range(4):
                    ft = fc * 4 + fi
                    hlf = ft % 2
                    t_mm = mm_group("A", hlf, [
                        (lambda dt=dt, fi=fi, bi=bi:
                         w1cb[bi][:, dt * 512 + fi * 128:dt * 512 + fi * 128 + 128],
                         lambda dt=dt: xnb[par][:, dt * CH:(dt + 1) * CH], None)
                        for dt in range(8)],
                        deps=[t_w, t_xn], tag=f"L{L}c{c}:kmm{ft}")
                    buf("w1c", bi).read(t_mm)
                    bxn.read(t_mm)
                    sb = ft % 2
                    bstg = buf("stg", sb)
                    if L == 0:
                        t_ev = reg("A", hlf).read(p.ve(
                            lambda e, sb=sb, hlf=hlf: e.tensor_copy(
                                out=stgb[sb][:], in_=psap("A", hlf)),
                            deps=bstg.wdeps([t_mm])))
                    else:
                        bm1, bm2 = buf("m1"), buf("m2")
                        t_m1 = reg("A", hlf).read(p.ve(lambda e, hlf=hlf, c=c: e.tensor_tensor(
                            out=m1t[:], in0=psap("A", hlf),
                            in1=t1b[:, c * CH:(c + 1) * CH], op=ALU.mult),
                            deps=bm1.wdeps([t_mm]) + INIT))
                        bm1.wrote(t_m1)
                        t_m2 = reg("A", hlf).read(p.ve(lambda e, hlf=hlf, c=c: e.tensor_tensor(
                            out=m2t[:], in0=psap("A", hlf),
                            in1=t2b[:, c * CH:(c + 1) * CH], op=ALU.mult),
                            deps=bm2.wdeps([t_mm])))
                        bm2.wrote(t_m2)
                        t_rot = mm_group("B", hlf, [
                            (lambda: r1b[:], lambda: m1t[:], None),
                            (lambda: r2b[:], lambda: m2t[:], None)],
                            deps=[t_m1, t_m2], tag=f"L{L}c{c}:krot{ft}")
                        bm1.read(t_rot); bm2.read(t_rot)
                        t_ev = reg("B", hlf).read(p.ve(
                            lambda e, sb=sb, hlf=hlf: e.tensor_copy(
                                out=stgb[sb][:], in_=psap("B", hlf)),
                            deps=bstg.wdeps([t_rot])))
                    bstg.wrote(t_ev)
                    t_kst = p.dma("sq", lambda e, ft=ft, sb=sb, c=c: e.dma_start(
                        out=k_scr[ft * 128:(ft + 1) * 128, c * CH:(c + 1) * CH],
                        in_=stgb[sb][:]),
                        deps=[t_ev] + buf("kscr").wdeps(), tag=f"L{L}c{c}:kst{ft}")
                    bstg.read(t_kst)
                    stores.append(t_kst)
            for fc in range(2):
                bi = fc % 2
                t_w = load_wT_fchunk(wv, fc * 512, "w2c", bi, tag=f"L{L}c{c}:wv{fc}")
                for tt_ in range(4):
                    hlf = tt_ % 2
                    t_mm = mm_group("A", hlf, [
                        (lambda dt=dt, tt_=tt_:
                         xnb[par][:, dt * CH + tt_ * 128:dt * CH + tt_ * 128 + 128],
                         lambda dt=dt, bi=bi: w2cb[bi][:, dt * 512:(dt + 1) * 512], None)
                        for dt in range(8)],
                        deps=[t_w, t_xn], tag=f"L{L}c{c}:vmm{fc}_{tt_}")
                    buf("w2c", bi).read(t_mm)
                    bxn.read(t_mm)
                    bstg = buf("vstg")
                    t_ev = reg("A", hlf).read(p.ve(
                        lambda e, hlf=hlf: e.tensor_copy(
                            out=vstg.ap().rearrange("p (h z) -> p h z", h=8)[:, :, 0:64],
                            in_=psap("A", hlf)),
                        deps=bstg.wdeps([t_mm]) + INIT))
                    bstg.wrote(t_ev)
                    kti = c * 4 + tt_
                    t_vst = p.dma("sq", lambda e, kti=kti, fc=fc: e.dma_start(
                        out=vp_scr.rearrange("p (pr kt z) -> p pr kt z", pr=NPAIR, kt=NKT)[
                            :, fc * 4:(fc + 1) * 4, kti, :],
                        in_=vstg.ap().rearrange("p (pp z) -> p pp z", pp=4)),
                        deps=[t_ev] + buf("vpscr").wdeps(), tag=f"L{L}c{c}:vst{fc}_{tt_}")
                    bstg.read(t_vst)
                    stores.append(t_vst)
            return stores

        def phase_attn(L, c, kv_done, xn_tok):
            par = c % 2
            scale = (1.0 / 32.0) if L == 0 else (1.0 / 8.0)
            wqq = wqT0 if L == 0 else wqkvT[:, 0:D]
            bxn = buf("xnb", par)
            bqc = buf("qc")
            q_toks = []
            for fc in range(2):
                bi = fc % 2
                t_w = load_wT_fchunk(wqq, fc * 512, "w2c", bi, tag=f"L{L}c{c}:wq{fc}")
                for fi in range(4):
                    ft = fc * 4 + fi
                    hlf = ft % 2
                    t_mm = mm_group("A", hlf, [
                        (lambda dt=dt, fi=fi, bi=bi:
                         w2cb[bi][:, dt * 512 + fi * 128:dt * 512 + fi * 128 + 128],
                         lambda dt=dt: xnb[par][:, dt * CH:(dt + 1) * CH], None)
                        for dt in range(8)],
                        deps=[t_w, xn_tok], tag=f"L{L}c{c}:qmm{ft}")
                    buf("w2c", bi).read(t_mm)
                    bxn.read(t_mm)
                    wq_deps = bqc.wdeps() if ft == 0 else []
                    bqa = buf("qaug")
                    wqa_deps = bqa.wdeps() if ft == 0 else []
                    if L == 0:
                        t_ev = reg("A", hlf).read(p.ve(lambda e, ft=ft, hlf=hlf: e.tensor_copy(
                            out=qc[:, ft * 512:(ft + 1) * 512], in_=psap("A", hlf)),
                            deps=[t_mm] + wq_deps))
                        t_a1 = reg("A", hlf).read(p.ve(lambda e, ft=ft, hlf=hlf: e.tensor_copy(
                            out=qaug[0:64, (2 * ft) * 512:(2 * ft + 1) * 512],
                            in_=psap("A", hlf, 0, 64)), deps=[t_mm] + wqa_deps))
                        t_a2 = reg("A", hlf).read(p.ve(lambda e, ft=ft, hlf=hlf: e.tensor_copy(
                            out=qaug[0:64, (2 * ft + 1) * 512:(2 * ft + 2) * 512],
                            in_=psap("A", hlf, 64, 128)), deps=[t_mm]))
                        bqa.wrote(t_a2)
                        q_toks.append(t_a1); q_toks.append(t_a2)
                    else:
                        bm1, bm2 = buf("m1"), buf("m2")
                        t_m1 = reg("A", hlf).read(p.ve(lambda e, hlf=hlf, c=c: e.tensor_tensor(
                            out=m1t[:], in0=psap("A", hlf),
                            in1=t1b[:, c * CH:(c + 1) * CH], op=ALU.mult),
                            deps=bm1.wdeps([t_mm]) + INIT))
                        bm1.wrote(t_m1)
                        t_m2 = reg("A", hlf).read(p.ve(lambda e, hlf=hlf, c=c: e.tensor_tensor(
                            out=m2t[:], in0=psap("A", hlf),
                            in1=t2b[:, c * CH:(c + 1) * CH], op=ALU.mult),
                            deps=bm2.wdeps([t_mm])))
                        bm2.wrote(t_m2)
                        t_rot = mm_group("B", hlf, [
                            (lambda: r1b[:], lambda: m1t[:], None),
                            (lambda: r2b[:], lambda: m2t[:], None)],
                            deps=[t_m1, t_m2], tag=f"L{L}c{c}:qrot{ft}")
                        bm1.read(t_rot); bm2.read(t_rot)
                        t_ev = reg("B", hlf).read(p.ve(lambda e, ft=ft, hlf=hlf: e.tensor_copy(
                            out=qc[:, ft * 512:(ft + 1) * 512], in_=psap("B", hlf)),
                            deps=[t_rot] + wq_deps))
                        t_a1 = reg("B", hlf).read(p.ve(lambda e, ft=ft, hlf=hlf: e.tensor_copy(
                            out=qaug[0:64, (2 * ft) * 512:(2 * ft + 1) * 512],
                            in_=psap("B", hlf, 0, 64)), deps=[t_rot] + wqa_deps))
                        t_a2 = reg("B", hlf).read(p.ve(lambda e, ft=ft, hlf=hlf: e.tensor_copy(
                            out=qaug[0:64, (2 * ft + 1) * 512:(2 * ft + 2) * 512],
                            in_=psap("B", hlf, 64, 128)), deps=[t_rot]))
                        bqa.wrote(t_a2)
                        q_toks.append(t_a1); q_toks.append(t_a2)
                    q_toks.append(t_ev)
            bqc.wrote(q_toks[-1])
            if f"q{L}c{c}" in tap_aps:
                t_tapq = p.dma("gq", lambda e, L=L, c=c: e.dma_start(
                    out=tap_aps[f"q{L}c{c}"], in_=qc[:]), deps=list(q_toks), tag="tapq")
                bqc.read(t_tapq)
                q_toks = q_toks + [t_tapq]
            Bc, lam, _scl = LB[L]
            exp_toks = []
            qr_dmas = []
            bqrs = buf("qrow_scr")
            war_qrs = bqrs.wdeps() if L == 1 else []
            for pr in (range(NPAIR) if L == 1 else ()):
                pair_exp_toks = []
                kb = pr % 2
                bkhp = buf("khp", kb)
                t_khp = p.dma("wq", lambda e, pr=pr, kb=kb: e.dma_start(
                    out=khp[kb], in_=k_scr[pr * 128:(pr + 1) * 128, :]),
                    deps=bkhp.wdeps(kv_done) + buf("hc").peek(), tag=f"khpl{pr}")
                bkhp.wrote(t_khp)
                buf("kscr").read(t_khp)
                for qsub in range(4):
                    for kc in range(4):
                        sn = ("A", "B", "OA", "OB")[kc]
                        sten = PS[sn]
                        rs0, rs1 = reg(sn, 0), reg(sn, 1)
                        war_s = rs0.wdeps() + rs1.wdeps()
                        t_la = p.te(lambda e, kb=kb, kc=kc, pr=pr, qsub=qsub, sten=sten:
                                    e.matmul(sten[:, 0:512],
                                             qc[0:64, pr * 512 + qsub * 128:pr * 512 + qsub * 128 + 128],
                                             khp[kb][0:64, kc * 512:(kc + 1) * 512],
                                             start=True, stop=True, tile_position=(0, 0)),
                                    deps=[t_khp] + q_toks + war_s, tag=f"lA{pr}_{qsub}_{kc}")
                        t_lb = p.te(lambda e, kb=kb, kc=kc, pr=pr, qsub=qsub, sten=sten:
                                    e.matmul(sten[:, 512:1024],
                                             qc[64:128, pr * 512 + qsub * 128:pr * 512 + qsub * 128 + 128],
                                             khp[kb][64:128, kc * 512:(kc + 1) * 512],
                                             start=True, stop=True, tile_position=(64, 0)),
                                    deps=[], tag=f"lB{pr}_{qsub}_{kc}")
                        rs0.wrote(t_la); rs1.wrote(t_lb)
                        bkhp.read(t_la); bkhp.read(t_lb)
                        buf("hc").read(t_la); buf("hc").read(t_lb)
                        bqc.read(t_la); bqc.read(t_lb)
                        colA = (2 * pr) * 16 + qsub * 4 + kc
                        colB = (2 * pr + 1) * 16 + qsub * 4 + kc
                        if pr % 2 == 0:
                            t_ea = rs0.read(p.ac(lambda e, sten=sten, colA=colA, L=L:
                                                 e.activation(out=m1t[:], in_=sten[:, 0:512],
                                                              func=AF.Exp, bias=negLB[L][:, 0:1],
                                                              scale=lam * _scl,
                                                              accum_out=macc256[:, colA:colA + 1]),
                                                 deps=[t_la] + INIT, tag=f"eA{pr}_{qsub}_{kc}"))
                            t_eb2 = rs1.read(p.ac(lambda e, sten=sten, colB=colB, L=L:
                                                  e.activation(out=m1t[:], in_=sten[:, 512:1024],
                                                               func=AF.Exp, bias=negLB[L][:, 0:1],
                                                               scale=lam * _scl,
                                                               accum_out=macc256[:, colB:colB + 1]),
                                                  deps=[t_lb], tag=f"eB{pr}_{qsub}_{kc}"))
                        else:
                            t_ea = rs0.read(p.ve(lambda e, sten=sten, colA=colA:
                                                 e.reduce_max(out=macc256[:, colA:colA + 1],
                                                              in_=sten[:, 0:512],
                                                              axis=mybir.AxisListType.X),
                                                 deps=[t_la], tag=f"mA{pr}_{qsub}_{kc}"))
                            t_eb2 = rs1.read(p.ve(lambda e, sten=sten, colB=colB:
                                                  e.reduce_max(out=macc256[:, colB:colB + 1],
                                                               in_=sten[:, 512:1024],
                                                               axis=mybir.AxisListType.X),
                                                  deps=[t_lb], tag=f"mB{pr}_{qsub}_{kc}"))
                        exp_toks += [t_ea, t_eb2]
                        pair_exp_toks += [t_ea, t_eb2]
                g0 = 8 * pr
                mvp = macc256.ap().rearrange("p (g k) -> p k g", k=4)[:, :, g0:g0 + 8]
                mcp = mcols[:, g0:g0 + 8]
                cop = ALU.add if pr % 2 == 0 else ALU.max
                t_r1 = p.ve(lambda e, mvp=mvp, mcp=mcp, op=cop: e.tensor_tensor(
                    out=mcp, in0=mvp[:, 0, :], in1=mvp[:, 1, :], op=op),
                    deps=pair_exp_toks)
                t_r2 = p.ve(lambda e, mvp=mvp, mcp=mcp, op=cop: e.tensor_tensor(
                    out=mcp, in0=mcp, in1=mvp[:, 2, :], op=op), deps=[t_r1])
                combine_op = ALU.add if pr % 2 == 0 else ALU.max
                t_r3 = p.ve(lambda e, mvp=mvp, mcp=mcp, op=combine_op: e.tensor_tensor(
                    out=mcp, in0=mcp, in1=mvp[:, 3, :], op=op), deps=[t_r2])
                if pr % 2 == 0:
                    t_lnm = p.ac(lambda e, mcp=mcp: e.activation(
                        out=mcp, in_=mcp, func=AF.Ln, bias=tiny_col[:, 0:1]), deps=[t_r3])
                    t_aff = p.ve(lambda e, mcp=mcp, L=L: e.tensor_scalar(
                        out=mcp, in0=mcp, scalar1=-1.0 / (lam * _scl),
                        scalar2=-Bc / _scl, op0=ALU.mult, op1=ALU.add), deps=[t_lnm])
                else:
                    t_aff = p.ve(lambda e, mcp=mcp: e.tensor_scalar_mul(
                        out=mcp, in0=mcp, scalar1=-1.0), deps=[t_r3])
                for hh in (2 * pr, 2 * pr + 1):
                    for qsub in range(4):
                        qr_dmas.append(p.dma("sq", lambda e, hh=hh, qsub=qsub: e.dma_start(
                            out=qrow_scr[hh:hh + 1, qsub * 128:(qsub + 1) * 128],
                            in_=mcols[:, hh * 4 + qsub:hh * 4 + qsub + 1]),
                            deps=[t_aff] + (war_qrs if (hh == 0 and qsub == 0) else []),
                            tag=f"qr{hh}_{qsub}"))
            t_qrow = None
            if L == 1:
                bqa = buf("qaug")
                t_qrow = p.dma("gq", lambda e: e.dma_start(
                    out=qaug[64:65, :],
                    in_=qrow_scr.rearrange("h q -> (h q)").unsqueeze(0)),
                    deps=qr_dmas, tag="qrowl")
                bqa.read(t_qrow)
                bqrs.read(t_qrow)
            bqa = buf("qaug")
            den_dmas = []
            ev_toks = []
            for pr in range(NPAIR):
                kb = pr % 2
                bka, bkb_ = buf("kha", 0), buf("kha", 1)
                t_ka = p.dma("wq", lambda e, pr=pr: e.dma_start(
                    out=khaug[0][0:64, :],
                    in_=k_scr[(2 * pr) * 64:(2 * pr) * 64 + 64, :]),
                    deps=bka.wdeps(kv_done), tag=f"kal{pr}")
                bka.wrote(t_ka)
                buf("kscr").read(t_ka)
                t_kb2 = p.dma("wq", lambda e, pr=pr: e.dma_start(
                    out=khaug[1][0:64, :],
                    in_=k_scr[(2 * pr + 1) * 64:(2 * pr + 1) * 64 + 64, :]),
                    deps=bkb_.wdeps(kv_done), tag=f"kbl{pr}")
                bkb_.wrote(t_kb2)
                buf("kscr").read(t_kb2)
                bvh = buf("vh", kb)
                t_vh = p.dma("wq", lambda e, pr=pr, kb=kb: e.dma_start(
                    out=vhb[kb][:],
                    in_=vp_scr[:, pr * NKT * 132:(pr + 1) * NKT * 132]),
                    deps=bvh.wdeps(kv_done), tag=f"vhl{pr}")
                bvh.wrote(t_vh)
                buf("vpscr").read(t_vh)
                on = "OA" if pr % 2 == 0 else "OB"
                oten = PS[on]
                ro0, ro1 = reg(on, 0), reg(on, 1)
                war_o = ro0.wdeps() + ro1.wdeps()
                pv_last = None
                pend = None

                def emit_pv(kt, eb, t_e):
                    nonlocal pv_last
                    t_pa = p.te(lambda e, kb=kb, kt=kt, eb=eb, oten=oten: e.matmul(
                        oten[0:65, 0:512], vhb[kb][:, kt * 132:kt * 132 + 65],
                        eabb[eb][:, 0:512], start=(kt == 0), stop=(kt == NKT - 1)),
                        deps=[t_e, t_vh] + (war_o if kt == 0 else []), tag=f"pA{pr}_{kt}")
                    t_pb = p.te(lambda e, kb=kb, kt=kt, eb=eb, oten=oten: e.matmul(
                        oten[0:65, 512:1024], vhb[kb][:, kt * 132 + 66:kt * 132 + 131],
                        eabb[eb][:, 512:1024], start=(kt == 0), stop=(kt == NKT - 1)),
                        deps=[], tag=f"pB{pr}_{kt}")
                    buf("eab", eb).read(t_pa); buf("eab", eb).read(t_pb)
                    bvh.read(t_pa); bvh.read(t_pb)
                    pv_last = t_pb

                for kt in range(NKT):
                    sn = "A" if kt % 2 == 0 else "B"
                    sten = PS[sn]
                    rs0, rs1 = reg(sn, 0), reg(sn, 1)
                    war_s = rs0.wdeps() + rs1.wdeps()
                    t_sa = p.te(lambda e, kt=kt, pr=pr, sten=sten: e.matmul(
                        sten[:, 0:512], khaug[0][0:65, kt * 128:(kt + 1) * 128],
                        qaug[0:65, (2 * pr) * 512:(2 * pr + 1) * 512],
                        start=True, stop=True),
                        deps=[t_ka] + ([t_qrow] if t_qrow else []) + q_toks + war_s,
                        tag=f"sA{pr}_{kt}")
                    t_sb = p.te(lambda e, kt=kt, pr=pr, sten=sten: e.matmul(
                        sten[:, 512:1024], khaug[1][0:65, kt * 128:(kt + 1) * 128],
                        qaug[0:65, (2 * pr + 1) * 512:(2 * pr + 2) * 512],
                        start=True, stop=True),
                        deps=[t_kb2], tag=f"sB{pr}_{kt}")
                    rs0.wrote(t_sa); rs1.wrote(t_sb)
                    bka.read(t_sa); bkb_.read(t_sb)
                    bqa.read(t_sa); bqa.read(t_sb)
                    eb = kt % 2
                    beab = buf("eab", eb)
                    t_e = p.ac(lambda e, eb=eb, sten=sten: e.activation(
                        out=eabb[eb][:], in_=sten[:], func=AF.Exp,
                        bias=zero_col[:, 0:1], scale=scale),
                        deps=beab.wdeps([t_sa, t_sb]) + INIT, tag=f"exp{pr}_{kt}")
                    beab.wrote(t_e)
                    rs0.read(t_e); rs1.read(t_e)
                    if pend is not None:
                        emit_pv(*pend)
                    pend = (kt, eb, t_e)
                emit_pv(*pend)
                ro0.wrote(pv_last); ro1.wrote(pv_last)
                bdp = buf("denrowA"); bdp2 = buf("denrowB")
                t_dc = p.ve(lambda e, oten=oten: e.tensor_copy(
                    out=rstdrow[:], in_=oten[64:65, 0:512]),
                    deps=bdp.wdeps([pv_last]))
                t_dc2 = p.ve(lambda e, oten=oten: e.tensor_copy(
                    out=mrrow[:], in_=oten[64:65, 512:1024]),
                    deps=bdp2.wdeps([pv_last]))
                bdp.wrote(t_dc); bdp2.wrote(t_dc2)
                ro0.read(t_dc); ro1.read(t_dc)
                ro0.read(t_dc2); ro1.read(t_dc2)
                bdscr = buf("den_scr")
                t_dd = p.dma("sq", lambda e, pr=pr: e.dma_start(
                    out=den_scr[2 * pr:2 * pr + 1, :], in_=rstdrow[:]),
                    deps=[t_dc] + (bdscr.wdeps() if pr == 0 else []),
                    tag=f"dendA{pr}")
                t_dd2 = p.dma("sq", lambda e, pr=pr: e.dma_start(
                    out=den_scr[2 * pr + 1:2 * pr + 2, :], in_=mrrow[:]),
                    deps=[t_dc2], tag=f"dendB{pr}")
                bdp.read(t_dd); bdp2.read(t_dd2)
                den_dmas.append(t_dd); den_dmas.append(t_dd2)
                bsc = buf("sc")
                t_e1 = p.ve(lambda e, pr=pr, oten=oten: e.tensor_copy(
                    out=sc[0:64, pr * 512:(pr + 1) * 512], in_=oten[0:64, 0:512]),
                    deps=[pv_last] + (bsc.wdeps() if pr == 0 else []))
                t_e2 = p.ve(lambda e, pr=pr, oten=oten: e.tensor_copy(
                    out=sc[64:128, pr * 512:(pr + 1) * 512], in_=oten[0:64, 512:1024]),
                    deps=[pv_last])
                ro0.read(t_e1); ro0.read(t_e2)
                ro1.read(t_e1); ro1.read(t_e2)
                bsc.wrote(t_e2)
                ev_toks += [t_e1, t_e2]
            bd16 = buf("den16")
            t_d16 = p.dma("sq", lambda e: e.dma_start(out=den16[:], in_=den_scr),
                          deps=bd16.wdeps(den_dmas), tag="den16")
            buf("den_scr").read(t_d16)
            bd16.wrote(t_d16)
            tap_d = []
            if f"den{L}c{c}" in tap_aps:
                t_tapd = p.dma("gq", lambda e, L=L, c=c: e.dma_start(
                    out=tap_aps[f"den{L}c{c}"], in_=den_scr), deps=[t_d16], tag="tapden")
                tap_d = [t_tapd]
            t_ln = bd16.read(p.ac(lambda e: e.activation(
                out=den16[:], in_=den16[:], func=AF.Ln), deps=[t_d16] + tap_d))
            t_rd = bd16.read(p.ac(lambda e: e.activation(
                out=den16[:], in_=den16[:], func=AF.Exp, scale=-1.0), deps=[t_ln]))
            norm_toks = []
            for ft in range(8):
                hlf = ft % 2
                t_dv = mm_group("A", hlf, [
                    (lambda ft=ft: indb[:, ft * 128:(ft + 1) * 128],
                     lambda: den16[:], None)], deps=[t_rd], tag=f"divmm{ft}")
                if L == 0:
                    t_n = reg("A", hlf).read(p.ve(lambda e, ft=ft, hlf=hlf: e.tensor_tensor(
                        out=sc[:, ft * 512:(ft + 1) * 512],
                        in0=sc[:, ft * 512:(ft + 1) * 512],
                        in1=psap("A", hlf), op=ALU.mult),
                        deps=[t_dv, ev_toks[2 * ft], ev_toks[2 * ft + 1]]))
                else:
                    bhc = buf("hc")
                    t_n = reg("A", hlf).read(bhc.read(p.ve(
                        lambda e, ft=ft, hlf=hlf: e.tensor_tensor(
                            out=hc[:, ft * 512:(ft + 1) * 512],
                            in0=sc[:, ft * 512:(ft + 1) * 512],
                            in1=psap("A", hlf), op=ALU.mult),
                        deps=([t_dv, ev_toks[2 * ft], ev_toks[2 * ft + 1]] +
                              (bhc.wdeps() if ft == 0 else [])))))
                norm_toks.append(t_n)
            if L == 1:
                buf("hc").wrote(norm_toks[-1])
                proj_toks = []
                for fc in range(2):
                    bi = fc % 2
                    t_w = load_wT_fchunk(woutT, fc * 512, "w1c", bi, tag=f"wout{fc}")
                    for fi in range(4):
                        ft = fc * 4 + fi
                        hlf = ft % 2
                        t_mm = mm_group("B", hlf, [
                            (lambda dt=dt, fi=fi, bi=bi:
                             w1cb[bi][:, dt * 512 + fi * 128:dt * 512 + fi * 128 + 128],
                             lambda dt=dt: hc[:, dt * 512:(dt + 1) * 512], None)
                            for dt in range(8)],
                            deps=[t_w] + norm_toks, tag=f"pjmm{ft}")
                        buf("w1c", bi).read(t_mm)
                        buf("hc").read(t_mm)
                        t_pe = reg("B", hlf).read(p.ve(lambda e, ft=ft, hlf=hlf: e.tensor_scalar(
                            out=sc[:, ft * 512:(ft + 1) * 512], in0=psap("B", hlf),
                            scalar1=boutc[:, ft:ft + 1], scalar2=None, op0=ALU.add),
                            deps=[t_mm, norm_toks[ft]]))
                        proj_toks.append(t_pe)
                norm_toks = proj_toks
            if f"sc{L}c{c}" in tap_aps:
                t_tap = p.dma("gq", lambda e, L=L, c=c: e.dma_start(
                    out=tap_aps[f"sc{L}c{c}"], in_=sc[:]), deps=list(norm_toks),
                    tag="tapsc")
                buf("sc").read(t_tap)
                norm_toks = norm_toks + [t_tap]
            return norm_toks

        def phase_ffn(L, c, s_toks, xn_tok, xsrc, xdst, wprep_toks):
            par = c % 2
            bxn = buf("xnb", par)
            bx = buf("xcb")
            t_x = load_x_chunk(xsrc, c, f"L{L}c{c}:x2")
            bx.wrote(t_x)
            lr_tok = None
            if L == 1:
                t_z1 = mm_group("OA", 0, [
                    (lambda dt=dt: vTb[:, dt * RANK:(dt + 1) * RANK],
                     lambda dt=dt: xnb[par][:, dt * CH:(dt + 1) * CH], None)
                    for dt in range(8)],
                    deps=[xn_tok] + INIT, tag="z1mm", p0=0, p1=RANK)
                bxn.read(t_z1)
                bz1 = buf("z1b")
                t_z1e = reg("OA", 0).read(p.ve(lambda e: e.tensor_copy(
                    out=z1b[:], in_=poA[0:RANK, 0:512]), deps=bz1.wdeps([t_z1])))
                bz1.wrote(t_z1e)
                t_z2 = mm_group("OB", 0, [(lambda: fbTb[:], lambda: z1b[:], None)],
                                deps=[t_z1e] + wprep_toks, tag="z2mm", p0=0, p1=RANK)
                bz1.read(t_z2)
                bz2 = buf("z2g")
                t_z2e = reg("OB", 0).read(p.ve(lambda e: e.tensor_scalar_mul(
                    out=z2g[:], in0=poB[0:RANK, 0:512], scalar1=gcol[:, 0:1]),
                    deps=bz2.wdeps([t_z2])))
                bz2.wrote(t_z2e)
                lr_tok = t_z2e
            h_toks = []
            bhc = buf("hc")
            war_h = bhc.wdeps()
            for ic in range(8):
                bi = ic % 2
                bw = buf("w1c", bi)
                t_w = p.dma("wq", lambda e, ic=ic, bi=bi, L=L: e.dma_start(
                    out=w1cb[bi][:], in_=w1s[L][:, ic * 4096:(ic + 1) * 4096]),
                    deps=bw.wdeps(wprep_toks), tag=f"w1l{ic}")
                bw.wrote(t_w)
                t_ul = None
                if L == 1:
                    but = buf("uTsb", ic % 2)
                    t_ul = p.dma("wq", lambda e, ic=ic: e.dma_start(
                        out=uTsb[ic % 2][:], in_=uts_scr[:, ic * 512:(ic + 1) * 512]),
                        deps=but.wdeps(wprep_toks), tag=f"utl{ic}")
                    but.wrote(t_ul)
                for it in range(4):
                    ii = ic * 4 + it
                    hlf = it % 2
                    mms = [(lambda dt=dt, it=it, bi=bi:
                            w1cb[bi][:, dt * 512 + it * 128:dt * 512 + it * 128 + 128],
                            lambda dt=dt: xnb[par][:, dt * CH:(dt + 1) * CH], None)
                           for dt in range(8)]
                    if L == 1:
                        mms.append((lambda it=it, bu=ic % 2: uTsb[bu][:, it * 128:(it + 1) * 128],
                                    lambda: z2g[:], None))
                    t_mm = mm_group("A", hlf, mms,
                                    deps=[t_w, xn_tok] + ([lr_tok, t_ul] if lr_tok else []),
                                    tag=f"hmm{ii}")
                    if L == 1:
                        buf("uTsb", ic % 2).read(t_mm)
                    bw.read(t_mm)
                    bxn.read(t_mm)
                    if L == 1:
                        buf("z2g").read(t_mm)
                    if L == 0:
                        t_h = reg("A", hlf).read(p.ve(lambda e, ii=ii, hlf=hlf: e.tensor_scalar(
                            out=hc[:, ii * 512:(ii + 1) * 512], in0=psap("A", hlf),
                            scalar1=b1c[:, ii:ii + 1], scalar2=0.0,
                            op0=ALU.add, op1=ALU.max),
                            deps=[t_mm] + INIT + (war_h if ii == 0 else [])))
                    else:
                        bzt = buf("zt")
                        t_t = reg("A", hlf).read(p.ve(lambda e, hlf=hlf: e.tensor_scalar_max(
                            out=zt[:], in0=psap("A", hlf), scalar1=0.0),
                            deps=bzt.wdeps([t_mm]) + (war_h if ii == 0 else [])))
                        bzt.wrote(t_t)
                        t_h = bzt.read(p.ve(lambda e, ii=ii: e.scalar_tensor_tensor(
                            out=hc[:, ii * 512:(ii + 1) * 512], in0=zt[:],
                            scalar=sclf_sq[:, ii:ii + 1], in1=zt[:],
                            op0=ALU.mult, op1=ALU.mult), deps=[t_t] + wprep_toks))
                    h_toks.append(t_h)
            bhc.wrote(h_toks[-1])
            comb_toks = []
            for ft in range(8):
                bi = ft % 2
                bw = buf("w2c", bi)
                t_w = p.dma("wq", lambda e, ft=ft, bi=bi, L=L: e.dma_start(
                    out=w2cb[bi][:], in_=w2s[L][:, ft * 4096:(ft + 1) * 4096]),
                    deps=bw.wdeps(wprep_toks), tag=f"w2l{ft}")
                bw.wrote(t_w)
                hlf = ft % 2
                t_mm = mm_group("B", hlf, [
                    (lambda it=it, bi=bi: w2cb[bi][:, it * 128:(it + 1) * 128],
                     lambda it=it: hc[:, it * 512:(it + 1) * 512], None)
                    for it in range(32)],
                    deps=[t_w] + h_toks, tag=f"omm{ft}")
                bw.read(t_mm)
                bhc.read(t_mm)
                bz = buf("ztf")
                if L == 0:
                    t_z = reg("B", hlf).read(p.ve(lambda e, ft=ft, hlf=hlf: e.tensor_scalar_add(
                        out=zt[:], in0=psap("B", hlf), scalar1=b2c[:, ft:ft + 1]),
                        deps=bz.wdeps([t_mm]) + INIT))
                else:
                    t_z = reg("B", hlf).read(p.ve(lambda e, ft=ft, hlf=hlf: e.tensor_scalar_mul(
                        out=zt[:], in0=psap("B", hlf), scalar1=sclp[:, ft:ft + 1]),
                        deps=bz.wdeps([t_mm]) + wprep_toks))
                bz.wrote(t_z)
                bu = buf("ut")
                t_u = bz.read(p.ac(lambda e: e.activation(
                    out=ut[:], in_=zt[:], func=AF.Exp, scale=-1.0),
                    deps=bu.wdeps([t_z])))
                bu.wrote(t_u)
                bt_ = buf("ttt")
                t_t = bu.read(p.ac(lambda e: e.activation(
                    out=ttt[:], in_=ut[:], func=AF.Ln, bias=one_col[:, 0:1]),
                    deps=bt_.wdeps([t_u]) + INIT))
                bt_.wrote(t_t)
                t_w2 = bt_.read(p.ac(lambda e: e.activation(
                    out=ttt[:], in_=ttt[:], func=AF.Exp, scale=-1.0), deps=[t_t]))
                bt_.wrote(t_w2)
                bu2 = buf("ut")
                t_g = bz.read(bt_.read(p.ve(lambda e: e.tensor_mul(
                    out=ut[:], in0=zt[:], in1=ttt[:]), deps=bu2.wdeps([t_w2, t_z]))))
                bu2.wrote(t_g)
                t_sg = buf("sc").read(bu2.read(p.po(lambda e, ft=ft: e.tensor_mul(
                    out=zt[:], in0=sc[:, ft * 512:(ft + 1) * 512], in1=ut[:]),
                    deps=bz.wdeps([t_g] + s_toks))))
                bz.wrote(t_sg)
                t_cx = bz.read(bx.read(p.ve(lambda e, ft=ft: e.tensor_add(
                    out=xcb[:, ft * 512:(ft + 1) * 512],
                    in0=xcb[:, ft * 512:(ft + 1) * 512], in1=zt[:]),
                    deps=[t_sg, t_x])))
                comb_toks.append(t_cx)
            t_xst = p.dma("gq", lambda e, c=c: e.dma_start(
                out=xdst[:, c * 4096:(c + 1) * 4096], in_=xcb[:]),
                deps=comb_toks, tag=f"L{L}c{c}:xst")
            bx.read(t_xst)
            return t_xst

        def make_l1_prep():
            st = {}

            def job_scales():
                toks = []
                red_toks = []
                btm0 = buf("sc")
                for jj in range(8):
                    t_l = p.dma("sq", lambda e, jj=jj: e.dma_start(
                        out=sc[:], in_=wfc[jj * 512:(jj + 1) * 512, :].rearrange(
                            "(r p) d -> p r d", p=128)),
                        deps=btm0.wdeps(), tag=f"fcn{jj}")
                    btm0.wrote(t_l)
                    for r in range(4):
                        t_r = btm0.read(p.ve(lambda e, jj=jj, r=r: e.tensor_reduce(
                            out=rcpf[:, jj * 4 + r:jj * 4 + r + 1],
                            in_=sc[:, r * 1024:(r + 1) * 1024],
                            axis=mybir.AxisListType.X,
                            op=ALU.add, apply_absolute_value=True), deps=[t_l]))
                        red_toks.append(t_r)
                t_s = p.ve(lambda e: e.tensor_scalar(
                    out=sclf_sq[:], in0=rcpf[:], scalar1=1.0 / D, scalar2=1e-5,
                    op0=ALU.mult, op1=ALU.max), deps=red_toks)
                t_rc = p.ve(lambda e: e.reciprocal(out=rcpf[:], in_=sclf_sq[:]), deps=[t_s])
                t_sq = p.ve(lambda e: e.tensor_mul(out=sclf_sq[:], in0=sclf_sq[:],
                                                   in1=sclf_sq[:]), deps=[t_rc])
                toks.append(t_sq)
                rowd = []
                for j in range(32):
                    rowd.append(p.dma("sq", lambda e, j=j: e.dma_start(
                        out=s1r_scr[0:1, j * 128:(j + 1) * 128],
                        in_=rcpf[:, j:j + 1]), deps=[t_rc], tag=f"s1r{j}"))
                btm = buf("sc")
                red2 = []
                for j in range(8):
                    t_l = p.dma("sq", lambda e, j=j: e.dma_start(
                        out=sc[:], in_=wproj[j * 128:(j + 1) * 128, :]),
                        deps=btm.wdeps(), tag=f"pjn{j}")
                    btm.wrote(t_l)
                    t_r1 = btm.read(p.ve(lambda e, j=j: e.tensor_reduce(
                        out=sclp[:, j:j + 1], in_=sc[:],
                        axis=mybir.AxisListType.X, op=ALU.add,
                        apply_absolute_value=True), deps=[t_l]))
                    t_c2 = p.ve(lambda e, j=j: e.tensor_scalar(
                        out=sclp[:, j:j + 1], in0=sclp[:, j:j + 1], scalar1=1.0 / INTER,
                        scalar2=1e-5, op0=ALU.mult, op1=ALU.max), deps=[t_r1])
                    red2.append(t_c2)
                t_rc2 = p.ve(lambda e: e.reciprocal(out=rcpp[:], in_=sclp[:]), deps=red2)
                toks.append(t_rc2)
                rowd2 = []
                for j in range(8):
                    rowd2.append(p.dma("sq", lambda e, j=j: e.dma_start(
                        out=s2r_scr[0:1, j * 128:(j + 1) * 128],
                        in_=rcpp[:, j:j + 1]), deps=[t_rc2], tag=f"s2r{j}"))
                rg = reg("OB", 0)
                war = rg.wdeps()
                t_g = p.te(lambda e: e.matmul(poB[0:RANK, 0:1], ones_row_f[0:1, 0:RANK],
                                              graw[:], start=True, stop=True),
                           deps=INIT + war, tag="gatebc")
                rg.wrote(t_g)
                t_ge = rg.read(p.ve(lambda e: e.tensor_copy(
                    out=gcol[:], in_=poB[0:RANK, 0:1]), deps=[t_g]))
                toks.append(t_ge)
                st["rowd"] = rowd
                st["rowd2"] = rowd2
                return toks

            def job_tern(si):
                toks = []
                bs1 = buf("s1row")
                bs2 = buf("s2row")
                def run_jobs(jobs):
                    for ld, prc in jobs:
                        t_l = ld()
                        toks.append(prc(t_l))

                for icc in (2 * si, 2 * si + 1):
                    tern_jobs = []
                    t_row1 = p.dma("sq", lambda e, icc=icc: e.dma_start(
                        out=s1row[:], in_=s1r_scr[0:1, icc * 512:(icc + 1) * 512]),
                        deps=bs1.wdeps(st["rowd"]), tag=f"s1row{icc}")
                    bs1.wrote(t_row1)
                    t_bc = mm_group("OB", 1, [
                        (lambda: ones_row_f[:], lambda: s1row[:], None)],
                        deps=[t_row1], tag=f"bc1_{icc}")
                    bs1.read(t_bc)
                    brb = buf("rb512")
                    t_be = reg("OB", 1).read(p.ve(lambda e: e.tensor_copy(
                        out=rb512[:], in_=poB[:, 512:1024]), deps=brb.wdeps([t_bc])))
                    brb.wrote(t_be)
                    but = buf("uTf")
                    t_ul = p.dma("gq", lambda e, icc=icc: e.dma_start(
                        out=uTf[:], in_=uT[:, icc * 512:(icc + 1) * 512]),
                        deps=but.wdeps(), tag=f"uTl{icc}")
                    but.wrote(t_ul)
                    bub = buf("uTsb", icc % 2)
                    t_u = but.read(brb.read(p.ve(lambda e, icc=icc: e.tensor_tensor(
                        out=uTsb[icc % 2][:], in0=uTf[:],
                        in1=rb512[0:RANK, :], op=ALU.mult),
                        deps=bub.wdeps([t_be, t_ul]))))
                    bub.wrote(t_u)
                    t_us = p.dma("sq", lambda e, icc=icc: e.dma_start(
                        out=uts_scr[:, icc * 512:(icc + 1) * 512], in_=uTsb[icc % 2][:]),
                        deps=[t_u], tag=f"utst{icc}")
                    bub.read(t_us)
                    toks.append(t_us)
                    for dt in range(8):
                        def mk_load(icc=icc, dt=dt):
                            bpw = buf("pw", dt % 2)
                            t_l = p.dma("sq", lambda e: e.dma_start(
                                out=pwb[dt % 2][:],
                                in_=wfcT[dt * 128:(dt + 1) * 128,
                                         icc * 512:(icc + 1) * 512]),
                                deps=bpw.wdeps(), tag=f"fct{icc}_{dt}")
                            bpw.wrote(t_l)
                            return t_l

                        def mk_proc(t_l, icc=icc, dt=dt, brb=brb, t_be=t_be):
                            bpw = buf("pw", dt % 2)
                            bpt = buf("stg", dt % 2)
                            t_m = brb.read(p.ve(lambda e: e.tensor_tensor(
                                out=pwb[dt % 2][:], in0=pwb[dt % 2][:],
                                in1=rb512[:], op=ALU.mult), deps=[t_l, t_be]))
                            t_rn = p.ve(lambda e: e.tensor_scalar(
                                out=pwb[dt % 2][:], in0=pwb[dt % 2][:], scalar1=MAGIC,
                                scalar2=MAGIC, op0=ALU.add, op1=ALU.subtract), deps=[t_m])
                            t_cl = bpw.read(p.ve(lambda e: e.tensor_scalar(
                                out=stgb[dt % 2][:], in0=pwb[dt % 2][:], scalar1=1.0,
                                scalar2=-1.0, op0=ALU.min, op1=ALU.max),
                                deps=bpt.wdeps([t_rn])))
                            bpt.wrote(t_cl)
                            t_st2 = p.dma("sq", lambda e: e.dma_start(
                                out=w1s[1][:, icc * 4096 + dt * 512:
                                           icc * 4096 + (dt + 1) * 512],
                                in_=stgb[dt % 2][:]), deps=[t_cl],
                                tag=f"fcts{icc}_{dt}")
                            bpt.read(t_st2)
                            return t_st2
                        tern_jobs.append((mk_load, mk_proc))
                    run_jobs(tern_jobs)
                tern_jobs = []
                t_row2 = p.dma("sq", lambda e: e.dma_start(
                    out=s2row[:], in_=s2r_scr[0:1, 0:512]),
                    deps=bs2.wdeps(st["rowd2"]), tag=f"s2rowa{si}")
                bs2.wrote(t_row2)
                t_bc2 = mm_group("OB", 1, [
                    (lambda: ones_row_f[:], lambda: s2row[:], None)],
                    deps=[t_row2], tag=f"bc2a{si}")
                bs2.read(t_bc2)
                brb = buf("rb512")
                t_be2 = reg("OB", 1).read(p.ve(lambda e: e.tensor_copy(
                    out=rb512[:], in_=poB[:, 512:1024]), deps=brb.wdeps([t_bc2])))
                brb.wrote(t_be2)
                t_row3 = p.dma("sq", lambda e: e.dma_start(
                    out=s2row[:], in_=s2r_scr[0:1, 512:1024]), deps=bs2.wdeps(),
                    tag=f"s2rowb{si}")
                bs2.wrote(t_row3)
                t_bc3 = mm_group("OB", 0, [
                    (lambda: ones_row_f[:], lambda: s2row[:], None)],
                    deps=[t_row3], tag=f"bc2b{si}")
                bs2.read(t_bc3)
                brc = buf("rb512b")
                t_be3 = reg("OB", 0).read(p.ve(lambda e: e.tensor_copy(
                    out=rb5122[:], in_=poB[:, 0:512]), deps=brc.wdeps([t_bc3])))
                brc.wrote(t_be3)
                ji = 0
                for jt in range(8 * si, 8 * si + 8):
                    for hhalf in (0, 1):
                        brx = brb if hhalf == 0 else brc
                        rbsel = rb512 if hhalf == 0 else rb5122
                        t_bex = t_be2 if hhalf == 0 else t_be3
                        bi_ = ji % 2
                        ji += 1

                        def mk_load(jt=jt, hhalf=hhalf, bi_=bi_):
                            bpw = buf("pw", bi_)
                            t_l = p.dma("sq", lambda e: e.dma_start(
                                out=pwb[bi_][:],
                                in_=wprojT[jt * 128:(jt + 1) * 128,
                                           hhalf * 512:(hhalf + 1) * 512]),
                                deps=bpw.wdeps(), tag=f"pjt{jt}_{hhalf}")
                            bpw.wrote(t_l)
                            return t_l

                        def mk_proc(t_l, jt=jt, hhalf=hhalf, brx=brx,
                                    rbsel=rbsel, t_bex=t_bex, bi_=bi_):
                            bpw = buf("pw", bi_)
                            bpt = buf("stg", bi_)
                            t_m1 = brx.read(p.ve(lambda e: e.tensor_tensor(
                                out=pwb[bi_][:], in0=pwb[bi_][:],
                                in1=rbsel[:], op=ALU.mult), deps=[t_l, t_bex]))
                            t_r1 = p.ve(lambda e: e.tensor_scalar(
                                out=pwb[bi_][:], in0=pwb[bi_][:], scalar1=MAGIC,
                                scalar2=MAGIC, op0=ALU.add, op1=ALU.subtract),
                                deps=[t_m1])
                            t_c1 = bpw.read(p.ve(lambda e: e.tensor_scalar(
                                out=stgb[bi_][:], in0=pwb[bi_][:], scalar1=1.0,
                                scalar2=-1.0, op0=ALU.min, op1=ALU.max),
                                deps=bpt.wdeps([t_r1])))
                            bpt.wrote(t_c1)
                            t_st2 = p.dma("sq", lambda e: e.dma_start(
                                out=w2s[1].rearrange(
                                    "p (ft it f) -> p ft it f", ft=8, it=32)[
                                    :, hhalf * 4:(hhalf + 1) * 4, jt, :],
                                in_=stgb[bi_].ap().rearrange(
                                    "p (ff f) -> p ff f", ff=4)),
                                deps=[t_c1], tag=f"pjts{jt}_{hhalf}")
                            bpt.read(t_st2)
                            return t_st2
                        tern_jobs.append((mk_load, mk_proc))
                run_jobs(tern_jobs)
                return toks

            return job_scales, job_tern


        def tap_dump(name, src_ap, deps):
            if name in tap_aps:
                return [p.dma("gq", lambda e: e.dma_start(out=tap_aps[name], in_=src_ap),
                              deps=deps, tag="tap_" + name)]
            return []

        final_stores = []
        all_taps = []
        prep1_toks = []
        prev_xsts = []
        for L in range(n_layers):
            xsrc = xin if L == 0 else x_scr
            xdst = (y if n_layers == 1 else x_scr) if L == 0 else y
            wprep = L0W_TOKS if L == 0 else prep1_toks
            ln_toks = []
            kv_stores = []
            for c in range(NCHUNK):
                extra = [prev_xsts[c]] if (L == 1 and prev_xsts) else []
                ln_toks.append(phase_ln(L, c, xsrc, extra_deps=extra))
                kv_stores += phase_kv(L, c, ln_toks[c])
            tapd = tap_dump(f"xn{L}", xn_scr, ln_toks)
            tapd += tap_dump(f"k{L}", k_scr, kv_stores)
            tapd += tap_dump(f"vp{L}", vp_scr, kv_stores)
            all_taps += tapd
            prep_tern = None
            if L == 0 and n_layers == 2:
                job_scales, prep_tern = make_l1_prep()
                prep1_toks = job_scales()
            xsts = []
            for c in range(NCHUNK):
                par = c % 2
                bxn = buf("xnb", par)
                t_xn = p.dma("wq", lambda e, c=c, par=par: e.dma_start(
                    out=xnb[par][:], in_=xn_scr[:, c * 4096:(c + 1) * 4096]),
                    deps=bxn.wdeps([ln_toks[c]]), tag=f"L{L}c{c}:xn3")
                bxn.wrote(t_xn)
                s_toks = phase_attn(L, c, kv_stores + tapd + VPPAD, t_xn)
                t_xst = phase_ffn(L, c, s_toks, t_xn, xsrc, xdst, list(wprep))
                xsts.append(t_xst)
                if prep_tern is not None:
                    prep1_toks += prep_tern(c)
            if L == 0 and n_layers == 2 and "x0" in tap_aps:
                t_tap = p.dma("gq", lambda e: e.dma_start(
                    out=tap_aps["x0"], in_=x_scr), deps=list(xsts), tag="tapx0")
                all_taps.append(t_tap)
            if L == 0 and n_layers == 2 and "tfcT" in tap_aps:
                all_taps.append(p.dma("gq", lambda e: e.dma_start(
                    out=tap_aps["tfcT"], in_=w1s[1]), deps=list(prep1_toks), tag="taptf"))
            if L == 0 and n_layers == 2 and "tpjT" in tap_aps:
                all_taps.append(p.dma("gq", lambda e: e.dma_start(
                    out=tap_aps["tpjT"], in_=w2s[1]), deps=list(prep1_toks), tag="taptp"))
            prev_xsts = xsts
            final_stores = xsts

        p.po(lambda e: e.nop(), deps=final_stores + all_taps, tag="fence")

        p.finalize()
        p.replay(nc, sems)
        nc.all_engine_barrier()
    return nc


_NC_CACHE = {}


def _get_nc(n_layers=2, taps=()):
    key = (n_layers, tuple(taps))
    if key not in _NC_CACHE:
        _NC_CACHE[key] = build_nc(n_layers, taps)
    return _NC_CACHE[key]


def make_in_maps(inputs):
    perm = _parity_perm()
    wqkv = np.asarray(inputs["L1_attn_w_qkv"], np.float32)
    wq1 = wqkv[0:D][perm]
    wk1 = wqkv[D:2 * D][perm]
    wv1 = wqkv[2 * D:3 * D]
    wqkvT = np.ascontiguousarray(np.concatenate([wq1, wk1, wv1], 0).T)
    T1, T2 = _rope_tables()
    R1T, R2T = _rot_mats()
    wfc = np.ascontiguousarray(np.asarray(inputs["L1_ffn_w_fc"], np.float32))
    wproj = np.ascontiguousarray(np.asarray(inputs["L1_ffn_w_proj"], np.float32))
    common = {
        "wqT0": np.ascontiguousarray(np.asarray(inputs["L0_attn_Wq"], np.float32).T),
        "wkT0": np.ascontiguousarray(np.asarray(inputs["L0_attn_Wk"], np.float32).T),
        "wvT0": np.ascontiguousarray(np.asarray(inputs["L0_attn_Wv"], np.float32).T),
        "w1": np.ascontiguousarray(np.asarray(inputs["L0_ffn_W1"], np.float32)),
        "b1": np.ascontiguousarray(np.asarray(inputs["L0_ffn_b1"], np.float32).reshape(32, 128).T),
        "w2": np.ascontiguousarray(np.asarray(inputs["L0_ffn_W2"], np.float32)),
        "b2": np.ascontiguousarray(np.asarray(inputs["L0_ffn_b2"], np.float32).reshape(8, 128).T),
        "wqkvT": wqkvT,
        "woutT": np.ascontiguousarray(np.asarray(inputs["L1_attn_w_out"], np.float32).T),
        "bout": np.ascontiguousarray(np.asarray(inputs["L1_attn_b_out"], np.float32).reshape(8, 128).T),
        "wfc": wfc,
        "wfcT": np.ascontiguousarray(wfc.T),
        "wproj": wproj,
        "wprojT": np.ascontiguousarray(wproj.T),
        "uT": np.ascontiguousarray(np.asarray(inputs["L1_ffn_u"], np.float32).T),
        "vT": np.ascontiguousarray(np.asarray(inputs["L1_ffn_v"], np.float32).T),
        "fbT": np.ascontiguousarray(np.asarray(inputs["L1_ffn_fast_b"], np.float32).T),
        "gate": np.asarray(inputs["L1_ffn_gate"], np.float32).reshape(1),
        "t1c": T1, "t2c": T2, "r1T": R1T, "r2T": R2T,
        "indm": _indmat(),
    }
    x = np.asarray(inputs["L0_attn_x"], np.float32)
    in_maps = []
    for i in range(8):
        m = dict(common)
        xt = x[i].reshape(NCHUNK, CH, 8, 128)
        m["x"] = np.ascontiguousarray(xt.transpose(3, 0, 2, 1).reshape(128, NCHUNK * 4096))
        in_maps.append(m)
    return in_maps


def kernel(**inputs):
    in_maps = make_in_maps(inputs)
    nc = _get_nc()
    res = run_bass_kernel_spmd(nc, in_maps, core_ids=list(range(8)))
    outs = []
    for r in res.results:
        yt = np.asarray(r["y"]).reshape(128, NCHUNK, 8, CH)
        outs.append(yt.transpose(1, 3, 2, 0).reshape(S, D))
    return np.stack(outs, 0).astype(np.float32)


# version 17
# speedup vs baseline: 1.0224x; 1.0224x over previous
import numpy as np
import ml_dtypes
import concourse.bass as bass
import concourse.mybir as mybir
from concourse.bass_utils import run_bass_kernel_spmd

F32 = mybir.dt.float32
BF16 = mybir.dt.bfloat16
AF = mybir.ActivationFunctionType
ALU = mybir.AluOpType

D = 1024
S = 2048
NCHUNK = 4
CH = S // NCHUNK
NH = 16
HD = 64
NPAIR = 8
NKT = 16
CSHIFT = 6.0
EPS = 1e-5
MAGIC = 12582912.0
INTER = 4096
RANK = 16


class Step:
    __slots__ = ("eng", "sem", "inc", "fn", "deps", "need", "val", "tag")

    def __init__(self, eng, sem, inc, fn, deps, tag):
        self.eng, self.sem, self.inc, self.fn = eng, sem, inc, fn
        self.deps = [d for d in deps if d is not None]
        self.need = False
        self.val = None
        self.tag = tag


class Prog:
    QENG = {"wq": "sync", "sq": "scalar", "gq": "gpsimd"}

    def __init__(self):
        self.steps = []

    def _add(self, eng, sem, inc, fn, deps, tag):
        st = Step(eng, sem, inc, fn, deps, tag)
        for d in st.deps:
            d.need = True
        self.steps.append(st)
        return st

    def dma(self, q, fn, deps=(), tag=""):
        st = self._add(self.QENG[q], q, 16, fn, deps, tag)
        st.need = True
        return st

    def te(self, fn, deps=(), tag=""):
        return self._add("tensor", "te", 1, fn, deps, tag)

    def ve(self, fn, deps=(), tag=""):
        return self._add("vector", "ve", 1, fn, deps, tag)

    def ac(self, fn, deps=(), tag=""):
        return self._add("scalar", "ac", 1, fn, deps, tag)

    def po(self, fn, deps=(), tag=""):
        return self._add("gpsimd", "po", 1, fn, deps, tag)

    def finalize(self):
        cnt = {}
        for st in self.steps:
            if st.need:
                cnt[st.sem] = cnt.get(st.sem, 0) + st.inc
                st.val = cnt[st.sem]

    def replay(self, nc, sems):
        with nc.Block() as blk:
            for engname in ("sync", "vector", "scalar", "tensor", "gpsimd"):
                mysteps = [s for s in self.steps if s.eng == engname]

                def body(eng, mysteps=mysteps):
                    hwm = {}
                    for st in mysteps:
                        for d in st.deps:
                            if d.val is not None and hwm.get(d.sem, 0) < d.val:
                                eng.wait_ge(sems[d.sem], d.val)
                                hwm[d.sem] = d.val
                        inst = st.fn(eng)
                        if st.need:
                            inst.then_inc(sems[st.sem], st.inc)

                getattr(blk, engname)(body)


class Buf:

    def __init__(self):
        self.users = []
        self.writer = None

    def wdeps(self, extra=()):
        d = list(extra) + self.users
        self.users = []
        return d

    def wrote(self, step):
        self.writer = step
        return step

    def read(self, step):
        self.users.append(step)
        return step

    def peek(self):
        return list(self.users)


def _rope_tables():
    inv_freq = 1.0 / 10000.0 ** (np.arange(0, HD, 2, dtype=np.float32) / HD)
    t = np.arange(S, dtype=np.float32)
    fr = np.outer(t, inv_freq)
    cosT = np.cos(fr).T.astype(np.float32)
    sinT = np.sin(fr).T.astype(np.float32)
    T1 = np.concatenate([cosT, sinT, cosT, sinT], 0)
    T2 = np.concatenate([sinT, cosT, sinT, cosT], 0)
    return np.ascontiguousarray(T1), np.ascontiguousarray(T2)


def _rot_mats():
    R1 = np.zeros((128, 128), np.float32)
    R2 = np.zeros((128, 128), np.float32)
    for base in (0, 64):
        for r in range(32):
            R1[base + r, base + r] = 1.0
            R1[base + r, base + r + 32] = -1.0
            R2[base + 32 + r, base + r] = 1.0
            R2[base + 32 + r, base + 32 + r] = 1.0
    return np.ascontiguousarray(R1.T), np.ascontiguousarray(R2.T)


def _parity_perm():
    p = np.zeros(D, np.int64)
    for h in range(NH):
        b = h * HD
        p[b:b + 32] = b + np.arange(0, HD, 2)
        p[b + 32:b + HD] = b + np.arange(1, HD, 2)
    return p


def _indmat():
    ind = np.zeros((NH, D), np.float32)
    for h in range(NH):
        ind[h, h * HD:(h + 1) * HD] = 1.0
    return ind


def build_nc(n_layers=2, taps=()):
    nc = bass.Bass("TRN2", target_bir_lowering=False)

    def ein(name, shape):
        return nc.dram_tensor(name, list(shape), F32, kind="ExternalInput").ap()

    xin = ein("x", [128, NCHUNK * 4096])
    y = nc.dram_tensor("y", [128, NCHUNK * 4096], F32, kind="ExternalOutput").ap()

    wqT0 = ein("wqT0", [D, D]); wkT0 = ein("wkT0", [D, D]); wvT0 = ein("wvT0", [D, D])
    w1 = ein("w1", [D, INTER]); b1 = ein("b1", [128, 32])
    w2 = ein("w2", [INTER, D]); b2 = ein("b2", [128, 8])
    wqkvT = ein("wqkvT", [D, 3 * D])
    woutT = ein("woutT", [D, D]); bout = ein("bout", [128, 8])
    wfc = ein("wfc", [INTER, D]); wfcT = ein("wfcT", [D, INTER])
    wproj = ein("wproj", [D, INTER]); wprojT = ein("wprojT", [INTER, D])
    uT = ein("uT", [RANK, INTER]); vT = ein("vT", [D, RANK])
    fbT = ein("fbT", [RANK, RANK]); gate = ein("gate", [1])
    t1c = ein("t1c", [128, S]); t2c = ein("t2c", [128, S])
    r1T = ein("r1T", [128, 128]); r2T = ein("r2T", [128, 128])
    indmat = ein("indm", [NH, D])

    def scr(name, shape, dtype):
        return nc.dram_tensor(name, list(shape), dtype, kind="Internal").ap()

    x_scr = scr("x_scr", [128, NCHUNK * 4096], F32)
    xn_scr = scr("xn_scr", [128, NCHUNK * 4096], BF16)
    k_scr = scr("k_scr", [D, S], BF16)
    vp_scr = scr("vp_scr", [128, NPAIR * NKT * 132], BF16)
    k_scr1 = scr("k_scr1", [D, S], BF16)
    vp_scr1 = scr("vp_scr1", [128, NPAIR * NKT * 132], BF16)
    den_scr = scr("den_scr", [NH, CH], F32)
    w1s = [scr("w1bf", [128, 8 * 4096], BF16), scr("tfcT", [128, 8 * 4096], BF16)]
    w2s = [scr("w2bf", [128, 8 * 4096], BF16), scr("tpjT", [128, 8 * 4096], BF16)]
    s1r_scr = scr("s1r_scr", [1, INTER], F32)
    qrow_scr = scr("qrow_scr", [NH, CH], F32)
    uts_scr = scr("uts_scr", [RANK, INTER], BF16)
    s2r_scr = scr("s2r_scr", [1, D], F32)

    tap_aps = {}
    for name, shape in taps:
        tap_aps[name] = nc.dram_tensor("tap_" + name, list(shape), F32,
                                       kind="ExternalOutput").ap()

    with nc.cleanup_on_exit():
        A = nc.alloc_sbuf_tensor
        xnb = [A("xnb0", [128, 8 * CH], BF16), A("xnb1", [128, 8 * CH], BF16)]
        xcb = A("xcb", [128, 8 * CH], F32)
        khaug = [A("kha0", [65, S], BF16), A("kha1", [65, S], BF16)]
        vhb = [A("vh0", [128, NKT * 132], BF16), A("vh1", [128, NKT * 132], BF16)]
        qc = A("qc", [128, 8 * CH], BF16)
        qaug = A("qaug", [65, 16 * CH], BF16)
        eabb = [A("eab0", [128, 1024], BF16), A("eab1", [128, 1024], BF16)]
        sc = A("sc", [128, 8 * CH], F32)
        hc = A("hc", [128, 32 * 512], BF16)
        w1cb = [A("w1c0", [128, 4096], BF16), A("w1c1", [128, 4096], BF16)]
        w2cb = [A("w2c0", [128, 4096], BF16), A("w2c1", [128, 4096], BF16)]
        stgb = [A("stg0", [128, 512], BF16), A("stg1", [128, 512], BF16)]
        vstg = A("vstg", [128, 528], BF16)
        m1t = A("m1t", [128, 512], BF16)
        m2t = A("m2t", [128, 512], BF16)
        zt = A("zt", [128, 512], F32)
        ut = A("ut", [128, 512], F32)
        ttt = A("ttt", [128, 512], F32)
        t1b = A("t1b", [128, S], BF16)
        t2b = A("t2b", [128, S], BF16)
        r1b = A("r1b", [128, 128], BF16)
        r2b = A("r2b", [128, 128], BF16)
        indb = A("indb", [NH, D], F32)
        macc256 = A("macc256", [128, 256], F32)
        mcols = A("mcols", [128, 64], F32)
        negLB = [A("negLB0", [128, 1], F32), A("negLB1", [128, 1], F32)]
        zero_col = A("zero_col", [128, 1], F32)
        tiny_col = A("tiny_col", [128, 1], F32)
        ones_col_bf = A("ones_col_bf", [128, 1], BF16)
        ones_row_f = A("ones_row_f", [1, 128], F32)
        one_col = A("one_col", [128, 1], F32)
        negC_col = A("negC_col", [128, 1], F32)
        eps1 = A("eps1", [1, 1], F32)
        b1c = A("b1c", [128, 32], F32)
        b2c = A("b2c", [128, 8], F32)
        boutc = A("boutc", [128, 8], F32)
        mrow = A("mrow", [1, CH], F32)
        msqrow = A("msqrow", [1, CH], F32)
        rstdrow = A("rstdrow", [1, CH], F32)
        mrrow = A("mrrow", [1, CH], F32)
        den16 = A("den16", [NH, CH], F32)
        uTsb = [A("uTs0", [RANK, 512], BF16), A("uTs1", [RANK, 512], BF16)]
        uTf = A("uTf", [RANK, 512], BF16)
        vTb = A("vTb", [128, 8 * RANK], BF16)
        fbTb = A("fbTb", [RANK, RANK], BF16)
        gcol = A("gcol", [RANK, 1], F32)
        graw = A("graw", [1, 1], F32)
        z1b = A("z1b", [RANK, CH], BF16)
        z2g = A("z2g", [RANK, CH], BF16)
        sclf_sq = A("sclf_sq", [128, 32], F32)
        sclp = A("sclp", [128, 8], F32)
        rcpf = A("rcpf", [128, 32], F32)
        rcpp = A("rcpp", [128, 8], F32)
        s1row = mrow
        s2row = msqrow
        rb512 = A("rb512", [128, 512], F32)
        rb5122 = A("rb5122", [128, 512], F32)
        pwb = [A("pw0", [128, 512], F32), A("pw1", [128, 512], F32)]

        psA = nc.alloc_psum_tensor("psA", [128, 1024], F32)
        psB = nc.alloc_psum_tensor("psB", [128, 1024], F32)
        poA = nc.alloc_psum_tensor("poA", [128, 1024], F32)
        poB = nc.alloc_psum_tensor("poB", [128, 1024], F32)
        PS = {"A": psA, "B": psB, "OA": poA, "OB": poB}
        khp = [hc.ap()[:, 8192:10240], hc.ap()[:, 10240:12288]]

        sems = {k: nc.alloc_semaphore(k) for k in
                ("wq", "sq", "gq", "te", "ve", "ac", "po")}

        p = Prog()
        B = {}

        def buf(name, i=0):
            key = (name, i)
            if key not in B:
                B[key] = Buf()
            return B[key]

        def reg(name, half):
            return buf("ps" + name, half)

        KS = [k_scr, k_scr1]
        VPS = [vp_scr, vp_scr1]

        def psap(name, half, p0=0, p1=128):
            return PS[name][p0:p1, half * 512:half * 512 + 512]

        def mm_group(rname, half, mms, deps=(), tag="", p0=0, p1=128):
            r = reg(rname, half)
            war = r.wdeps()
            last = None
            n = len(mms)
            for i, (lf, rf, tp) in enumerate(mms):
                dd = (list(deps) + war) if i == 0 else []
                last = p.te(lambda e, lf=lf, rf=rf, tp=tp, st=(i == 0),
                            sp=(i == n - 1), p0=p0, p1=p1, half=half, rname=rname:
                            e.matmul(psap(rname, half, p0, p1), lf(), rf(),
                                     start=st, stop=sp, tile_position=tp),
                            deps=dd, tag=tag)
            r.wrote(last)
            return last

        INIT = []
        for fn in (lambda e: e.memset(ones_col_bf[:], 1.0),
                   lambda e: e.memset(ones_row_f[:], 1.0),
                   lambda e: e.memset(one_col[:], 1.0),
                   lambda e: e.memset(negC_col[:], -CSHIFT),
                   lambda e: e.memset(eps1[:], EPS)):
            INIT.append(p.ve(fn))
        INIT.append(p.dma("wq", lambda e: e.dma_start(out=indb[:], in_=indmat)))
        LB = [(150.0, 87.0 / 300.0, 1.0 / 32.0), (300.0, 87.0 / 600.0, 1.0 / 8.0)]
        for Li in (0, 1):
            Bc, lam, scl_ = LB[Li]
            INIT.append(p.ve(lambda e, Li=Li, v=-LB[Li][1] * LB[Li][0]:
                             e.memset(negLB[Li][:], v)))
        INIT.append(p.ve(lambda e: e.memset(zero_col[:], 0.0)))
        INIT.append(p.ve(lambda e: e.memset(tiny_col[:], 1e-38)))
        for b_ in range(2):
            INIT.append(p.ve(lambda e, b_=b_: e.memset(khaug[b_][64:65, :], 1.0)))
        INIT.append(p.ve(lambda e: e.memset(qaug[64:65, :], -640.0)))
        for dst, src in ((t1b, t1c), (t2b, t2c)):
            INIT.append(p.dma("gq", lambda e, d=dst, s=src: e.dma_start(out=d[:], in_=s)))
        for dst, src in ((r1b, r1T), (r2b, r2T), (fbTb, fbT)):
            INIT.append(p.dma("gq", lambda e, d=dst, s=src: e.dma_start(out=d[:], in_=s)))
        INIT.append(p.dma("gq", lambda e: e.dma_start(out=b1c[:], in_=b1)))
        INIT.append(p.dma("gq", lambda e: e.dma_start(out=b2c[:], in_=b2)))
        INIT.append(p.dma("gq", lambda e: e.dma_start(out=boutc[:], in_=bout)))
        INIT.append(p.dma("gq", lambda e: e.dma_start(
            out=vTb.ap().rearrange("p (t r) -> p t r", t=8),
            in_=vT.rearrange("(t p) r -> p t r", p=128))))
        INIT.append(p.dma("gq", lambda e: e.dma_start(
            out=graw[:], in_=gate.unsqueeze(0))))
        VPPAD = []
        INIT.append(p.ve(lambda e: e.memset(
            vstg.ap().rearrange("p (h z) -> p h z", h=8)[:, :, 64:66], 1.0)))

        L0W_TOKS = []
        for ic in range(8):
            L0W_TOKS.append(p.dma("gq", lambda e, ic=ic: e.dma_start(
                out=w1s[0][:, ic * 4096:(ic + 1) * 4096].rearrange(
                    "p (dt t) -> p dt t", dt=8),
                in_=w1.rearrange("(dt p) i -> p dt i", p=128)[:, :, ic * 512:(ic + 1) * 512]),
                tag=f"w1cast{ic}"))
        for ft in range(8):
            L0W_TOKS.append(p.dma("gq", lambda e, ft=ft: e.dma_start(
                out=w2s[0][:, ft * 4096:(ft + 1) * 4096].rearrange(
                    "p (it f) -> p it f", it=32),
                in_=w2.rearrange("(it p) f -> p it f", p=128)[:, :, ft * 128:(ft + 1) * 128]),
                tag=f"w2cast{ft}"))

        def load_wT_fchunk(whbm, col0, bufname, bi, deps=(), tag=""):
            bb = buf(bufname, bi)
            dst = (w1cb if bufname == "w1c" else w2cb)[bi]
            t = p.dma("gq", lambda e: e.dma_start(
                out=dst.ap().rearrange("p (dt t) -> p dt t", dt=8),
                in_=whbm[:, col0:col0 + 512].rearrange("(dt p) t -> p dt t", p=128)),
                deps=bb.wdeps(deps), tag=tag)
            bb.wrote(t)
            return t

        def load_x_chunk(xsrc, c, tag):
            return p.dma("gq", lambda e: e.dma_start(
                out=xcb[:], in_=xsrc[:, c * 4096:(c + 1) * 4096]),
                deps=buf("xcb").wdeps(), tag=tag)

        def phase_ln(L, c, xsrc, extra_deps=(), x_res=None, st_extra=()):
            par = c % 2
            bx = buf("xcb")
            if x_res is None:
                t_x = load_x_chunk(xsrc, c, f"L{L}c{c}:x1")
                for d in extra_deps:
                    t_x.deps.append(d); d.need = True
                bx.wrote(t_x)
                xdeps = [t_x]
            else:
                xdeps = list(x_res)
            bq = buf("qc")
            t_cast = bx.read(p.ve(lambda e: e.tensor_copy(out=qc[:], in_=xcb[:]),
                                  deps=bq.wdeps(xdeps)))
            bq.wrote(t_cast)
            bh = buf("hc")
            t_sq = bq.read(p.ve(lambda e: e.tensor_mul(out=hc[:, 0:4096], in0=qc[:], in1=qc[:]),
                                deps=bh.wdeps([t_cast])))
            bh.wrote(t_sq)
            t_s1 = mm_group("A", 0, [
                (lambda: ones_col_bf[:, 0:1],
                 lambda dt=dt: qc[:, dt * CH:(dt + 1) * CH], None) for dt in range(8)],
                deps=[t_cast] + INIT, tag=f"L{L}c{c}:lnsum", p0=0, p1=1)
            bq.read(t_s1)
            t_s2 = mm_group("A", 1, [
                (lambda: ones_col_bf[:, 0:1],
                 lambda dt=dt: hc[:, dt * 512:(dt + 1) * 512], None) for dt in range(8)],
                deps=[t_sq], tag=f"L{L}c{c}:lnsq", p0=0, p1=1)
            bh.read(t_s2)
            bmr, bms = buf("s1row"), buf("s2row")
            brs, bmrr = buf("denrowA"), buf("denrowB")
            t_m = reg("A", 0).read(p.ve(lambda e: e.tensor_scalar_mul(
                out=mrow[:], in0=psA[0:1, 0:512], scalar1=1.0 / D),
                deps=[t_s1] + bmr.wdeps()))
            bmr.wrote(t_m)
            t_mq = reg("A", 1).read(p.ve(lambda e: e.tensor_scalar_mul(
                out=msqrow[:], in0=psA[0:1, 512:1024], scalar1=1.0 / D),
                deps=[t_s2] + bms.wdeps()))
            bms.wrote(t_mq)
            t_v1 = p.ve(lambda e: e.tensor_mul(out=rstdrow[:], in0=mrow[:], in1=mrow[:]),
                        deps=[t_m] + brs.wdeps())
            brs.wrote(t_v1)
            t_v2 = p.ve(lambda e: e.tensor_sub(out=msqrow[:], in0=msqrow[:], in1=rstdrow[:]),
                        deps=[t_v1, t_mq])
            t_l = p.ac(lambda e: e.activation(out=rstdrow[:], in_=msqrow[:], func=AF.Ln,
                                              bias=eps1[:, 0:1]), deps=[t_v2])
            bms.read(t_l)
            t_r = p.ac(lambda e: e.activation(out=rstdrow[:], in_=rstdrow[:], func=AF.Exp,
                                              scale=-0.5), deps=[t_l])
            t_mr = p.ve(lambda e: e.tensor_mul(out=mrrow[:], in0=mrow[:], in1=rstdrow[:]),
                        deps=[t_r, t_m] + bmrr.wdeps())
            bmrr.wrote(t_mr)
            bmr.read(t_mr)
            t_b1 = mm_group("B", 0, [(lambda: ones_row_f[:], lambda: rstdrow[:], None)],
                            deps=[t_r], tag="lnb1")
            brs.read(t_b1)
            t_b2 = mm_group("B", 1, [(lambda: ones_row_f[:], lambda: mrrow[:], None)],
                            deps=[t_mr], tag="lnb2")
            bmrr.read(t_b2)
            bsc = buf("sc")
            tt1 = []
            for dt in range(8):
                dd = [t_b1] + xdeps[-1:] + (bsc.wdeps() if dt == 0 else [])
                tt1.append(reg("B", 0).read(bx.read(p.ve(lambda e, dt=dt: e.tensor_tensor(
                    out=sc[:, dt * CH:(dt + 1) * CH], in0=xcb[:, dt * CH:(dt + 1) * CH],
                    in1=psB[:, 0:512], op=ALU.mult), deps=dd))))
            bsc.wrote(tt1[-1])
            bxn = buf("xnb", par)
            tt2 = []
            for dt in range(8):
                dd = [t_b2, tt1[dt]] + (bxn.wdeps() if dt == 0 else [])
                tt2.append(bsc.read(reg("B", 1).read(p.ve(lambda e, dt=dt: e.tensor_tensor(
                    out=xnb[par][:, dt * CH:(dt + 1) * CH],
                    in0=sc[:, dt * CH:(dt + 1) * CH],
                    in1=psB[:, 512:1024], op=ALU.subtract), deps=dd))))
            bxn.wrote(tt2[-1])
            t_st = p.dma("wq", lambda e: e.dma_start(
                out=xn_scr[:, c * 4096:(c + 1) * 4096], in_=xnb[par][:]),
                deps=tt2 + list(st_extra), tag=f"L{L}c{c}:xnst")
            bxn.read(t_st)
            return t_st, tt2[-1]

        def phase_kv(L, c, xn_st, xn_res=None):
            par = c % 2
            bxn = buf("xnb", par)
            if xn_res is None:
                t_xn = p.dma("wq", lambda e: e.dma_start(
                    out=xnb[par][:], in_=xn_scr[:, c * 4096:(c + 1) * 4096]),
                    deps=bxn.wdeps([xn_st]), tag=f"L{L}c{c}:xn2")
                bxn.wrote(t_xn)
            else:
                t_xn = xn_res
            wk = wkT0 if L == 0 else wqkvT[:, D:2 * D]
            wv = wvT0 if L == 0 else wqkvT[:, 2 * D:3 * D]
            stores = []
            for fc in range(2):
                bi = fc % 2
                t_w = load_wT_fchunk(wk, fc * 512, "w1c", bi, tag=f"L{L}c{c}:wk{fc}")
                for fi in range(4):
                    ft = fc * 4 + fi
                    hlf = ft % 2
                    t_mm = mm_group("A", hlf, [
                        (lambda dt=dt, fi=fi, bi=bi:
                         w1cb[bi][:, dt * 512 + fi * 128:dt * 512 + fi * 128 + 128],
                         lambda dt=dt: xnb[par][:, dt * CH:(dt + 1) * CH], None)
                        for dt in range(8)],
                        deps=[t_w, t_xn], tag=f"L{L}c{c}:kmm{ft}")
                    buf("w1c", bi).read(t_mm)
                    bxn.read(t_mm)
                    sb = ft % 2
                    bstg = buf("stg", sb)
                    if L == 0:
                        t_ev = reg("A", hlf).read(p.ve(
                            lambda e, sb=sb, hlf=hlf: e.tensor_copy(
                                out=stgb[sb][:], in_=psap("A", hlf)),
                            deps=bstg.wdeps([t_mm])))
                    else:
                        bm1, bm2 = buf("m1"), buf("m2")
                        t_m1 = reg("A", hlf).read(p.ve(lambda e, hlf=hlf, c=c: e.tensor_tensor(
                            out=m1t[:], in0=psap("A", hlf),
                            in1=t1b[:, c * CH:(c + 1) * CH], op=ALU.mult),
                            deps=bm1.wdeps([t_mm]) + INIT))
                        bm1.wrote(t_m1)
                        t_m2 = reg("A", hlf).read(p.ve(lambda e, hlf=hlf, c=c: e.tensor_tensor(
                            out=m2t[:], in0=psap("A", hlf),
                            in1=t2b[:, c * CH:(c + 1) * CH], op=ALU.mult),
                            deps=bm2.wdeps([t_mm])))
                        bm2.wrote(t_m2)
                        t_rot = mm_group("B", hlf, [
                            (lambda: r1b[:], lambda: m1t[:], None),
                            (lambda: r2b[:], lambda: m2t[:], None)],
                            deps=[t_m1, t_m2] + INIT, tag=f"L{L}c{c}:krot{ft}")
                        bm1.read(t_rot); bm2.read(t_rot)
                        t_ev = reg("B", hlf).read(p.ve(
                            lambda e, sb=sb, hlf=hlf: e.tensor_copy(
                                out=stgb[sb][:], in_=psap("B", hlf)),
                            deps=bstg.wdeps([t_rot])))
                    bstg.wrote(t_ev)
                    t_kst = p.dma("sq", lambda e, ft=ft, sb=sb, c=c, L=L: e.dma_start(
                        out=KS[L][ft * 128:(ft + 1) * 128, c * CH:(c + 1) * CH],
                        in_=stgb[sb][:]),
                        deps=[t_ev] + buf("kscr", L).wdeps(), tag=f"L{L}c{c}:kst{ft}")
                    bstg.read(t_kst)
                    stores.append(t_kst)
            for fc in range(2):
                bi = fc % 2
                t_w = load_wT_fchunk(wv, fc * 512, "w2c", bi, tag=f"L{L}c{c}:wv{fc}")
                for tt_ in range(4):
                    hlf = tt_ % 2
                    t_mm = mm_group("A", hlf, [
                        (lambda dt=dt, tt_=tt_:
                         xnb[par][:, dt * CH + tt_ * 128:dt * CH + tt_ * 128 + 128],
                         lambda dt=dt, bi=bi: w2cb[bi][:, dt * 512:(dt + 1) * 512], None)
                        for dt in range(8)],
                        deps=[t_w, t_xn], tag=f"L{L}c{c}:vmm{fc}_{tt_}")
                    buf("w2c", bi).read(t_mm)
                    bxn.read(t_mm)
                    bstg = buf("vstg")
                    t_ev = reg("A", hlf).read(p.ve(
                        lambda e, hlf=hlf: e.tensor_copy(
                            out=vstg.ap().rearrange("p (h z) -> p h z", h=8)[:, :, 0:64],
                            in_=psap("A", hlf)),
                        deps=bstg.wdeps([t_mm]) + INIT))
                    bstg.wrote(t_ev)
                    kti = c * 4 + tt_
                    t_vst = p.dma("sq", lambda e, kti=kti, fc=fc, L=L: e.dma_start(
                        out=VPS[L].rearrange("p (pr kt z) -> p pr kt z", pr=NPAIR, kt=NKT)[
                            :, fc * 4:(fc + 1) * 4, kti, :],
                        in_=vstg.ap().rearrange("p (pp z) -> p pp z", pp=4)),
                        deps=[t_ev] + buf("vpscr", L).wdeps(), tag=f"L{L}c{c}:vst{fc}_{tt_}")
                    bstg.read(t_vst)
                    stores.append(t_vst)
            return stores

        def phase_attn(L, c, kv_done, xn_tok):
            par = c % 2
            scale = (1.0 / 32.0) if L == 0 else (1.0 / 8.0)
            wqq = wqT0 if L == 0 else wqkvT[:, 0:D]
            bxn = buf("xnb", par)
            bqc = buf("qc")
            q_toks = []
            for fc in range(2):
                bi = fc % 2
                t_w = load_wT_fchunk(wqq, fc * 512, "w2c", bi, tag=f"L{L}c{c}:wq{fc}")
                for fi in range(4):
                    ft = fc * 4 + fi
                    hlf = ft % 2
                    t_mm = mm_group("A", hlf, [
                        (lambda dt=dt, fi=fi, bi=bi:
                         w2cb[bi][:, dt * 512 + fi * 128:dt * 512 + fi * 128 + 128],
                         lambda dt=dt: xnb[par][:, dt * CH:(dt + 1) * CH], None)
                        for dt in range(8)],
                        deps=[t_w, xn_tok], tag=f"L{L}c{c}:qmm{ft}")
                    buf("w2c", bi).read(t_mm)
                    bxn.read(t_mm)
                    wq_deps = bqc.wdeps() if (ft == 0 and L == 1) else []
                    bqa = buf("qaug")
                    wqa_deps = bqa.wdeps() if ft == 0 else []
                    if L == 0:
                        if f"q{L}c{c}" in tap_aps:
                            t_ev = reg("A", hlf).read(p.ve(
                                lambda e, ft=ft, hlf=hlf: e.tensor_copy(
                                    out=qc[:, ft * 512:(ft + 1) * 512],
                                    in_=psap("A", hlf)),
                                deps=[t_mm] + bqc.wdeps() if ft == 0 else [t_mm]))
                            bqc.wrote(t_ev)
                        else:
                            t_ev = None
                        t_a1 = reg("A", hlf).read(p.ve(lambda e, ft=ft, hlf=hlf: e.tensor_copy(
                            out=qaug[0:64, (2 * ft) * 512:(2 * ft + 1) * 512],
                            in_=psap("A", hlf, 0, 64)), deps=[t_mm] + wqa_deps))
                        t_a2 = reg("A", hlf).read(p.ve(lambda e, ft=ft, hlf=hlf: e.tensor_copy(
                            out=qaug[0:64, (2 * ft + 1) * 512:(2 * ft + 2) * 512],
                            in_=psap("A", hlf, 64, 128)), deps=[t_mm]))
                        bqa.wrote(t_a2)
                        q_toks.append(t_a1); q_toks.append(t_a2)
                    else:
                        bm1, bm2 = buf("m1"), buf("m2")
                        t_m1 = reg("A", hlf).read(p.ve(lambda e, hlf=hlf, c=c: e.tensor_tensor(
                            out=m1t[:], in0=psap("A", hlf),
                            in1=t1b[:, c * CH:(c + 1) * CH], op=ALU.mult),
                            deps=bm1.wdeps([t_mm]) + INIT))
                        bm1.wrote(t_m1)
                        t_m2 = reg("A", hlf).read(p.ve(lambda e, hlf=hlf, c=c: e.tensor_tensor(
                            out=m2t[:], in0=psap("A", hlf),
                            in1=t2b[:, c * CH:(c + 1) * CH], op=ALU.mult),
                            deps=bm2.wdeps([t_mm])))
                        bm2.wrote(t_m2)
                        t_rot = mm_group("B", hlf, [
                            (lambda: r1b[:], lambda: m1t[:], None),
                            (lambda: r2b[:], lambda: m2t[:], None)],
                            deps=[t_m1, t_m2], tag=f"L{L}c{c}:qrot{ft}")
                        bm1.read(t_rot); bm2.read(t_rot)
                        t_ev = reg("B", hlf).read(p.ve(lambda e, ft=ft, hlf=hlf: e.tensor_copy(
                            out=qc[:, ft * 512:(ft + 1) * 512], in_=psap("B", hlf)),
                            deps=[t_rot] + wq_deps))
                        t_a1 = reg("B", hlf).read(p.ve(lambda e, ft=ft, hlf=hlf: e.tensor_copy(
                            out=qaug[0:64, (2 * ft) * 512:(2 * ft + 1) * 512],
                            in_=psap("B", hlf, 0, 64)), deps=[t_rot] + wqa_deps))
                        t_a2 = reg("B", hlf).read(p.ve(lambda e, ft=ft, hlf=hlf: e.tensor_copy(
                            out=qaug[0:64, (2 * ft + 1) * 512:(2 * ft + 2) * 512],
                            in_=psap("B", hlf, 64, 128)), deps=[t_rot]))
                        bqa.wrote(t_a2)
                        q_toks.append(t_a1); q_toks.append(t_a2)
                    if t_ev is not None:
                        q_toks.append(t_ev)
            if L == 1 or f"q{L}c{c}" in tap_aps:
                bqc.wrote(q_toks[-1])
            if f"q{L}c{c}" in tap_aps:
                t_tapq = p.dma("gq", lambda e, L=L, c=c: e.dma_start(
                    out=tap_aps[f"q{L}c{c}"], in_=qc[:]), deps=list(q_toks), tag="tapq")
                bqc.read(t_tapq)
                q_toks = q_toks + [t_tapq]
            Bc, lam, _scl = LB[L]
            exp_toks = []
            qr_dmas = []
            bqrs = buf("qrow_scr")
            war_qrs = bqrs.wdeps() if L == 1 else []
            for pr in (range(NPAIR) if L == 1 else ()):
                pair_exp_toks = []
                kb = pr % 2
                bkhp = buf("khp", kb)
                t_khp = p.dma("wq", lambda e, pr=pr, kb=kb, L=L: e.dma_start(
                    out=khp[kb], in_=KS[L][pr * 128:(pr + 1) * 128, :]),
                    deps=bkhp.wdeps(kv_done) + buf("hc").peek(), tag=f"khpl{pr}")
                bkhp.wrote(t_khp)
                buf("kscr", L).read(t_khp)
                for qsub in range(4):
                    for kc in range(4):
                        sn = ("A", "B", "OA", "OB")[kc]
                        sten = PS[sn]
                        rs0, rs1 = reg(sn, 0), reg(sn, 1)
                        war_s = rs0.wdeps() + rs1.wdeps()
                        t_la = p.te(lambda e, kb=kb, kc=kc, pr=pr, qsub=qsub, sten=sten:
                                    e.matmul(sten[:, 0:512],
                                             qc[0:64, pr * 512 + qsub * 128:pr * 512 + qsub * 128 + 128],
                                             khp[kb][0:64, kc * 512:(kc + 1) * 512],
                                             start=True, stop=True, tile_position=(0, 0)),
                                    deps=[t_khp] + q_toks + war_s, tag=f"lA{pr}_{qsub}_{kc}")
                        t_lb = p.te(lambda e, kb=kb, kc=kc, pr=pr, qsub=qsub, sten=sten:
                                    e.matmul(sten[:, 512:1024],
                                             qc[64:128, pr * 512 + qsub * 128:pr * 512 + qsub * 128 + 128],
                                             khp[kb][64:128, kc * 512:(kc + 1) * 512],
                                             start=True, stop=True, tile_position=(64, 0)),
                                    deps=[], tag=f"lB{pr}_{qsub}_{kc}")
                        rs0.wrote(t_la); rs1.wrote(t_lb)
                        bkhp.read(t_la); bkhp.read(t_lb)
                        buf("hc").read(t_la); buf("hc").read(t_lb)
                        bqc.read(t_la); bqc.read(t_lb)
                        colA = (2 * pr) * 16 + qsub * 4 + kc
                        colB = (2 * pr + 1) * 16 + qsub * 4 + kc
                        if pr % 2 == 0:
                            t_ea = rs0.read(p.ac(lambda e, sten=sten, colA=colA, L=L:
                                                 e.activation(out=m1t[:], in_=sten[:, 0:512],
                                                              func=AF.Exp, bias=negLB[L][:, 0:1],
                                                              scale=lam * _scl,
                                                              accum_out=macc256[:, colA:colA + 1]),
                                                 deps=[t_la] + INIT, tag=f"eA{pr}_{qsub}_{kc}"))
                            t_eb2 = rs1.read(p.ac(lambda e, sten=sten, colB=colB, L=L:
                                                  e.activation(out=m1t[:], in_=sten[:, 512:1024],
                                                               func=AF.Exp, bias=negLB[L][:, 0:1],
                                                               scale=lam * _scl,
                                                               accum_out=macc256[:, colB:colB + 1]),
                                                  deps=[t_lb], tag=f"eB{pr}_{qsub}_{kc}"))
                        else:
                            t_ea = rs0.read(p.ve(lambda e, sten=sten, colA=colA:
                                                 e.reduce_max(out=macc256[:, colA:colA + 1],
                                                              in_=sten[:, 0:512],
                                                              axis=mybir.AxisListType.X),
                                                 deps=[t_la], tag=f"mA{pr}_{qsub}_{kc}"))
                            t_eb2 = rs1.read(p.ve(lambda e, sten=sten, colB=colB:
                                                  e.reduce_max(out=macc256[:, colB:colB + 1],
                                                               in_=sten[:, 512:1024],
                                                               axis=mybir.AxisListType.X),
                                                  deps=[t_lb], tag=f"mB{pr}_{qsub}_{kc}"))
                        exp_toks += [t_ea, t_eb2]
                        pair_exp_toks += [t_ea, t_eb2]
                g0 = 8 * pr
                mvp = macc256.ap().rearrange("p (g k) -> p k g", k=4)[:, :, g0:g0 + 8]
                mcp = mcols[:, g0:g0 + 8]
                cop = ALU.add if pr % 2 == 0 else ALU.max
                t_r1 = p.ve(lambda e, mvp=mvp, mcp=mcp, op=cop: e.tensor_tensor(
                    out=mcp, in0=mvp[:, 0, :], in1=mvp[:, 1, :], op=op),
                    deps=pair_exp_toks)
                t_r2 = p.ve(lambda e, mvp=mvp, mcp=mcp, op=cop: e.tensor_tensor(
                    out=mcp, in0=mcp, in1=mvp[:, 2, :], op=op), deps=[t_r1])
                combine_op = ALU.add if pr % 2 == 0 else ALU.max
                t_r3 = p.ve(lambda e, mvp=mvp, mcp=mcp, op=combine_op: e.tensor_tensor(
                    out=mcp, in0=mcp, in1=mvp[:, 3, :], op=op), deps=[t_r2])
                if pr % 2 == 0:
                    t_lnm = p.ac(lambda e, mcp=mcp: e.activation(
                        out=mcp, in_=mcp, func=AF.Ln, bias=tiny_col[:, 0:1]), deps=[t_r3])
                    t_aff = p.ve(lambda e, mcp=mcp, L=L: e.tensor_scalar(
                        out=mcp, in0=mcp, scalar1=-1.0 / (lam * _scl),
                        scalar2=-Bc / _scl, op0=ALU.mult, op1=ALU.add), deps=[t_lnm])
                else:
                    t_aff = p.ve(lambda e, mcp=mcp: e.tensor_scalar_mul(
                        out=mcp, in0=mcp, scalar1=-1.0), deps=[t_r3])
                for hh in (2 * pr, 2 * pr + 1):
                    for qsub in range(4):
                        qr_dmas.append(p.dma("sq", lambda e, hh=hh, qsub=qsub: e.dma_start(
                            out=qrow_scr[hh:hh + 1, qsub * 128:(qsub + 1) * 128],
                            in_=mcols[:, hh * 4 + qsub:hh * 4 + qsub + 1]),
                            deps=[t_aff] + (war_qrs if (hh == 0 and qsub == 0) else []),
                            tag=f"qr{hh}_{qsub}"))
            t_qrow = None
            if L == 1:
                bqa = buf("qaug")
                t_qrow = p.dma("gq", lambda e: e.dma_start(
                    out=qaug[64:65, :],
                    in_=qrow_scr.rearrange("h q -> (h q)").unsqueeze(0)),
                    deps=qr_dmas, tag="qrowl")
                bqa.read(t_qrow)
                bqrs.read(t_qrow)
            bqa = buf("qaug")
            den_dmas = []
            ev_toks = []
            for pr in range(NPAIR):
                kb = pr % 2
                bka, bkb_ = buf("kha", 0), buf("kha", 1)
                t_ka = p.dma("wq", lambda e, pr=pr, L=L: e.dma_start(
                    out=khaug[0][0:64, :],
                    in_=KS[L][(2 * pr) * 64:(2 * pr) * 64 + 64, :]),
                    deps=bka.wdeps(kv_done), tag=f"kal{pr}")
                bka.wrote(t_ka)
                buf("kscr", L).read(t_ka)
                t_kb2 = p.dma("wq", lambda e, pr=pr, L=L: e.dma_start(
                    out=khaug[1][0:64, :],
                    in_=KS[L][(2 * pr + 1) * 64:(2 * pr + 1) * 64 + 64, :]),
                    deps=bkb_.wdeps(kv_done), tag=f"kbl{pr}")
                bkb_.wrote(t_kb2)
                buf("kscr", L).read(t_kb2)
                bvh = buf("vh", kb)
                t_vh = p.dma("wq", lambda e, pr=pr, kb=kb, L=L: e.dma_start(
                    out=vhb[kb][:],
                    in_=VPS[L][:, pr * NKT * 132:(pr + 1) * NKT * 132]),
                    deps=bvh.wdeps(kv_done), tag=f"vhl{pr}")
                bvh.wrote(t_vh)
                buf("vpscr", L).read(t_vh)
                on = "OA" if pr % 2 == 0 else "OB"
                oten = PS[on]
                ro0, ro1 = reg(on, 0), reg(on, 1)
                war_o = ro0.wdeps() + ro1.wdeps()
                pv_last = None
                pend = None

                def emit_pv(kt, eb, t_e):
                    nonlocal pv_last
                    t_pa = p.te(lambda e, kb=kb, kt=kt, eb=eb, oten=oten: e.matmul(
                        oten[0:65, 0:512], vhb[kb][:, kt * 132:kt * 132 + 65],
                        eabb[eb][:, 0:512], start=(kt == 0), stop=(kt == NKT - 1)),
                        deps=[t_e, t_vh] + (war_o if kt == 0 else []), tag=f"pA{pr}_{kt}")
                    t_pb = p.te(lambda e, kb=kb, kt=kt, eb=eb, oten=oten: e.matmul(
                        oten[0:65, 512:1024], vhb[kb][:, kt * 132 + 66:kt * 132 + 131],
                        eabb[eb][:, 512:1024], start=(kt == 0), stop=(kt == NKT - 1)),
                        deps=[], tag=f"pB{pr}_{kt}")
                    buf("eab", eb).read(t_pa); buf("eab", eb).read(t_pb)
                    bvh.read(t_pa); bvh.read(t_pb)
                    pv_last = t_pb

                for kt in range(NKT):
                    sn = "A" if kt % 2 == 0 else "B"
                    sten = PS[sn]
                    rs0, rs1 = reg(sn, 0), reg(sn, 1)
                    war_s = rs0.wdeps() + rs1.wdeps()
                    t_sa = p.te(lambda e, kt=kt, pr=pr, sten=sten: e.matmul(
                        sten[:, 0:512], khaug[0][0:65, kt * 128:(kt + 1) * 128],
                        qaug[0:65, (2 * pr) * 512:(2 * pr + 1) * 512],
                        start=True, stop=True),
                        deps=[t_ka] + ([t_qrow] if t_qrow else []) + q_toks + war_s,
                        tag=f"sA{pr}_{kt}")
                    t_sb = p.te(lambda e, kt=kt, pr=pr, sten=sten: e.matmul(
                        sten[:, 512:1024], khaug[1][0:65, kt * 128:(kt + 1) * 128],
                        qaug[0:65, (2 * pr + 1) * 512:(2 * pr + 2) * 512],
                        start=True, stop=True),
                        deps=[t_kb2], tag=f"sB{pr}_{kt}")
                    rs0.wrote(t_sa); rs1.wrote(t_sb)
                    bka.read(t_sa); bkb_.read(t_sb)
                    bqa.read(t_sa); bqa.read(t_sb)
                    eb = kt % 2
                    beab = buf("eab", eb)
                    t_e = p.ac(lambda e, eb=eb, sten=sten: e.activation(
                        out=eabb[eb][:], in_=sten[:], func=AF.Exp,
                        bias=zero_col[:, 0:1], scale=scale),
                        deps=beab.wdeps([t_sa, t_sb]) + INIT, tag=f"exp{pr}_{kt}")
                    beab.wrote(t_e)
                    rs0.read(t_e); rs1.read(t_e)
                    if pend is not None:
                        emit_pv(*pend)
                    pend = (kt, eb, t_e)
                emit_pv(*pend)
                ro0.wrote(pv_last); ro1.wrote(pv_last)
                bdp = buf("denrowA"); bdp2 = buf("denrowB")
                t_dc = p.ve(lambda e, oten=oten: e.tensor_copy(
                    out=rstdrow[:], in_=oten[64:65, 0:512]),
                    deps=bdp.wdeps([pv_last]))
                t_dc2 = p.ve(lambda e, oten=oten: e.tensor_copy(
                    out=mrrow[:], in_=oten[64:65, 512:1024]),
                    deps=bdp2.wdeps([pv_last]))
                bdp.wrote(t_dc); bdp2.wrote(t_dc2)
                ro0.read(t_dc); ro1.read(t_dc)
                ro0.read(t_dc2); ro1.read(t_dc2)
                bdscr = buf("den_scr")
                t_dd = p.dma("sq", lambda e, pr=pr: e.dma_start(
                    out=den_scr[2 * pr:2 * pr + 1, :], in_=rstdrow[:]),
                    deps=[t_dc] + (bdscr.wdeps() if pr == 0 else []),
                    tag=f"dendA{pr}")
                t_dd2 = p.dma("sq", lambda e, pr=pr: e.dma_start(
                    out=den_scr[2 * pr + 1:2 * pr + 2, :], in_=mrrow[:]),
                    deps=[t_dc2], tag=f"dendB{pr}")
                bdp.read(t_dd); bdp2.read(t_dd2)
                den_dmas.append(t_dd); den_dmas.append(t_dd2)
                bsc = buf("sc")
                t_e1 = p.ve(lambda e, pr=pr, oten=oten: e.tensor_copy(
                    out=sc[0:64, pr * 512:(pr + 1) * 512], in_=oten[0:64, 0:512]),
                    deps=[pv_last] + (bsc.wdeps() if pr == 0 else []))
                t_e2 = p.ve(lambda e, pr=pr, oten=oten: e.tensor_copy(
                    out=sc[64:128, pr * 512:(pr + 1) * 512], in_=oten[0:64, 512:1024]),
                    deps=[pv_last])
                ro0.read(t_e1); ro0.read(t_e2)
                ro1.read(t_e1); ro1.read(t_e2)
                bsc.wrote(t_e2)
                ev_toks += [t_e1, t_e2]
            bd16 = buf("den16")
            t_d16 = p.dma("sq", lambda e: e.dma_start(out=den16[:], in_=den_scr),
                          deps=bd16.wdeps(den_dmas), tag="den16")
            buf("den_scr").read(t_d16)
            bd16.wrote(t_d16)
            tap_d = []
            if f"den{L}c{c}" in tap_aps:
                t_tapd = p.dma("gq", lambda e, L=L, c=c: e.dma_start(
                    out=tap_aps[f"den{L}c{c}"], in_=den_scr), deps=[t_d16], tag="tapden")
                tap_d = [t_tapd]
            t_ln = bd16.read(p.ac(lambda e: e.activation(
                out=den16[:], in_=den16[:], func=AF.Ln), deps=[t_d16] + tap_d))
            t_rd = bd16.read(p.ac(lambda e: e.activation(
                out=den16[:], in_=den16[:], func=AF.Exp, scale=-1.0), deps=[t_ln]))
            norm_toks = []
            for ft in range(8):
                hlf = ft % 2
                t_dv = mm_group("A", hlf, [
                    (lambda ft=ft: indb[:, ft * 128:(ft + 1) * 128],
                     lambda: den16[:], None)], deps=[t_rd], tag=f"divmm{ft}")
                if L == 0:
                    t_n = reg("A", hlf).read(p.ve(lambda e, ft=ft, hlf=hlf: e.tensor_tensor(
                        out=sc[:, ft * 512:(ft + 1) * 512],
                        in0=sc[:, ft * 512:(ft + 1) * 512],
                        in1=psap("A", hlf), op=ALU.mult),
                        deps=[t_dv, ev_toks[2 * ft], ev_toks[2 * ft + 1]]))
                else:
                    bhc = buf("hc")
                    t_n = reg("A", hlf).read(bhc.read(p.ve(
                        lambda e, ft=ft, hlf=hlf: e.tensor_tensor(
                            out=hc[:, ft * 512:(ft + 1) * 512],
                            in0=sc[:, ft * 512:(ft + 1) * 512],
                            in1=psap("A", hlf), op=ALU.mult),
                        deps=([t_dv, ev_toks[2 * ft], ev_toks[2 * ft + 1]] +
                              (bhc.wdeps() if ft == 0 else [])))))
                norm_toks.append(t_n)
            if L == 1:
                buf("hc").wrote(norm_toks[-1])
                proj_toks = []
                for fc in range(2):
                    bi = fc % 2
                    t_w = load_wT_fchunk(woutT, fc * 512, "w1c", bi, tag=f"wout{fc}")
                    for fi in range(4):
                        ft = fc * 4 + fi
                        hlf = ft % 2
                        t_mm = mm_group("B", hlf, [
                            (lambda dt=dt, fi=fi, bi=bi:
                             w1cb[bi][:, dt * 512 + fi * 128:dt * 512 + fi * 128 + 128],
                             lambda dt=dt: hc[:, dt * 512:(dt + 1) * 512], None)
                            for dt in range(8)],
                            deps=[t_w] + norm_toks, tag=f"pjmm{ft}")
                        buf("w1c", bi).read(t_mm)
                        buf("hc").read(t_mm)
                        t_pe = reg("B", hlf).read(p.ve(lambda e, ft=ft, hlf=hlf: e.tensor_scalar(
                            out=sc[:, ft * 512:(ft + 1) * 512], in0=psap("B", hlf),
                            scalar1=boutc[:, ft:ft + 1], scalar2=None, op0=ALU.add),
                            deps=[t_mm, norm_toks[ft]]))
                        proj_toks.append(t_pe)
                norm_toks = proj_toks
            if f"sc{L}c{c}" in tap_aps:
                t_tap = p.dma("gq", lambda e, L=L, c=c: e.dma_start(
                    out=tap_aps[f"sc{L}c{c}"], in_=sc[:]), deps=list(norm_toks),
                    tag="tapsc")
                buf("sc").read(t_tap)
                norm_toks = norm_toks + [t_tap]
            return norm_toks

        def phase_ffn(L, c, s_toks, xn_tok, xsrc, xdst, wprep_toks, x_extra=()):
            par = c % 2
            bxn = buf("xnb", par)
            bx = buf("xcb")
            t_x = load_x_chunk(xsrc, c, f"L{L}c{c}:x2")
            for d in x_extra:
                t_x.deps.append(d); d.need = True
            bx.wrote(t_x)
            lr_tok = None
            if L == 1:
                t_z1 = mm_group("OA", 0, [
                    (lambda dt=dt: vTb[:, dt * RANK:(dt + 1) * RANK],
                     lambda dt=dt: xnb[par][:, dt * CH:(dt + 1) * CH], None)
                    for dt in range(8)],
                    deps=[xn_tok] + INIT, tag="z1mm", p0=0, p1=RANK)
                bxn.read(t_z1)
                bz1 = buf("z1b")
                t_z1e = reg("OA", 0).read(p.ve(lambda e: e.tensor_copy(
                    out=z1b[:], in_=poA[0:RANK, 0:512]), deps=bz1.wdeps([t_z1])))
                bz1.wrote(t_z1e)
                t_z2 = mm_group("OB", 0, [(lambda: fbTb[:], lambda: z1b[:], None)],
                                deps=[t_z1e] + wprep_toks, tag="z2mm", p0=0, p1=RANK)
                bz1.read(t_z2)
                bz2 = buf("z2g")
                t_z2e = reg("OB", 0).read(p.ve(lambda e: e.tensor_scalar_mul(
                    out=z2g[:], in0=poB[0:RANK, 0:512], scalar1=gcol[:, 0:1]),
                    deps=bz2.wdeps([t_z2])))
                bz2.wrote(t_z2e)
                lr_tok = t_z2e
            h_toks = []
            bhc = buf("hc")
            war_h = bhc.wdeps()
            for ic in range(8):
                bi = ic % 2
                bw = buf("w1c", bi)
                t_w = p.dma("wq", lambda e, ic=ic, bi=bi, L=L: e.dma_start(
                    out=w1cb[bi][:], in_=w1s[L][:, ic * 4096:(ic + 1) * 4096]),
                    deps=bw.wdeps(wprep_toks), tag=f"w1l{ic}")
                bw.wrote(t_w)
                t_ul = None
                if L == 1:
                    but = buf("uTsb", ic % 2)
                    t_ul = p.dma("wq", lambda e, ic=ic: e.dma_start(
                        out=uTsb[ic % 2][:], in_=uts_scr[:, ic * 512:(ic + 1) * 512]),
                        deps=but.wdeps(wprep_toks), tag=f"utl{ic}")
                    but.wrote(t_ul)
                for it in range(4):
                    ii = ic * 4 + it
                    hlf = it % 2
                    mms = [(lambda dt=dt, it=it, bi=bi:
                            w1cb[bi][:, dt * 512 + it * 128:dt * 512 + it * 128 + 128],
                            lambda dt=dt: xnb[par][:, dt * CH:(dt + 1) * CH], None)
                           for dt in range(8)]
                    if L == 1:
                        mms.append((lambda it=it, bu=ic % 2: uTsb[bu][:, it * 128:(it + 1) * 128],
                                    lambda: z2g[:], None))
                    t_mm = mm_group("A", hlf, mms,
                                    deps=[t_w, xn_tok] + ([lr_tok, t_ul] if lr_tok else []),
                                    tag=f"hmm{ii}")
                    if L == 1:
                        buf("uTsb", ic % 2).read(t_mm)
                    bw.read(t_mm)
                    bxn.read(t_mm)
                    if L == 1:
                        buf("z2g").read(t_mm)
                    if L == 0:
                        t_h = reg("A", hlf).read(p.ve(lambda e, ii=ii, hlf=hlf: e.tensor_scalar(
                            out=hc[:, ii * 512:(ii + 1) * 512], in0=psap("A", hlf),
                            scalar1=b1c[:, ii:ii + 1], scalar2=0.0,
                            op0=ALU.add, op1=ALU.max),
                            deps=[t_mm] + INIT + (war_h if ii == 0 else [])))
                    else:
                        bzt = buf("zt")
                        t_t = reg("A", hlf).read(p.ve(lambda e, hlf=hlf: e.tensor_scalar_max(
                            out=zt[:], in0=psap("A", hlf), scalar1=0.0),
                            deps=bzt.wdeps([t_mm]) + (war_h if ii == 0 else [])))
                        bzt.wrote(t_t)
                        t_h = bzt.read(p.ve(lambda e, ii=ii: e.scalar_tensor_tensor(
                            out=hc[:, ii * 512:(ii + 1) * 512], in0=zt[:],
                            scalar=sclf_sq[:, ii:ii + 1], in1=zt[:],
                            op0=ALU.mult, op1=ALU.mult), deps=[t_t] + wprep_toks))
                    h_toks.append(t_h)
            bhc.wrote(h_toks[-1])
            comb_toks = []
            for ft in range(8):
                bi = ft % 2
                bw = buf("w2c", bi)
                t_w = p.dma("wq", lambda e, ft=ft, bi=bi, L=L: e.dma_start(
                    out=w2cb[bi][:], in_=w2s[L][:, ft * 4096:(ft + 1) * 4096]),
                    deps=bw.wdeps(wprep_toks), tag=f"w2l{ft}")
                bw.wrote(t_w)
                hlf = ft % 2
                t_mm = mm_group("B", hlf, [
                    (lambda it=it, bi=bi: w2cb[bi][:, it * 128:(it + 1) * 128],
                     lambda it=it: hc[:, it * 512:(it + 1) * 512], None)
                    for it in range(32)],
                    deps=[t_w] + h_toks, tag=f"omm{ft}")
                bw.read(t_mm)
                bhc.read(t_mm)
                bz = buf("ztf")
                if L == 0:
                    t_z = reg("B", hlf).read(p.ve(lambda e, ft=ft, hlf=hlf: e.tensor_scalar_add(
                        out=zt[:], in0=psap("B", hlf), scalar1=b2c[:, ft:ft + 1]),
                        deps=bz.wdeps([t_mm]) + INIT))
                else:
                    t_z = reg("B", hlf).read(p.ve(lambda e, ft=ft, hlf=hlf: e.tensor_scalar_mul(
                        out=zt[:], in0=psap("B", hlf), scalar1=sclp[:, ft:ft + 1]),
                        deps=bz.wdeps([t_mm]) + wprep_toks))
                bz.wrote(t_z)
                bu = buf("ut")
                t_u = bz.read(p.ac(lambda e: e.activation(
                    out=ut[:], in_=zt[:], func=AF.Exp, scale=-1.0),
                    deps=bu.wdeps([t_z])))
                bu.wrote(t_u)
                bt_ = buf("ttt")
                t_t = bu.read(p.ac(lambda e: e.activation(
                    out=ttt[:], in_=ut[:], func=AF.Ln, bias=one_col[:, 0:1]),
                    deps=bt_.wdeps([t_u]) + INIT))
                bt_.wrote(t_t)
                t_w2 = bt_.read(p.ac(lambda e: e.activation(
                    out=ttt[:], in_=ttt[:], func=AF.Exp, scale=-1.0), deps=[t_t]))
                bt_.wrote(t_w2)
                bu2 = buf("ut")
                t_g = bz.read(bt_.read(p.ve(lambda e: e.tensor_mul(
                    out=ut[:], in0=zt[:], in1=ttt[:]), deps=bu2.wdeps([t_w2, t_z]))))
                bu2.wrote(t_g)
                t_sg = buf("sc").read(bu2.read(p.po(lambda e, ft=ft: e.tensor_mul(
                    out=zt[:], in0=sc[:, ft * 512:(ft + 1) * 512], in1=ut[:]),
                    deps=bz.wdeps([t_g] + s_toks))))
                bz.wrote(t_sg)
                t_cx = bz.read(bx.read(p.ve(lambda e, ft=ft: e.tensor_add(
                    out=xcb[:, ft * 512:(ft + 1) * 512],
                    in0=xcb[:, ft * 512:(ft + 1) * 512], in1=zt[:]),
                    deps=[t_sg, t_x])))
                comb_toks.append(t_cx)
            t_xst = p.dma("gq", lambda e, c=c: e.dma_start(
                out=xdst[:, c * 4096:(c + 1) * 4096], in_=xcb[:]),
                deps=comb_toks, tag=f"L{L}c{c}:xst")
            bx.read(t_xst)
            return t_xst, comb_toks

        def make_l1_prep():
            st = {}

            def job_scales():
                toks = []
                red_toks = []
                btm0 = buf("sc")
                for jj in range(8):
                    t_l = p.dma("sq", lambda e, jj=jj: e.dma_start(
                        out=sc[:], in_=wfc[jj * 512:(jj + 1) * 512, :].rearrange(
                            "(r p) d -> p r d", p=128)),
                        deps=btm0.wdeps(), tag=f"fcn{jj}")
                    btm0.wrote(t_l)
                    for r in range(4):
                        t_r = btm0.read(p.ve(lambda e, jj=jj, r=r: e.tensor_reduce(
                            out=rcpf[:, jj * 4 + r:jj * 4 + r + 1],
                            in_=sc[:, r * 1024:(r + 1) * 1024],
                            axis=mybir.AxisListType.X,
                            op=ALU.add, apply_absolute_value=True), deps=[t_l]))
                        red_toks.append(t_r)
                t_s = p.ve(lambda e: e.tensor_scalar(
                    out=sclf_sq[:], in0=rcpf[:], scalar1=1.0 / D, scalar2=1e-5,
                    op0=ALU.mult, op1=ALU.max), deps=red_toks)
                t_rc = p.ve(lambda e: e.reciprocal(out=rcpf[:], in_=sclf_sq[:]), deps=[t_s])
                t_sq = p.ve(lambda e: e.tensor_mul(out=sclf_sq[:], in0=sclf_sq[:],
                                                   in1=sclf_sq[:]), deps=[t_rc])
                toks.append(t_sq)
                rowd = []
                for j in range(32):
                    rowd.append(p.dma("sq", lambda e, j=j: e.dma_start(
                        out=s1r_scr[0:1, j * 128:(j + 1) * 128],
                        in_=rcpf[:, j:j + 1]), deps=[t_rc], tag=f"s1r{j}"))
                btm = buf("sc")
                red2 = []
                for j in range(8):
                    t_l = p.dma("sq", lambda e, j=j: e.dma_start(
                        out=sc[:], in_=wproj[j * 128:(j + 1) * 128, :]),
                        deps=btm.wdeps(), tag=f"pjn{j}")
                    btm.wrote(t_l)
                    t_r1 = btm.read(p.ve(lambda e, j=j: e.tensor_reduce(
                        out=sclp[:, j:j + 1], in_=sc[:],
                        axis=mybir.AxisListType.X, op=ALU.add,
                        apply_absolute_value=True), deps=[t_l]))
                    t_c2 = p.ve(lambda e, j=j: e.tensor_scalar(
                        out=sclp[:, j:j + 1], in0=sclp[:, j:j + 1], scalar1=1.0 / INTER,
                        scalar2=1e-5, op0=ALU.mult, op1=ALU.max), deps=[t_r1])
                    red2.append(t_c2)
                t_rc2 = p.ve(lambda e: e.reciprocal(out=rcpp[:], in_=sclp[:]), deps=red2)
                toks.append(t_rc2)
                rowd2 = []
                for j in range(8):
                    rowd2.append(p.dma("sq", lambda e, j=j: e.dma_start(
                        out=s2r_scr[0:1, j * 128:(j + 1) * 128],
                        in_=rcpp[:, j:j + 1]), deps=[t_rc2], tag=f"s2r{j}"))
                rg = reg("OB", 0)
                war = rg.wdeps()
                t_g = p.te(lambda e: e.matmul(poB[0:RANK, 0:1], ones_row_f[0:1, 0:RANK],
                                              graw[:], start=True, stop=True),
                           deps=INIT + war, tag="gatebc")
                rg.wrote(t_g)
                t_ge = rg.read(p.ve(lambda e: e.tensor_copy(
                    out=gcol[:], in_=poB[0:RANK, 0:1]), deps=[t_g]))
                toks.append(t_ge)
                st["rowd"] = rowd
                st["rowd2"] = rowd2
                return toks

            def job_tern(si):
                toks = []
                bs1 = buf("s1row")
                bs2 = buf("s2row")
                def run_jobs(jobs):
                    for ld, prc in jobs:
                        t_l = ld()
                        toks.append(prc(t_l))

                for icc in (2 * si, 2 * si + 1):
                    tern_jobs = []
                    t_row1 = p.dma("sq", lambda e, icc=icc: e.dma_start(
                        out=s1row[:], in_=s1r_scr[0:1, icc * 512:(icc + 1) * 512]),
                        deps=bs1.wdeps(st["rowd"]), tag=f"s1row{icc}")
                    bs1.wrote(t_row1)
                    t_bc = mm_group("OB", 1, [
                        (lambda: ones_row_f[:], lambda: s1row[:], None)],
                        deps=[t_row1], tag=f"bc1_{icc}")
                    bs1.read(t_bc)
                    brb = buf("rb512")
                    t_be = reg("OB", 1).read(p.ve(lambda e: e.tensor_copy(
                        out=rb512[:], in_=poB[:, 512:1024]), deps=brb.wdeps([t_bc])))
                    brb.wrote(t_be)
                    but = buf("uTf")
                    t_ul = p.dma("gq", lambda e, icc=icc: e.dma_start(
                        out=uTf[:], in_=uT[:, icc * 512:(icc + 1) * 512]),
                        deps=but.wdeps(), tag=f"uTl{icc}")
                    but.wrote(t_ul)
                    bub = buf("uTsb", icc % 2)
                    t_u = but.read(brb.read(p.ve(lambda e, icc=icc: e.tensor_tensor(
                        out=uTsb[icc % 2][:], in0=uTf[:],
                        in1=rb512[0:RANK, :], op=ALU.mult),
                        deps=bub.wdeps([t_be, t_ul]))))
                    bub.wrote(t_u)
                    t_us = p.dma("sq", lambda e, icc=icc: e.dma_start(
                        out=uts_scr[:, icc * 512:(icc + 1) * 512], in_=uTsb[icc % 2][:]),
                        deps=[t_u], tag=f"utst{icc}")
                    bub.read(t_us)
                    toks.append(t_us)
                    for dt in range(8):
                        def mk_load(icc=icc, dt=dt):
                            bpw = buf("pw", dt % 2)
                            t_l = p.dma("sq", lambda e: e.dma_start(
                                out=pwb[dt % 2][:],
                                in_=wfcT[dt * 128:(dt + 1) * 128,
                                         icc * 512:(icc + 1) * 512]),
                                deps=bpw.wdeps(), tag=f"fct{icc}_{dt}")
                            bpw.wrote(t_l)
                            return t_l

                        def mk_proc(t_l, icc=icc, dt=dt, brb=brb, t_be=t_be):
                            bpw = buf("pw", dt % 2)
                            bpt = buf("stg", dt % 2)
                            t_m = brb.read(p.ve(lambda e: e.tensor_tensor(
                                out=pwb[dt % 2][:], in0=pwb[dt % 2][:],
                                in1=rb512[:], op=ALU.mult), deps=[t_l, t_be]))
                            t_rn = p.ve(lambda e: e.tensor_scalar(
                                out=pwb[dt % 2][:], in0=pwb[dt % 2][:], scalar1=MAGIC,
                                scalar2=MAGIC, op0=ALU.add, op1=ALU.subtract), deps=[t_m])
                            t_cl = bpw.read(p.ve(lambda e: e.tensor_scalar(
                                out=stgb[dt % 2][:], in0=pwb[dt % 2][:], scalar1=1.0,
                                scalar2=-1.0, op0=ALU.min, op1=ALU.max),
                                deps=bpt.wdeps([t_rn])))
                            bpt.wrote(t_cl)
                            t_st2 = p.dma("sq", lambda e: e.dma_start(
                                out=w1s[1][:, icc * 4096 + dt * 512:
                                           icc * 4096 + (dt + 1) * 512],
                                in_=stgb[dt % 2][:]), deps=[t_cl],
                                tag=f"fcts{icc}_{dt}")
                            bpt.read(t_st2)
                            return t_st2
                        tern_jobs.append((mk_load, mk_proc))
                    run_jobs(tern_jobs)
                tern_jobs = []
                t_row2 = p.dma("sq", lambda e: e.dma_start(
                    out=s2row[:], in_=s2r_scr[0:1, 0:512]),
                    deps=bs2.wdeps(st["rowd2"]), tag=f"s2rowa{si}")
                bs2.wrote(t_row2)
                t_bc2 = mm_group("OB", 1, [
                    (lambda: ones_row_f[:], lambda: s2row[:], None)],
                    deps=[t_row2], tag=f"bc2a{si}")
                bs2.read(t_bc2)
                brb = buf("rb512")
                t_be2 = reg("OB", 1).read(p.ve(lambda e: e.tensor_copy(
                    out=rb512[:], in_=poB[:, 512:1024]), deps=brb.wdeps([t_bc2])))
                brb.wrote(t_be2)
                t_row3 = p.dma("sq", lambda e: e.dma_start(
                    out=s2row[:], in_=s2r_scr[0:1, 512:1024]), deps=bs2.wdeps(),
                    tag=f"s2rowb{si}")
                bs2.wrote(t_row3)
                t_bc3 = mm_group("OB", 0, [
                    (lambda: ones_row_f[:], lambda: s2row[:], None)],
                    deps=[t_row3], tag=f"bc2b{si}")
                bs2.read(t_bc3)
                brc = buf("rb512b")
                t_be3 = reg("OB", 0).read(p.ve(lambda e: e.tensor_copy(
                    out=rb5122[:], in_=poB[:, 0:512]), deps=brc.wdeps([t_bc3])))
                brc.wrote(t_be3)
                ji = 0
                for jt in range(8 * si, 8 * si + 8):
                    for hhalf in (0, 1):
                        brx = brb if hhalf == 0 else brc
                        rbsel = rb512 if hhalf == 0 else rb5122
                        t_bex = t_be2 if hhalf == 0 else t_be3
                        bi_ = ji % 2
                        ji += 1

                        def mk_load(jt=jt, hhalf=hhalf, bi_=bi_):
                            bpw = buf("pw", bi_)
                            t_l = p.dma("sq", lambda e: e.dma_start(
                                out=pwb[bi_][:],
                                in_=wprojT[jt * 128:(jt + 1) * 128,
                                           hhalf * 512:(hhalf + 1) * 512]),
                                deps=bpw.wdeps(), tag=f"pjt{jt}_{hhalf}")
                            bpw.wrote(t_l)
                            return t_l

                        def mk_proc(t_l, jt=jt, hhalf=hhalf, brx=brx,
                                    rbsel=rbsel, t_bex=t_bex, bi_=bi_):
                            bpw = buf("pw", bi_)
                            bpt = buf("stg", bi_)
                            t_m1 = brx.read(p.ve(lambda e: e.tensor_tensor(
                                out=pwb[bi_][:], in0=pwb[bi_][:],
                                in1=rbsel[:], op=ALU.mult), deps=[t_l, t_bex]))
                            t_r1 = p.ve(lambda e: e.tensor_scalar(
                                out=pwb[bi_][:], in0=pwb[bi_][:], scalar1=MAGIC,
                                scalar2=MAGIC, op0=ALU.add, op1=ALU.subtract),
                                deps=[t_m1])
                            t_c1 = bpw.read(p.ve(lambda e: e.tensor_scalar(
                                out=stgb[bi_][:], in0=pwb[bi_][:], scalar1=1.0,
                                scalar2=-1.0, op0=ALU.min, op1=ALU.max),
                                deps=bpt.wdeps([t_r1])))
                            bpt.wrote(t_c1)
                            t_st2 = p.dma("sq", lambda e: e.dma_start(
                                out=w2s[1].rearrange(
                                    "p (ft it f) -> p ft it f", ft=8, it=32)[
                                    :, hhalf * 4:(hhalf + 1) * 4, jt, :],
                                in_=stgb[bi_].ap().rearrange(
                                    "p (ff f) -> p ff f", ff=4)),
                                deps=[t_c1], tag=f"pjts{jt}_{hhalf}")
                            bpt.read(t_st2)
                            return t_st2
                        tern_jobs.append((mk_load, mk_proc))
                run_jobs(tern_jobs)
                return toks

            return job_scales, job_tern


        def tap_dump(name, src_ap, deps):
            if name in tap_aps:
                return [p.dma("gq", lambda e: e.dma_start(out=tap_aps[name], in_=src_ap),
                              deps=deps, tag="tap_" + name)]
            return []

        final_stores = []
        all_taps = []
        prep1_toks = []
        prev_xsts = []
        ln1_toks = []
        kv1_stores = []
        for L in range(n_layers):
            xsrc = xin if L == 0 else x_scr
            xdst = (y if n_layers == 1 else x_scr) if L == 0 else y
            wprep = L0W_TOKS if L == 0 else prep1_toks
            if L == 0 or not ln1_toks:
                ln_toks = []
                kv_stores = []
                for c in range(NCHUNK):
                    extra = [prev_xsts[c]] if (L == 1 and prev_xsts) else []
                    t_lnst, _ = phase_ln(L, c, xsrc, extra_deps=extra)
                    ln_toks.append(t_lnst)
                    kv_stores += phase_kv(L, c, ln_toks[c])
            else:
                ln_toks = ln1_toks
                kv_stores = kv1_stores
            tapd = tap_dump(f"xn{L}", xn_scr, ln_toks)
            tapd += tap_dump(f"k{L}", KS[L], kv_stores)
            tapd += tap_dump(f"vp{L}", VPS[L], kv_stores)
            all_taps += tapd
            prep_tern = None
            if L == 0 and n_layers == 2:
                job_scales, prep_tern = make_l1_prep()
                prep1_toks = job_scales()
            xsts = []
            for c in range(NCHUNK):
                par = c % 2
                bxn = buf("xnb", par)
                t_xn = p.dma("wq", lambda e, c=c, par=par: e.dma_start(
                    out=xnb[par][:], in_=xn_scr[:, c * 4096:(c + 1) * 4096]),
                    deps=bxn.wdeps([ln_toks[c]]), tag=f"L{L}c{c}:xn3")
                bxn.wrote(t_xn)
                s_toks = phase_attn(L, c, kv_stores + tapd + VPPAD, t_xn)
                t_xst, comb = phase_ffn(
                    L, c, s_toks, t_xn, xsrc, xdst, list(wprep),
                    x_extra=([prev_xsts[c]] if (L == 1 and prev_xsts) else []))
                xsts.append(t_xst)
                if L == 0 and n_layers == 2:
                    t_st1, t_xn1 = phase_ln(1, c, None, x_res=comb,
                                            st_extra=[t_xn])
                    ln1_toks.append(t_st1)
                    kv1_stores += phase_kv(1, c, t_st1, xn_res=t_xn1)
                if prep_tern is not None:
                    prep1_toks += prep_tern(c)
            if L == 0 and n_layers == 2 and "x0" in tap_aps:
                t_tap = p.dma("gq", lambda e: e.dma_start(
                    out=tap_aps["x0"], in_=x_scr), deps=list(xsts), tag="tapx0")
                all_taps.append(t_tap)
            if L == 0 and n_layers == 2 and "tfcT" in tap_aps:
                all_taps.append(p.dma("gq", lambda e: e.dma_start(
                    out=tap_aps["tfcT"], in_=w1s[1]), deps=list(prep1_toks), tag="taptf"))
            if L == 0 and n_layers == 2 and "tpjT" in tap_aps:
                all_taps.append(p.dma("gq", lambda e: e.dma_start(
                    out=tap_aps["tpjT"], in_=w2s[1]), deps=list(prep1_toks), tag="taptp"))
            prev_xsts = xsts
            final_stores = xsts

        p.po(lambda e: e.nop(), deps=final_stores + all_taps, tag="fence")

        p.finalize()
        p.replay(nc, sems)
        nc.all_engine_barrier()
    return nc


_NC_CACHE = {}


def _get_nc(n_layers=2, taps=()):
    key = (n_layers, tuple(taps))
    if key not in _NC_CACHE:
        _NC_CACHE[key] = build_nc(n_layers, taps)
    return _NC_CACHE[key]


def make_in_maps(inputs):
    perm = _parity_perm()
    wqkv = np.asarray(inputs["L1_attn_w_qkv"], np.float32)
    wq1 = wqkv[0:D][perm]
    wk1 = wqkv[D:2 * D][perm]
    wv1 = wqkv[2 * D:3 * D]
    wqkvT = np.ascontiguousarray(np.concatenate([wq1, wk1, wv1], 0).T)
    T1, T2 = _rope_tables()
    R1T, R2T = _rot_mats()
    wfc = np.ascontiguousarray(np.asarray(inputs["L1_ffn_w_fc"], np.float32))
    wproj = np.ascontiguousarray(np.asarray(inputs["L1_ffn_w_proj"], np.float32))
    common = {
        "wqT0": np.ascontiguousarray(np.asarray(inputs["L0_attn_Wq"], np.float32).T),
        "wkT0": np.ascontiguousarray(np.asarray(inputs["L0_attn_Wk"], np.float32).T),
        "wvT0": np.ascontiguousarray(np.asarray(inputs["L0_attn_Wv"], np.float32).T),
        "w1": np.ascontiguousarray(np.asarray(inputs["L0_ffn_W1"], np.float32)),
        "b1": np.ascontiguousarray(np.asarray(inputs["L0_ffn_b1"], np.float32).reshape(32, 128).T),
        "w2": np.ascontiguousarray(np.asarray(inputs["L0_ffn_W2"], np.float32)),
        "b2": np.ascontiguousarray(np.asarray(inputs["L0_ffn_b2"], np.float32).reshape(8, 128).T),
        "wqkvT": wqkvT,
        "woutT": np.ascontiguousarray(np.asarray(inputs["L1_attn_w_out"], np.float32).T),
        "bout": np.ascontiguousarray(np.asarray(inputs["L1_attn_b_out"], np.float32).reshape(8, 128).T),
        "wfc": wfc,
        "wfcT": np.ascontiguousarray(wfc.T),
        "wproj": wproj,
        "wprojT": np.ascontiguousarray(wproj.T),
        "uT": np.ascontiguousarray(np.asarray(inputs["L1_ffn_u"], np.float32).T),
        "vT": np.ascontiguousarray(np.asarray(inputs["L1_ffn_v"], np.float32).T),
        "fbT": np.ascontiguousarray(np.asarray(inputs["L1_ffn_fast_b"], np.float32).T),
        "gate": np.asarray(inputs["L1_ffn_gate"], np.float32).reshape(1),
        "t1c": T1, "t2c": T2, "r1T": R1T, "r2T": R2T,
        "indm": _indmat(),
    }
    x = np.asarray(inputs["L0_attn_x"], np.float32)
    in_maps = []
    for i in range(8):
        m = dict(common)
        xt = x[i].reshape(NCHUNK, CH, 8, 128)
        m["x"] = np.ascontiguousarray(xt.transpose(3, 0, 2, 1).reshape(128, NCHUNK * 4096))
        in_maps.append(m)
    return in_maps


def kernel(**inputs):
    in_maps = make_in_maps(inputs)
    nc = _get_nc()
    res = run_bass_kernel_spmd(nc, in_maps, core_ids=list(range(8)))
    outs = []
    for r in res.results:
        yt = np.asarray(r["y"]).reshape(128, NCHUNK, 8, CH)
        outs.append(yt.transpose(1, 3, 2, 0).reshape(S, D))
    return np.stack(outs, 0).astype(np.float32)


# version 18
# speedup vs baseline: 1.1010x; 1.0769x over previous
import numpy as np
import ml_dtypes
import concourse.bass as bass
import concourse.mybir as mybir
from concourse.bass_utils import run_bass_kernel_spmd

F32 = mybir.dt.float32
BF16 = mybir.dt.bfloat16
AF = mybir.ActivationFunctionType
ALU = mybir.AluOpType

D = 1024
S = 2048
NCHUNK = 4
CH = S // NCHUNK
NH = 16
HD = 64
NPAIR = 8
NKT = 16
CSHIFT = 6.0
EPS = 1e-5
MAGIC = 12582912.0
INTER = 4096
RANK = 16


class Step:
    __slots__ = ("eng", "sem", "inc", "fn", "deps", "need", "val", "tag")

    def __init__(self, eng, sem, inc, fn, deps, tag):
        self.eng, self.sem, self.inc, self.fn = eng, sem, inc, fn
        self.deps = [d for d in deps if d is not None]
        self.need = False
        self.val = None
        self.tag = tag


class Prog:
    QENG = {"wq": "sync", "sq": "scalar", "gq": "gpsimd"}

    def __init__(self):
        self.steps = []

    def _add(self, eng, sem, inc, fn, deps, tag):
        st = Step(eng, sem, inc, fn, deps, tag)
        for d in st.deps:
            d.need = True
        self.steps.append(st)
        return st

    def dma(self, q, fn, deps=(), tag=""):
        st = self._add(self.QENG[q], q, 16, fn, deps, tag)
        st.need = True
        return st

    def te(self, fn, deps=(), tag=""):
        return self._add("tensor", "te", 1, fn, deps, tag)

    def ve(self, fn, deps=(), tag=""):
        return self._add("vector", "ve", 1, fn, deps, tag)

    def ac(self, fn, deps=(), tag=""):
        return self._add("scalar", "ac", 1, fn, deps, tag)

    def po(self, fn, deps=(), tag=""):
        return self._add("gpsimd", "po", 1, fn, deps, tag)

    def finalize(self):
        cnt = {}
        for st in self.steps:
            if st.need:
                cnt[st.sem] = cnt.get(st.sem, 0) + st.inc
                st.val = cnt[st.sem]

    def replay(self, nc, sems):
        with nc.Block() as blk:
            for engname in ("sync", "vector", "scalar", "tensor", "gpsimd"):
                mysteps = [s for s in self.steps if s.eng == engname]

                def body(eng, mysteps=mysteps):
                    hwm = {}
                    for st in mysteps:
                        for d in st.deps:
                            if d.val is not None and hwm.get(d.sem, 0) < d.val:
                                eng.wait_ge(sems[d.sem], d.val)
                                hwm[d.sem] = d.val
                        inst = st.fn(eng)
                        if st.need:
                            inst.then_inc(sems[st.sem], st.inc)

                getattr(blk, engname)(body)


class Buf:

    def __init__(self):
        self.users = []
        self.writer = None

    def wdeps(self, extra=()):
        d = list(extra) + self.users
        self.users = []
        return d

    def wrote(self, step):
        self.writer = step
        return step

    def read(self, step):
        self.users.append(step)
        return step

    def peek(self):
        return list(self.users)


def _rope_tables():
    inv_freq = 1.0 / 10000.0 ** (np.arange(0, HD, 2, dtype=np.float32) / HD)
    t = np.arange(S, dtype=np.float32)
    fr = np.outer(t, inv_freq)
    cosT = np.cos(fr).T.astype(np.float32)
    sinT = np.sin(fr).T.astype(np.float32)
    T1 = np.concatenate([cosT, sinT, cosT, sinT], 0)
    T2 = np.concatenate([sinT, cosT, sinT, cosT], 0)
    return np.ascontiguousarray(T1), np.ascontiguousarray(T2)


def _rot_mats():
    R1 = np.zeros((128, 128), np.float32)
    R2 = np.zeros((128, 128), np.float32)
    for base in (0, 64):
        for r in range(32):
            R1[base + r, base + r] = 1.0
            R1[base + r, base + r + 32] = -1.0
            R2[base + 32 + r, base + r] = 1.0
            R2[base + 32 + r, base + 32 + r] = 1.0
    return np.ascontiguousarray(R1.T), np.ascontiguousarray(R2.T)


def _parity_perm():
    p = np.zeros(D, np.int64)
    for h in range(NH):
        b = h * HD
        p[b:b + 32] = b + np.arange(0, HD, 2)
        p[b + 32:b + HD] = b + np.arange(1, HD, 2)
    return p


def _indmat():
    ind = np.zeros((NH, D), np.float32)
    for h in range(NH):
        ind[h, h * HD:(h + 1) * HD] = 1.0
    return ind


def build_nc(n_layers=2, taps=()):
    nc = bass.Bass("TRN2", target_bir_lowering=False)

    def ein(name, shape):
        return nc.dram_tensor(name, list(shape), F32, kind="ExternalInput").ap()

    xin = ein("x", [128, NCHUNK * 4096])
    y = nc.dram_tensor("y", [128, NCHUNK * 4096], F32, kind="ExternalOutput").ap()

    wqT0 = ein("wqT0", [D, D]); wkT0 = ein("wkT0", [D, D]); wvT0 = ein("wvT0", [D, D])
    w1 = ein("w1", [D, INTER]); b1 = ein("b1", [128, 32])
    w2 = ein("w2", [INTER, D]); b2 = ein("b2", [128, 8])
    wqkvT = ein("wqkvT", [D, 3 * D])
    woutT = ein("woutT", [D, D]); bout = ein("bout", [128, 8])
    wfc = ein("wfc", [INTER, D]); wfcT = ein("wfcT", [D, INTER])
    wproj = ein("wproj", [D, INTER]); wprojT = ein("wprojT", [INTER, D])
    uT = ein("uT", [RANK, INTER]); vT = ein("vT", [D, RANK])
    fbT = ein("fbT", [RANK, RANK]); gate = ein("gate", [1])
    t1c = ein("t1c", [128, S]); t2c = ein("t2c", [128, S])
    r1T = ein("r1T", [128, 128]); r2T = ein("r2T", [128, 128])
    indmat = ein("indm", [NH, D])

    def scr(name, shape, dtype):
        return nc.dram_tensor(name, list(shape), dtype, kind="Internal").ap()

    x_scr = scr("x_scr", [128, NCHUNK * 4096], F32)
    xn_scr = scr("xn_scr", [128, NCHUNK * 4096], BF16)
    k_scr = scr("k_scr", [D, S], BF16)
    vp_scr = scr("vp_scr", [128, NPAIR * NKT * 132], BF16)
    k_scr1 = scr("k_scr1", [D, S], BF16)
    vp_scr1 = scr("vp_scr1", [128, NPAIR * NKT * 132], BF16)
    den_scr = scr("den_scr", [NH, CH], F32)
    w1s = [scr("w1bf", [128, 8 * 4096], BF16), scr("tfcT", [128, 8 * 4096], BF16)]
    w2s = [scr("w2bf", [128, 8 * 4096], BF16), scr("tpjT", [128, 8 * 4096], BF16)]
    s1r_scr = scr("s1r_scr", [1, INTER], F32)
    qrow_scr = scr("qrow_scr", [NH, CH], F32)
    uts_scr = scr("uts_scr", [RANK, INTER], BF16)
    s2r_scr = scr("s2r_scr", [1, D], F32)

    tap_aps = {}
    for name, shape in taps:
        tap_aps[name] = nc.dram_tensor("tap_" + name, list(shape), F32,
                                       kind="ExternalOutput").ap()

    with nc.cleanup_on_exit():
        A = nc.alloc_sbuf_tensor
        xnb = [A("xnb0", [128, 8 * CH], BF16), A("xnb1", [128, 8 * CH], BF16)]
        xcb = A("xcb", [128, 8 * CH], F32)
        khaug = [A("kha0", [65, S], BF16), A("kha1", [65, S], BF16)]
        vhb = [A("vh0", [128, NKT * 132], BF16), A("vh1", [128, NKT * 132], BF16)]
        qc = A("qc", [128, 8 * CH], BF16)
        qaug = A("qaug", [65, 16 * CH], BF16)
        eabb = [A("eab0", [128, 1024], BF16), A("eab1", [128, 1024], BF16)]
        sc = A("sc", [128, 8 * CH], F32)
        hc = A("hc", [128, 32 * 512], BF16)
        w1cb = [A("w1c0", [128, 4096], BF16), A("w1c1", [128, 4096], BF16)]
        w2cb = [A("w2c0", [128, 4096], BF16), A("w2c1", [128, 4096], BF16)]
        stgb = [A("stg0", [128, 512], BF16), A("stg1", [128, 512], BF16)]
        vstg = A("vstg", [128, 528], BF16)
        m1t = A("m1t", [128, 512], BF16)
        m2t = A("m2t", [128, 512], BF16)
        zt = A("zt", [128, 512], F32)
        ut = A("ut", [128, 512], F32)
        ttt = A("ttt", [128, 512], F32)
        t1b = A("t1b", [128, S], BF16)
        t2b = A("t2b", [128, S], BF16)
        r1b = A("r1b", [128, 128], BF16)
        r2b = A("r2b", [128, 128], BF16)
        indb = A("indb", [NH, D], F32)
        macc256 = A("macc256", [128, 256], F32)
        mcols = A("mcols", [128, 64], F32)
        negLB = [A("negLB0", [128, 1], F32), A("negLB1", [128, 1], F32)]
        zero_col = A("zero_col", [128, 1], F32)
        tiny_col = A("tiny_col", [128, 1], F32)
        ones_col_bf = A("ones_col_bf", [128, 1], BF16)
        ones_row_f = A("ones_row_f", [1, 128], F32)
        one_col = A("one_col", [128, 1], F32)
        negC_col = A("negC_col", [128, 1], F32)
        eps1 = A("eps1", [1, 1], F32)
        b1c = A("b1c", [128, 32], F32)
        b2c = A("b2c", [128, 8], F32)
        boutc = A("boutc", [128, 8], F32)
        mrow = A("mrow", [1, CH], F32)
        msqrow = A("msqrow", [1, CH], F32)
        rstdrow = A("rstdrow", [1, CH], F32)
        mrrow = A("mrrow", [1, CH], F32)
        den16 = A("den16", [NH, CH], F32)
        uTsb = [A("uTs0", [RANK, 512], BF16), A("uTs1", [RANK, 512], BF16)]
        uTf = A("uTf", [RANK, 512], BF16)
        vTb = A("vTb", [128, 8 * RANK], BF16)
        fbTb = A("fbTb", [RANK, RANK], BF16)
        gcol = A("gcol", [RANK, 1], F32)
        graw = A("graw", [1, 1], F32)
        z1b = A("z1b", [RANK, CH], BF16)
        z2g = A("z2g", [RANK, CH], BF16)
        sclf_sq = A("sclf_sq", [128, 32], F32)
        sclp = A("sclp", [128, 8], F32)
        rcpf = A("rcpf", [128, 32], F32)
        rcpp = A("rcpp", [128, 8], F32)
        s1row = mrow
        s2row = msqrow
        rb512 = A("rb512", [128, 512], F32)
        rb5122 = A("rb5122", [128, 512], F32)
        pwb = [A("pw0", [128, 512], F32), A("pw1", [128, 512], F32)]

        psA = nc.alloc_psum_tensor("psA", [128, 1024], F32)
        psB = nc.alloc_psum_tensor("psB", [128, 1024], F32)
        poA = nc.alloc_psum_tensor("poA", [128, 1024], F32)
        poB = nc.alloc_psum_tensor("poB", [128, 1024], F32)
        PS = {"A": psA, "B": psB, "OA": poA, "OB": poB}
        khp = [hc.ap()[:, 8192:10240], hc.ap()[:, 10240:12288]]

        sems = {k: nc.alloc_semaphore(k) for k in
                ("wq", "sq", "gq", "te", "ve", "ac", "po")}

        p = Prog()
        B = {}

        def buf(name, i=0):
            key = (name, i)
            if key not in B:
                B[key] = Buf()
            return B[key]

        def reg(name, half):
            return buf("ps" + name, half)

        KS = [k_scr, k_scr1]
        VPS = [vp_scr, vp_scr1]

        def psap(name, half, p0=0, p1=128):
            return PS[name][p0:p1, half * 512:half * 512 + 512]

        def mm_group(rname, half, mms, deps=(), tag="", p0=0, p1=128):
            r = reg(rname, half)
            war = r.wdeps()
            last = None
            n = len(mms)
            for i, (lf, rf, tp) in enumerate(mms):
                dd = (list(deps) + war) if i == 0 else []
                last = p.te(lambda e, lf=lf, rf=rf, tp=tp, st=(i == 0),
                            sp=(i == n - 1), p0=p0, p1=p1, half=half, rname=rname:
                            e.matmul(psap(rname, half, p0, p1), lf(), rf(),
                                     start=st, stop=sp, tile_position=tp),
                            deps=dd, tag=tag)
            r.wrote(last)
            return last

        INIT = []
        for fn in (lambda e: e.memset(ones_col_bf[:], 1.0),
                   lambda e: e.memset(ones_row_f[:], 1.0),
                   lambda e: e.memset(one_col[:], 1.0),
                   lambda e: e.memset(negC_col[:], -CSHIFT),
                   lambda e: e.memset(eps1[:], EPS)):
            INIT.append(p.ve(fn))
        INIT.append(p.dma("wq", lambda e: e.dma_start(out=indb[:], in_=indmat)))
        LB = [(150.0, 87.0 / 300.0, 1.0 / 32.0), (300.0, 87.0 / 600.0, 1.0 / 8.0)]
        for Li in (0, 1):
            Bc, lam, scl_ = LB[Li]
            INIT.append(p.ve(lambda e, Li=Li, v=-LB[Li][1] * LB[Li][0]:
                             e.memset(negLB[Li][:], v)))
        INIT.append(p.ve(lambda e: e.memset(zero_col[:], 0.0)))
        INIT.append(p.ve(lambda e: e.memset(tiny_col[:], 1e-38)))
        for b_ in range(2):
            INIT.append(p.ve(lambda e, b_=b_: e.memset(khaug[b_][64:65, :], 1.0)))
        INIT.append(p.ve(lambda e: e.memset(qaug[64:65, :], -640.0)))
        for dst, src in ((t1b, t1c), (t2b, t2c)):
            INIT.append(p.dma("gq", lambda e, d=dst, s=src: e.dma_start(out=d[:], in_=s)))
        for dst, src in ((r1b, r1T), (r2b, r2T), (fbTb, fbT)):
            INIT.append(p.dma("gq", lambda e, d=dst, s=src: e.dma_start(out=d[:], in_=s)))
        INIT.append(p.dma("gq", lambda e: e.dma_start(out=b1c[:], in_=b1)))
        INIT.append(p.dma("gq", lambda e: e.dma_start(out=b2c[:], in_=b2)))
        INIT.append(p.dma("gq", lambda e: e.dma_start(out=boutc[:], in_=bout)))
        INIT.append(p.dma("gq", lambda e: e.dma_start(
            out=vTb.ap().rearrange("p (t r) -> p t r", t=8),
            in_=vT.rearrange("(t p) r -> p t r", p=128))))
        INIT.append(p.dma("gq", lambda e: e.dma_start(
            out=graw[:], in_=gate.unsqueeze(0))))
        VPPAD = []
        INIT.append(p.ve(lambda e: e.memset(
            vstg.ap().rearrange("p (h z) -> p h z", h=8)[:, :, 64:66], 1.0)))

        L0W_TOKS = []
        for ic in range(8):
            L0W_TOKS.append(p.dma("gq", lambda e, ic=ic: e.dma_start(
                out=w1s[0][:, ic * 4096:(ic + 1) * 4096].rearrange(
                    "p (dt t) -> p dt t", dt=8),
                in_=w1.rearrange("(dt p) i -> p dt i", p=128)[:, :, ic * 512:(ic + 1) * 512]),
                tag=f"w1cast{ic}"))
        for ft in range(8):
            L0W_TOKS.append(p.dma("gq", lambda e, ft=ft: e.dma_start(
                out=w2s[0][:, ft * 4096:(ft + 1) * 4096].rearrange(
                    "p (it f) -> p it f", it=32),
                in_=w2.rearrange("(it p) f -> p it f", p=128)[:, :, ft * 128:(ft + 1) * 128]),
                tag=f"w2cast{ft}"))

        def load_wT_fchunk(whbm, col0, bufname, bi, deps=(), tag=""):
            bb = buf(bufname, bi)
            dst = (w1cb if bufname == "w1c" else w2cb)[bi]
            t = p.dma("gq", lambda e: e.dma_start(
                out=dst.ap().rearrange("p (dt t) -> p dt t", dt=8),
                in_=whbm[:, col0:col0 + 512].rearrange("(dt p) t -> p dt t", p=128)),
                deps=bb.wdeps(deps), tag=tag)
            bb.wrote(t)
            return t

        def load_x_chunk(xsrc, c, tag):
            return p.dma("gq", lambda e: e.dma_start(
                out=xcb[:], in_=xsrc[:, c * 4096:(c + 1) * 4096]),
                deps=buf("xcb").wdeps(), tag=tag)

        fillq = []
        prep1_list = []

        def pump_fill(n):
            while fillq and n > 0:
                prep1_list.extend(fillq.pop(0)())
                n -= 1

        def phase_ln(L, c, xsrc, extra_deps=(), x_res=None, st_extra=()):
            par = c % 2
            bx = buf("xcb")
            if x_res is None:
                t_x = load_x_chunk(xsrc, c, f"L{L}c{c}:x1")
                for d in extra_deps:
                    t_x.deps.append(d); d.need = True
                bx.wrote(t_x)
                xdeps = [t_x]
            else:
                xdeps = list(x_res)
            bq = buf("qc")
            t_cast = bx.read(p.ve(lambda e: e.tensor_copy(out=qc[:], in_=xcb[:]),
                                  deps=bq.wdeps(xdeps)))
            bq.wrote(t_cast)
            bh = buf("hc")
            t_sq = bq.read(p.ve(lambda e: e.tensor_mul(out=hc[:, 0:4096], in0=qc[:], in1=qc[:]),
                                deps=bh.wdeps([t_cast])))
            bh.wrote(t_sq)
            t_s1 = mm_group("A", 0, [
                (lambda: ones_col_bf[:, 0:1],
                 lambda dt=dt: qc[:, dt * CH:(dt + 1) * CH], None) for dt in range(8)],
                deps=[t_cast] + INIT, tag=f"L{L}c{c}:lnsum", p0=0, p1=1)
            bq.read(t_s1)
            t_s2 = mm_group("A", 1, [
                (lambda: ones_col_bf[:, 0:1],
                 lambda dt=dt: hc[:, dt * 512:(dt + 1) * 512], None) for dt in range(8)],
                deps=[t_sq], tag=f"L{L}c{c}:lnsq", p0=0, p1=1)
            bh.read(t_s2)
            bmr, bms = buf("s1row"), buf("s2row")
            brs, bmrr = buf("denrowA"), buf("denrowB")
            t_m = reg("A", 0).read(p.ve(lambda e: e.tensor_scalar_mul(
                out=mrow[:], in0=psA[0:1, 0:512], scalar1=1.0 / D),
                deps=[t_s1] + bmr.wdeps()))
            bmr.wrote(t_m)
            t_mq = reg("A", 1).read(p.ve(lambda e: e.tensor_scalar_mul(
                out=msqrow[:], in0=psA[0:1, 512:1024], scalar1=1.0 / D),
                deps=[t_s2] + bms.wdeps()))
            bms.wrote(t_mq)
            t_v1 = p.ve(lambda e: e.tensor_mul(out=rstdrow[:], in0=mrow[:], in1=mrow[:]),
                        deps=[t_m] + brs.wdeps())
            brs.wrote(t_v1)
            t_v2 = p.ve(lambda e: e.tensor_sub(out=msqrow[:], in0=msqrow[:], in1=rstdrow[:]),
                        deps=[t_v1, t_mq])
            t_l = p.ac(lambda e: e.activation(out=rstdrow[:], in_=msqrow[:], func=AF.Ln,
                                              bias=eps1[:, 0:1]), deps=[t_v2])
            bms.read(t_l)
            t_r = p.ac(lambda e: e.activation(out=rstdrow[:], in_=rstdrow[:], func=AF.Exp,
                                              scale=-0.5), deps=[t_l])
            t_mr = p.ve(lambda e: e.tensor_mul(out=mrrow[:], in0=mrow[:], in1=rstdrow[:]),
                        deps=[t_r, t_m] + bmrr.wdeps())
            bmrr.wrote(t_mr)
            bmr.read(t_mr)
            t_b1 = mm_group("B", 0, [(lambda: ones_row_f[:], lambda: rstdrow[:], None)],
                            deps=[t_r], tag="lnb1")
            brs.read(t_b1)
            t_b2 = mm_group("B", 1, [(lambda: ones_row_f[:], lambda: mrrow[:], None)],
                            deps=[t_mr], tag="lnb2")
            bmrr.read(t_b2)
            bsc = buf("sc")
            tt1 = []
            for dt in range(8):
                dd = [t_b1] + xdeps[-1:] + (bsc.wdeps() if dt == 0 else [])
                tt1.append(reg("B", 0).read(bx.read(p.ve(lambda e, dt=dt: e.tensor_tensor(
                    out=sc[:, dt * CH:(dt + 1) * CH], in0=xcb[:, dt * CH:(dt + 1) * CH],
                    in1=psB[:, 0:512], op=ALU.mult), deps=dd))))
            bsc.wrote(tt1[-1])
            bxn = buf("xnb", par)
            tt2 = []
            for dt in range(8):
                dd = [t_b2, tt1[dt]] + (bxn.wdeps() if dt == 0 else [])
                tt2.append(bsc.read(reg("B", 1).read(p.ve(lambda e, dt=dt: e.tensor_tensor(
                    out=xnb[par][:, dt * CH:(dt + 1) * CH],
                    in0=sc[:, dt * CH:(dt + 1) * CH],
                    in1=psB[:, 512:1024], op=ALU.subtract), deps=dd))))
            bxn.wrote(tt2[-1])
            t_st = p.dma("wq", lambda e: e.dma_start(
                out=xn_scr[:, c * 4096:(c + 1) * 4096], in_=xnb[par][:]),
                deps=tt2 + list(st_extra), tag=f"L{L}c{c}:xnst")
            bxn.read(t_st)
            return t_st, tt2[-1]

        def phase_kv(L, c, xn_st, xn_res=None):
            par = c % 2
            bxn = buf("xnb", par)
            if xn_res is None:
                t_xn = p.dma("wq", lambda e: e.dma_start(
                    out=xnb[par][:], in_=xn_scr[:, c * 4096:(c + 1) * 4096]),
                    deps=bxn.wdeps([xn_st]), tag=f"L{L}c{c}:xn2")
                bxn.wrote(t_xn)
            else:
                t_xn = xn_res
            wk = wkT0 if L == 0 else wqkvT[:, D:2 * D]
            wv = wvT0 if L == 0 else wqkvT[:, 2 * D:3 * D]
            stores = []
            for fc in range(2):
                bi = fc % 2
                t_w = load_wT_fchunk(wk, fc * 512, "w1c", bi, tag=f"L{L}c{c}:wk{fc}")
                for fi in range(4):
                    ft = fc * 4 + fi
                    hlf = ft % 2
                    t_mm = mm_group("A", hlf, [
                        (lambda dt=dt, fi=fi, bi=bi:
                         w1cb[bi][:, dt * 512 + fi * 128:dt * 512 + fi * 128 + 128],
                         lambda dt=dt: xnb[par][:, dt * CH:(dt + 1) * CH], None)
                        for dt in range(8)],
                        deps=[t_w, t_xn], tag=f"L{L}c{c}:kmm{ft}")
                    buf("w1c", bi).read(t_mm)
                    bxn.read(t_mm)
                    sb = ft % 2
                    bstg = buf("stg", sb)
                    if L == 0:
                        t_ev = reg("A", hlf).read(p.ve(
                            lambda e, sb=sb, hlf=hlf: e.tensor_copy(
                                out=stgb[sb][:], in_=psap("A", hlf)),
                            deps=bstg.wdeps([t_mm])))
                    else:
                        bm1, bm2 = buf("m1"), buf("m2")
                        t_m1 = reg("A", hlf).read(p.ve(lambda e, hlf=hlf, c=c: e.tensor_tensor(
                            out=m1t[:], in0=psap("A", hlf),
                            in1=t1b[:, c * CH:(c + 1) * CH], op=ALU.mult),
                            deps=bm1.wdeps([t_mm]) + INIT))
                        bm1.wrote(t_m1)
                        t_m2 = reg("A", hlf).read(p.ve(lambda e, hlf=hlf, c=c: e.tensor_tensor(
                            out=m2t[:], in0=psap("A", hlf),
                            in1=t2b[:, c * CH:(c + 1) * CH], op=ALU.mult),
                            deps=bm2.wdeps([t_mm])))
                        bm2.wrote(t_m2)
                        t_rot = mm_group("B", hlf, [
                            (lambda: r1b[:], lambda: m1t[:], None),
                            (lambda: r2b[:], lambda: m2t[:], None)],
                            deps=[t_m1, t_m2] + INIT, tag=f"L{L}c{c}:krot{ft}")
                        bm1.read(t_rot); bm2.read(t_rot)
                        t_ev = reg("B", hlf).read(p.ve(
                            lambda e, sb=sb, hlf=hlf: e.tensor_copy(
                                out=stgb[sb][:], in_=psap("B", hlf)),
                            deps=bstg.wdeps([t_rot])))
                    bstg.wrote(t_ev)
                    t_kst = p.dma("sq", lambda e, ft=ft, sb=sb, c=c, L=L: e.dma_start(
                        out=KS[L][ft * 128:(ft + 1) * 128, c * CH:(c + 1) * CH],
                        in_=stgb[sb][:]),
                        deps=[t_ev] + buf("kscr", L).wdeps(), tag=f"L{L}c{c}:kst{ft}")
                    bstg.read(t_kst)
                    stores.append(t_kst)
            for fc in range(2):
                bi = fc % 2
                t_w = load_wT_fchunk(wv, fc * 512, "w2c", bi, tag=f"L{L}c{c}:wv{fc}")
                for tt_ in range(4):
                    hlf = tt_ % 2
                    t_mm = mm_group("A", hlf, [
                        (lambda dt=dt, tt_=tt_:
                         xnb[par][:, dt * CH + tt_ * 128:dt * CH + tt_ * 128 + 128],
                         lambda dt=dt, bi=bi: w2cb[bi][:, dt * 512:(dt + 1) * 512], None)
                        for dt in range(8)],
                        deps=[t_w, t_xn], tag=f"L{L}c{c}:vmm{fc}_{tt_}")
                    buf("w2c", bi).read(t_mm)
                    bxn.read(t_mm)
                    bstg = buf("vstg")
                    t_ev = reg("A", hlf).read(p.ve(
                        lambda e, hlf=hlf: e.tensor_copy(
                            out=vstg.ap().rearrange("p (h z) -> p h z", h=8)[:, :, 0:64],
                            in_=psap("A", hlf)),
                        deps=bstg.wdeps([t_mm]) + INIT))
                    bstg.wrote(t_ev)
                    kti = c * 4 + tt_
                    t_vst = p.dma("sq", lambda e, kti=kti, fc=fc, L=L: e.dma_start(
                        out=VPS[L].rearrange("p (pr kt z) -> p pr kt z", pr=NPAIR, kt=NKT)[
                            :, fc * 4:(fc + 1) * 4, kti, :],
                        in_=vstg.ap().rearrange("p (pp z) -> p pp z", pp=4)),
                        deps=[t_ev] + buf("vpscr", L).wdeps(), tag=f"L{L}c{c}:vst{fc}_{tt_}")
                    bstg.read(t_vst)
                    stores.append(t_vst)
            return stores

        def phase_attn(L, c, kv_done, xn_tok):
            par = c % 2
            scale = (1.0 / 32.0) if L == 0 else (1.0 / 8.0)
            wqq = wqT0 if L == 0 else wqkvT[:, 0:D]
            bxn = buf("xnb", par)
            bqc = buf("qc")
            q_toks = []
            for fc in range(2):
                bi = fc % 2
                t_w = load_wT_fchunk(wqq, fc * 512, "w2c", bi, tag=f"L{L}c{c}:wq{fc}")
                for fi in range(4):
                    ft = fc * 4 + fi
                    hlf = ft % 2
                    t_mm = mm_group("A", hlf, [
                        (lambda dt=dt, fi=fi, bi=bi:
                         w2cb[bi][:, dt * 512 + fi * 128:dt * 512 + fi * 128 + 128],
                         lambda dt=dt: xnb[par][:, dt * CH:(dt + 1) * CH], None)
                        for dt in range(8)],
                        deps=[t_w, xn_tok], tag=f"L{L}c{c}:qmm{ft}")
                    buf("w2c", bi).read(t_mm)
                    bxn.read(t_mm)
                    wq_deps = bqc.wdeps() if (ft == 0 and L == 1) else []
                    bqa = buf("qaug")
                    wqa_deps = bqa.wdeps() if ft == 0 else []
                    if L == 0:
                        if f"q{L}c{c}" in tap_aps:
                            t_ev = reg("A", hlf).read(p.ve(
                                lambda e, ft=ft, hlf=hlf: e.tensor_copy(
                                    out=qc[:, ft * 512:(ft + 1) * 512],
                                    in_=psap("A", hlf)),
                                deps=[t_mm] + bqc.wdeps() if ft == 0 else [t_mm]))
                            bqc.wrote(t_ev)
                        else:
                            t_ev = None
                        t_a1 = reg("A", hlf).read(p.ve(lambda e, ft=ft, hlf=hlf: e.tensor_copy(
                            out=qaug[0:64, (2 * ft) * 512:(2 * ft + 1) * 512],
                            in_=psap("A", hlf, 0, 64)), deps=[t_mm] + wqa_deps))
                        t_a2 = reg("A", hlf).read(p.ve(lambda e, ft=ft, hlf=hlf: e.tensor_copy(
                            out=qaug[0:64, (2 * ft + 1) * 512:(2 * ft + 2) * 512],
                            in_=psap("A", hlf, 64, 128)), deps=[t_mm]))
                        bqa.wrote(t_a2)
                        q_toks.append(t_a1); q_toks.append(t_a2)
                    else:
                        bm1, bm2 = buf("m1"), buf("m2")
                        t_m1 = reg("A", hlf).read(p.ve(lambda e, hlf=hlf, c=c: e.tensor_tensor(
                            out=m1t[:], in0=psap("A", hlf),
                            in1=t1b[:, c * CH:(c + 1) * CH], op=ALU.mult),
                            deps=bm1.wdeps([t_mm]) + INIT))
                        bm1.wrote(t_m1)
                        t_m2 = reg("A", hlf).read(p.ve(lambda e, hlf=hlf, c=c: e.tensor_tensor(
                            out=m2t[:], in0=psap("A", hlf),
                            in1=t2b[:, c * CH:(c + 1) * CH], op=ALU.mult),
                            deps=bm2.wdeps([t_mm])))
                        bm2.wrote(t_m2)
                        t_rot = mm_group("B", hlf, [
                            (lambda: r1b[:], lambda: m1t[:], None),
                            (lambda: r2b[:], lambda: m2t[:], None)],
                            deps=[t_m1, t_m2], tag=f"L{L}c{c}:qrot{ft}")
                        bm1.read(t_rot); bm2.read(t_rot)
                        t_ev = reg("B", hlf).read(p.ve(lambda e, ft=ft, hlf=hlf: e.tensor_copy(
                            out=qc[:, ft * 512:(ft + 1) * 512], in_=psap("B", hlf)),
                            deps=[t_rot] + wq_deps))
                        t_a1 = reg("B", hlf).read(p.ve(lambda e, ft=ft, hlf=hlf: e.tensor_copy(
                            out=qaug[0:64, (2 * ft) * 512:(2 * ft + 1) * 512],
                            in_=psap("B", hlf, 0, 64)), deps=[t_rot] + wqa_deps))
                        t_a2 = reg("B", hlf).read(p.ve(lambda e, ft=ft, hlf=hlf: e.tensor_copy(
                            out=qaug[0:64, (2 * ft + 1) * 512:(2 * ft + 2) * 512],
                            in_=psap("B", hlf, 64, 128)), deps=[t_rot]))
                        bqa.wrote(t_a2)
                        q_toks.append(t_a1); q_toks.append(t_a2)
                    if t_ev is not None:
                        q_toks.append(t_ev)
            if L == 1 or f"q{L}c{c}" in tap_aps:
                bqc.wrote(q_toks[-1])
            if f"q{L}c{c}" in tap_aps:
                t_tapq = p.dma("gq", lambda e, L=L, c=c: e.dma_start(
                    out=tap_aps[f"q{L}c{c}"], in_=qc[:]), deps=list(q_toks), tag="tapq")
                bqc.read(t_tapq)
                q_toks = q_toks + [t_tapq]
            Bc, lam, _scl = LB[L]
            exp_toks = []
            qr_dmas = []
            bqrs = buf("qrow_scr")
            war_qrs = bqrs.wdeps() if L == 1 else []
            for pr in (range(NPAIR) if L == 1 else ()):
                pump_fill(3)
                pair_exp_toks = []
                kb = pr % 2
                bkhp = buf("khp", kb)
                t_khp = p.dma("wq", lambda e, pr=pr, kb=kb, L=L: e.dma_start(
                    out=khp[kb], in_=KS[L][pr * 128:(pr + 1) * 128, :]),
                    deps=bkhp.wdeps(kv_done) + buf("hc").peek(), tag=f"khpl{pr}")
                bkhp.wrote(t_khp)
                buf("kscr", L).read(t_khp)
                for qsub in range(4):
                    for kc in range(4):
                        sn = ("A", "B", "OA", "OB")[kc]
                        sten = PS[sn]
                        rs0, rs1 = reg(sn, 0), reg(sn, 1)
                        war_s = rs0.wdeps() + rs1.wdeps()
                        t_la = p.te(lambda e, kb=kb, kc=kc, pr=pr, qsub=qsub, sten=sten:
                                    e.matmul(sten[:, 0:512],
                                             qc[0:64, pr * 512 + qsub * 128:pr * 512 + qsub * 128 + 128],
                                             khp[kb][0:64, kc * 512:(kc + 1) * 512],
                                             start=True, stop=True, tile_position=(0, 0)),
                                    deps=[t_khp] + q_toks + war_s, tag=f"lA{pr}_{qsub}_{kc}")
                        t_lb = p.te(lambda e, kb=kb, kc=kc, pr=pr, qsub=qsub, sten=sten:
                                    e.matmul(sten[:, 512:1024],
                                             qc[64:128, pr * 512 + qsub * 128:pr * 512 + qsub * 128 + 128],
                                             khp[kb][64:128, kc * 512:(kc + 1) * 512],
                                             start=True, stop=True, tile_position=(64, 0)),
                                    deps=[], tag=f"lB{pr}_{qsub}_{kc}")
                        rs0.wrote(t_la); rs1.wrote(t_lb)
                        bkhp.read(t_la); bkhp.read(t_lb)
                        buf("hc").read(t_la); buf("hc").read(t_lb)
                        bqc.read(t_la); bqc.read(t_lb)
                        colA = (2 * pr) * 16 + qsub * 4 + kc
                        colB = (2 * pr + 1) * 16 + qsub * 4 + kc
                        if pr % 2 == 0:
                            t_ea = rs0.read(p.ac(lambda e, sten=sten, colA=colA, L=L:
                                                 e.activation(out=m1t[:], in_=sten[:, 0:512],
                                                              func=AF.Exp, bias=negLB[L][:, 0:1],
                                                              scale=lam * _scl,
                                                              accum_out=macc256[:, colA:colA + 1]),
                                                 deps=[t_la] + INIT, tag=f"eA{pr}_{qsub}_{kc}"))
                            t_eb2 = rs1.read(p.ac(lambda e, sten=sten, colB=colB, L=L:
                                                  e.activation(out=m1t[:], in_=sten[:, 512:1024],
                                                               func=AF.Exp, bias=negLB[L][:, 0:1],
                                                               scale=lam * _scl,
                                                               accum_out=macc256[:, colB:colB + 1]),
                                                  deps=[t_lb], tag=f"eB{pr}_{qsub}_{kc}"))
                        else:
                            t_ea = rs0.read(p.ve(lambda e, sten=sten, colA=colA:
                                                 e.reduce_max(out=macc256[:, colA:colA + 1],
                                                              in_=sten[:, 0:512],
                                                              axis=mybir.AxisListType.X),
                                                 deps=[t_la], tag=f"mA{pr}_{qsub}_{kc}"))
                            t_eb2 = rs1.read(p.ve(lambda e, sten=sten, colB=colB:
                                                  e.reduce_max(out=macc256[:, colB:colB + 1],
                                                               in_=sten[:, 512:1024],
                                                               axis=mybir.AxisListType.X),
                                                  deps=[t_lb], tag=f"mB{pr}_{qsub}_{kc}"))
                        exp_toks += [t_ea, t_eb2]
                        pair_exp_toks += [t_ea, t_eb2]
                g0 = 8 * pr
                mvp = macc256.ap().rearrange("p (g k) -> p k g", k=4)[:, :, g0:g0 + 8]
                mcp = mcols[:, g0:g0 + 8]
                cop = ALU.add if pr % 2 == 0 else ALU.max
                t_r1 = p.ve(lambda e, mvp=mvp, mcp=mcp, op=cop: e.tensor_tensor(
                    out=mcp, in0=mvp[:, 0, :], in1=mvp[:, 1, :], op=op),
                    deps=pair_exp_toks)
                t_r2 = p.ve(lambda e, mvp=mvp, mcp=mcp, op=cop: e.tensor_tensor(
                    out=mcp, in0=mcp, in1=mvp[:, 2, :], op=op), deps=[t_r1])
                combine_op = ALU.add if pr % 2 == 0 else ALU.max
                t_r3 = p.ve(lambda e, mvp=mvp, mcp=mcp, op=combine_op: e.tensor_tensor(
                    out=mcp, in0=mcp, in1=mvp[:, 3, :], op=op), deps=[t_r2])
                if pr % 2 == 0:
                    t_lnm = p.ac(lambda e, mcp=mcp: e.activation(
                        out=mcp, in_=mcp, func=AF.Ln, bias=tiny_col[:, 0:1]), deps=[t_r3])
                    t_aff = p.ve(lambda e, mcp=mcp, L=L: e.tensor_scalar(
                        out=mcp, in0=mcp, scalar1=-1.0 / (lam * _scl),
                        scalar2=-Bc / _scl, op0=ALU.mult, op1=ALU.add), deps=[t_lnm])
                else:
                    t_aff = p.ve(lambda e, mcp=mcp: e.tensor_scalar_mul(
                        out=mcp, in0=mcp, scalar1=-1.0), deps=[t_r3])
                for hh in (2 * pr, 2 * pr + 1):
                    for qsub in range(4):
                        qr_dmas.append(p.dma("sq", lambda e, hh=hh, qsub=qsub: e.dma_start(
                            out=qrow_scr[hh:hh + 1, qsub * 128:(qsub + 1) * 128],
                            in_=mcols[:, hh * 4 + qsub:hh * 4 + qsub + 1]),
                            deps=[t_aff] + (war_qrs if (hh == 0 and qsub == 0) else []),
                            tag=f"qr{hh}_{qsub}"))
            t_qrow = None
            if L == 1:
                bqa = buf("qaug")
                t_qrow = p.dma("gq", lambda e: e.dma_start(
                    out=qaug[64:65, :],
                    in_=qrow_scr.rearrange("h q -> (h q)").unsqueeze(0)),
                    deps=qr_dmas, tag="qrowl")
                bqa.read(t_qrow)
                bqrs.read(t_qrow)
            bqa = buf("qaug")
            den_dmas = []
            ev_toks = []
            for pr in range(NPAIR):
                pump_fill(3)
                kb = pr % 2
                bka, bkb_ = buf("kha", 0), buf("kha", 1)
                t_ka = p.dma("wq", lambda e, pr=pr, L=L: e.dma_start(
                    out=khaug[0][0:64, :],
                    in_=KS[L][(2 * pr) * 64:(2 * pr) * 64 + 64, :]),
                    deps=bka.wdeps(kv_done), tag=f"kal{pr}")
                bka.wrote(t_ka)
                buf("kscr", L).read(t_ka)
                t_kb2 = p.dma("wq", lambda e, pr=pr, L=L: e.dma_start(
                    out=khaug[1][0:64, :],
                    in_=KS[L][(2 * pr + 1) * 64:(2 * pr + 1) * 64 + 64, :]),
                    deps=bkb_.wdeps(kv_done), tag=f"kbl{pr}")
                bkb_.wrote(t_kb2)
                buf("kscr", L).read(t_kb2)
                bvh = buf("vh", kb)
                t_vh = p.dma("wq", lambda e, pr=pr, kb=kb, L=L: e.dma_start(
                    out=vhb[kb][:],
                    in_=VPS[L][:, pr * NKT * 132:(pr + 1) * NKT * 132]),
                    deps=bvh.wdeps(kv_done), tag=f"vhl{pr}")
                bvh.wrote(t_vh)
                buf("vpscr", L).read(t_vh)
                on = "OA" if pr % 2 == 0 else "OB"
                oten = PS[on]
                ro0, ro1 = reg(on, 0), reg(on, 1)
                war_o = ro0.wdeps() + ro1.wdeps()
                pv_last = None
                pend = None

                def emit_pv(kt, eb, t_e):
                    nonlocal pv_last
                    t_pa = p.te(lambda e, kb=kb, kt=kt, eb=eb, oten=oten: e.matmul(
                        oten[0:65, 0:512], vhb[kb][:, kt * 132:kt * 132 + 65],
                        eabb[eb][:, 0:512], start=(kt == 0), stop=(kt == NKT - 1)),
                        deps=[t_e, t_vh] + (war_o if kt == 0 else []), tag=f"pA{pr}_{kt}")
                    t_pb = p.te(lambda e, kb=kb, kt=kt, eb=eb, oten=oten: e.matmul(
                        oten[0:65, 512:1024], vhb[kb][:, kt * 132 + 66:kt * 132 + 131],
                        eabb[eb][:, 512:1024], start=(kt == 0), stop=(kt == NKT - 1)),
                        deps=[], tag=f"pB{pr}_{kt}")
                    buf("eab", eb).read(t_pa); buf("eab", eb).read(t_pb)
                    bvh.read(t_pa); bvh.read(t_pb)
                    pv_last = t_pb

                for kt in range(NKT):
                    sn = "A" if kt % 2 == 0 else "B"
                    sten = PS[sn]
                    rs0, rs1 = reg(sn, 0), reg(sn, 1)
                    war_s = rs0.wdeps() + rs1.wdeps()
                    t_sa = p.te(lambda e, kt=kt, pr=pr, sten=sten: e.matmul(
                        sten[:, 0:512], khaug[0][0:65, kt * 128:(kt + 1) * 128],
                        qaug[0:65, (2 * pr) * 512:(2 * pr + 1) * 512],
                        start=True, stop=True),
                        deps=[t_ka] + ([t_qrow] if t_qrow else []) + q_toks + war_s,
                        tag=f"sA{pr}_{kt}")
                    t_sb = p.te(lambda e, kt=kt, pr=pr, sten=sten: e.matmul(
                        sten[:, 512:1024], khaug[1][0:65, kt * 128:(kt + 1) * 128],
                        qaug[0:65, (2 * pr + 1) * 512:(2 * pr + 2) * 512],
                        start=True, stop=True),
                        deps=[t_kb2], tag=f"sB{pr}_{kt}")
                    rs0.wrote(t_sa); rs1.wrote(t_sb)
                    bka.read(t_sa); bkb_.read(t_sb)
                    bqa.read(t_sa); bqa.read(t_sb)
                    eb = kt % 2
                    beab = buf("eab", eb)
                    t_e = p.ac(lambda e, eb=eb, sten=sten: e.activation(
                        out=eabb[eb][:], in_=sten[:], func=AF.Exp,
                        bias=zero_col[:, 0:1], scale=scale),
                        deps=beab.wdeps([t_sa, t_sb]) + INIT, tag=f"exp{pr}_{kt}")
                    beab.wrote(t_e)
                    rs0.read(t_e); rs1.read(t_e)
                    if pend is not None:
                        emit_pv(*pend)
                    pend = (kt, eb, t_e)
                emit_pv(*pend)
                ro0.wrote(pv_last); ro1.wrote(pv_last)
                bdp = buf("denrowA"); bdp2 = buf("denrowB")
                t_dc = p.ve(lambda e, oten=oten: e.tensor_copy(
                    out=rstdrow[:], in_=oten[64:65, 0:512]),
                    deps=bdp.wdeps([pv_last]))
                t_dc2 = p.ve(lambda e, oten=oten: e.tensor_copy(
                    out=mrrow[:], in_=oten[64:65, 512:1024]),
                    deps=bdp2.wdeps([pv_last]))
                bdp.wrote(t_dc); bdp2.wrote(t_dc2)
                ro0.read(t_dc); ro1.read(t_dc)
                ro0.read(t_dc2); ro1.read(t_dc2)
                bdscr = buf("den_scr")
                t_dd = p.dma("sq", lambda e, pr=pr: e.dma_start(
                    out=den_scr[2 * pr:2 * pr + 1, :], in_=rstdrow[:]),
                    deps=[t_dc] + (bdscr.wdeps() if pr == 0 else []),
                    tag=f"dendA{pr}")
                t_dd2 = p.dma("sq", lambda e, pr=pr: e.dma_start(
                    out=den_scr[2 * pr + 1:2 * pr + 2, :], in_=mrrow[:]),
                    deps=[t_dc2], tag=f"dendB{pr}")
                bdp.read(t_dd); bdp2.read(t_dd2)
                den_dmas.append(t_dd); den_dmas.append(t_dd2)
                bsc = buf("sc")
                t_e1 = p.ve(lambda e, pr=pr, oten=oten: e.tensor_copy(
                    out=sc[0:64, pr * 512:(pr + 1) * 512], in_=oten[0:64, 0:512]),
                    deps=[pv_last] + (bsc.wdeps() if pr == 0 else []))
                t_e2 = p.ve(lambda e, pr=pr, oten=oten: e.tensor_copy(
                    out=sc[64:128, pr * 512:(pr + 1) * 512], in_=oten[0:64, 512:1024]),
                    deps=[pv_last])
                ro0.read(t_e1); ro0.read(t_e2)
                ro1.read(t_e1); ro1.read(t_e2)
                bsc.wrote(t_e2)
                ev_toks += [t_e1, t_e2]
            bd16 = buf("den16")
            t_d16 = p.dma("sq", lambda e: e.dma_start(out=den16[:], in_=den_scr),
                          deps=bd16.wdeps(den_dmas), tag="den16")
            buf("den_scr").read(t_d16)
            bd16.wrote(t_d16)
            tap_d = []
            if f"den{L}c{c}" in tap_aps:
                t_tapd = p.dma("gq", lambda e, L=L, c=c: e.dma_start(
                    out=tap_aps[f"den{L}c{c}"], in_=den_scr), deps=[t_d16], tag="tapden")
                tap_d = [t_tapd]
            t_ln = bd16.read(p.ac(lambda e: e.activation(
                out=den16[:], in_=den16[:], func=AF.Ln), deps=[t_d16] + tap_d))
            t_rd = bd16.read(p.ac(lambda e: e.activation(
                out=den16[:], in_=den16[:], func=AF.Exp, scale=-1.0), deps=[t_ln]))
            norm_toks = []
            for ft in range(8):
                hlf = ft % 2
                t_dv = mm_group("A", hlf, [
                    (lambda ft=ft: indb[:, ft * 128:(ft + 1) * 128],
                     lambda: den16[:], None)], deps=[t_rd], tag=f"divmm{ft}")
                if L == 0:
                    t_n = reg("A", hlf).read(p.ve(lambda e, ft=ft, hlf=hlf: e.tensor_tensor(
                        out=sc[:, ft * 512:(ft + 1) * 512],
                        in0=sc[:, ft * 512:(ft + 1) * 512],
                        in1=psap("A", hlf), op=ALU.mult),
                        deps=[t_dv, ev_toks[2 * ft], ev_toks[2 * ft + 1]]))
                else:
                    bhc = buf("hc")
                    t_n = reg("A", hlf).read(bhc.read(p.ve(
                        lambda e, ft=ft, hlf=hlf: e.tensor_tensor(
                            out=hc[:, ft * 512:(ft + 1) * 512],
                            in0=sc[:, ft * 512:(ft + 1) * 512],
                            in1=psap("A", hlf), op=ALU.mult),
                        deps=([t_dv, ev_toks[2 * ft], ev_toks[2 * ft + 1]] +
                              (bhc.wdeps() if ft == 0 else [])))))
                norm_toks.append(t_n)
            if L == 1:
                buf("hc").wrote(norm_toks[-1])
                proj_toks = []
                for fc in range(2):
                    bi = fc % 2
                    t_w = load_wT_fchunk(woutT, fc * 512, "w1c", bi, tag=f"wout{fc}")
                    for fi in range(4):
                        ft = fc * 4 + fi
                        hlf = ft % 2
                        t_mm = mm_group("B", hlf, [
                            (lambda dt=dt, fi=fi, bi=bi:
                             w1cb[bi][:, dt * 512 + fi * 128:dt * 512 + fi * 128 + 128],
                             lambda dt=dt: hc[:, dt * 512:(dt + 1) * 512], None)
                            for dt in range(8)],
                            deps=[t_w] + norm_toks, tag=f"pjmm{ft}")
                        buf("w1c", bi).read(t_mm)
                        buf("hc").read(t_mm)
                        t_pe = reg("B", hlf).read(p.ve(lambda e, ft=ft, hlf=hlf: e.tensor_scalar(
                            out=sc[:, ft * 512:(ft + 1) * 512], in0=psap("B", hlf),
                            scalar1=boutc[:, ft:ft + 1], scalar2=None, op0=ALU.add),
                            deps=[t_mm, norm_toks[ft]]))
                        proj_toks.append(t_pe)
                norm_toks = proj_toks
            if f"sc{L}c{c}" in tap_aps:
                t_tap = p.dma("gq", lambda e, L=L, c=c: e.dma_start(
                    out=tap_aps[f"sc{L}c{c}"], in_=sc[:]), deps=list(norm_toks),
                    tag="tapsc")
                buf("sc").read(t_tap)
                norm_toks = norm_toks + [t_tap]
            return norm_toks

        def phase_ffn(L, c, s_toks, xn_tok, xsrc, xdst, wprep_toks, x_extra=()):
            par = c % 2
            bxn = buf("xnb", par)
            bx = buf("xcb")
            t_x = load_x_chunk(xsrc, c, f"L{L}c{c}:x2")
            for d in x_extra:
                t_x.deps.append(d); d.need = True
            bx.wrote(t_x)
            lr_tok = None
            if L == 1:
                t_z1 = mm_group("OA", 0, [
                    (lambda dt=dt: vTb[:, dt * RANK:(dt + 1) * RANK],
                     lambda dt=dt: xnb[par][:, dt * CH:(dt + 1) * CH], None)
                    for dt in range(8)],
                    deps=[xn_tok] + INIT, tag="z1mm", p0=0, p1=RANK)
                bxn.read(t_z1)
                bz1 = buf("z1b")
                t_z1e = reg("OA", 0).read(p.ve(lambda e: e.tensor_copy(
                    out=z1b[:], in_=poA[0:RANK, 0:512]), deps=bz1.wdeps([t_z1])))
                bz1.wrote(t_z1e)
                t_z2 = mm_group("OB", 0, [(lambda: fbTb[:], lambda: z1b[:], None)],
                                deps=[t_z1e] + wprep_toks, tag="z2mm", p0=0, p1=RANK)
                bz1.read(t_z2)
                bz2 = buf("z2g")
                t_z2e = reg("OB", 0).read(p.ve(lambda e: e.tensor_scalar_mul(
                    out=z2g[:], in0=poB[0:RANK, 0:512], scalar1=gcol[:, 0:1]),
                    deps=bz2.wdeps([t_z2])))
                bz2.wrote(t_z2e)
                lr_tok = t_z2e
            h_toks = []
            bhc = buf("hc")
            war_h = bhc.wdeps()
            for ic in range(8):
                if L == 0:
                    pump_fill(2)
                bi = ic % 2
                bw = buf("w1c", bi)
                t_w = p.dma("wq", lambda e, ic=ic, bi=bi, L=L: e.dma_start(
                    out=w1cb[bi][:], in_=w1s[L][:, ic * 4096:(ic + 1) * 4096]),
                    deps=bw.wdeps(wprep_toks), tag=f"w1l{ic}")
                bw.wrote(t_w)
                t_ul = None
                if L == 1:
                    but = buf("uTsb", ic % 2)
                    t_ul = p.dma("wq", lambda e, ic=ic: e.dma_start(
                        out=uTsb[ic % 2][:], in_=uts_scr[:, ic * 512:(ic + 1) * 512]),
                        deps=but.wdeps(wprep_toks), tag=f"utl{ic}")
                    but.wrote(t_ul)
                for it in range(4):
                    ii = ic * 4 + it
                    hlf = it % 2
                    mms = [(lambda dt=dt, it=it, bi=bi:
                            w1cb[bi][:, dt * 512 + it * 128:dt * 512 + it * 128 + 128],
                            lambda dt=dt: xnb[par][:, dt * CH:(dt + 1) * CH], None)
                           for dt in range(8)]
                    if L == 1:
                        mms.append((lambda it=it, bu=ic % 2: uTsb[bu][:, it * 128:(it + 1) * 128],
                                    lambda: z2g[:], None))
                    t_mm = mm_group("A", hlf, mms,
                                    deps=[t_w, xn_tok] + ([lr_tok, t_ul] if lr_tok else []),
                                    tag=f"hmm{ii}")
                    if L == 1:
                        buf("uTsb", ic % 2).read(t_mm)
                    bw.read(t_mm)
                    bxn.read(t_mm)
                    if L == 1:
                        buf("z2g").read(t_mm)
                    if L == 0:
                        t_h = reg("A", hlf).read(p.ve(lambda e, ii=ii, hlf=hlf: e.tensor_scalar(
                            out=hc[:, ii * 512:(ii + 1) * 512], in0=psap("A", hlf),
                            scalar1=b1c[:, ii:ii + 1], scalar2=0.0,
                            op0=ALU.add, op1=ALU.max),
                            deps=[t_mm] + INIT + (war_h if ii == 0 else [])))
                    else:
                        bzt = buf("zt")
                        t_t = reg("A", hlf).read(p.ve(lambda e, hlf=hlf: e.tensor_scalar_max(
                            out=zt[:], in0=psap("A", hlf), scalar1=0.0),
                            deps=bzt.wdeps([t_mm]) + (war_h if ii == 0 else [])))
                        bzt.wrote(t_t)
                        t_h = bzt.read(p.ve(lambda e, ii=ii: e.scalar_tensor_tensor(
                            out=hc[:, ii * 512:(ii + 1) * 512], in0=zt[:],
                            scalar=sclf_sq[:, ii:ii + 1], in1=zt[:],
                            op0=ALU.mult, op1=ALU.mult), deps=[t_t] + wprep_toks))
                    h_toks.append(t_h)
            bhc.wrote(h_toks[-1])
            comb_toks = []
            for ft in range(8):
                bi = ft % 2
                bw = buf("w2c", bi)
                t_w = p.dma("wq", lambda e, ft=ft, bi=bi, L=L: e.dma_start(
                    out=w2cb[bi][:], in_=w2s[L][:, ft * 4096:(ft + 1) * 4096]),
                    deps=bw.wdeps(wprep_toks), tag=f"w2l{ft}")
                bw.wrote(t_w)
                hlf = ft % 2
                t_mm = mm_group("B", hlf, [
                    (lambda it=it, bi=bi: w2cb[bi][:, it * 128:(it + 1) * 128],
                     lambda it=it: hc[:, it * 512:(it + 1) * 512], None)
                    for it in range(32)],
                    deps=[t_w] + h_toks, tag=f"omm{ft}")
                bw.read(t_mm)
                bhc.read(t_mm)
                bz = buf("ztf")
                if L == 0:
                    t_z = reg("B", hlf).read(p.ve(lambda e, ft=ft, hlf=hlf: e.tensor_scalar_add(
                        out=zt[:], in0=psap("B", hlf), scalar1=b2c[:, ft:ft + 1]),
                        deps=bz.wdeps([t_mm]) + INIT))
                else:
                    t_z = reg("B", hlf).read(p.ve(lambda e, ft=ft, hlf=hlf: e.tensor_scalar_mul(
                        out=zt[:], in0=psap("B", hlf), scalar1=sclp[:, ft:ft + 1]),
                        deps=bz.wdeps([t_mm]) + wprep_toks))
                bz.wrote(t_z)
                bu = buf("ut")
                t_u = bz.read(p.ac(lambda e: e.activation(
                    out=ut[:], in_=zt[:], func=AF.Exp, scale=-1.0),
                    deps=bu.wdeps([t_z])))
                bu.wrote(t_u)
                bt_ = buf("ttt")
                t_t = bu.read(p.ac(lambda e: e.activation(
                    out=ttt[:], in_=ut[:], func=AF.Ln, bias=one_col[:, 0:1]),
                    deps=bt_.wdeps([t_u]) + INIT))
                bt_.wrote(t_t)
                t_w2 = bt_.read(p.ac(lambda e: e.activation(
                    out=ttt[:], in_=ttt[:], func=AF.Exp, scale=-1.0), deps=[t_t]))
                bt_.wrote(t_w2)
                bu2 = buf("ut")
                t_g = bz.read(bt_.read(p.ve(lambda e: e.tensor_mul(
                    out=ut[:], in0=zt[:], in1=ttt[:]), deps=bu2.wdeps([t_w2, t_z]))))
                bu2.wrote(t_g)
                t_sg = buf("sc").read(bu2.read(p.po(lambda e, ft=ft: e.tensor_mul(
                    out=zt[:], in0=sc[:, ft * 512:(ft + 1) * 512], in1=ut[:]),
                    deps=bz.wdeps([t_g] + s_toks))))
                bz.wrote(t_sg)
                t_cx = bz.read(bx.read(p.ve(lambda e, ft=ft: e.tensor_add(
                    out=xcb[:, ft * 512:(ft + 1) * 512],
                    in0=xcb[:, ft * 512:(ft + 1) * 512], in1=zt[:]),
                    deps=[t_sg, t_x])))
                comb_toks.append(t_cx)
            t_xst = p.dma("gq", lambda e, c=c: e.dma_start(
                out=xdst[:, c * 4096:(c + 1) * 4096], in_=xcb[:]),
                deps=comb_toks, tag=f"L{L}c{c}:xst")
            bx.read(t_xst)
            return t_xst, comb_toks

        def make_l1_prep():
            st = {}

            def job_scales():
                toks = []
                red_toks = []
                btm0 = buf("sc")
                for jj in range(8):
                    t_l = p.dma("sq", lambda e, jj=jj: e.dma_start(
                        out=sc[:], in_=wfc[jj * 512:(jj + 1) * 512, :].rearrange(
                            "(r p) d -> p r d", p=128)),
                        deps=btm0.wdeps(), tag=f"fcn{jj}")
                    btm0.wrote(t_l)
                    for r in range(4):
                        t_r = btm0.read(p.ve(lambda e, jj=jj, r=r: e.tensor_reduce(
                            out=rcpf[:, jj * 4 + r:jj * 4 + r + 1],
                            in_=sc[:, r * 1024:(r + 1) * 1024],
                            axis=mybir.AxisListType.X,
                            op=ALU.add, apply_absolute_value=True), deps=[t_l]))
                        red_toks.append(t_r)
                t_s = p.ve(lambda e: e.tensor_scalar(
                    out=sclf_sq[:], in0=rcpf[:], scalar1=1.0 / D, scalar2=1e-5,
                    op0=ALU.mult, op1=ALU.max), deps=red_toks)
                t_rc = p.ve(lambda e: e.reciprocal(out=rcpf[:], in_=sclf_sq[:]), deps=[t_s])
                t_sq = p.ve(lambda e: e.tensor_mul(out=sclf_sq[:], in0=sclf_sq[:],
                                                   in1=sclf_sq[:]), deps=[t_rc])
                toks.append(t_sq)
                rowd = []
                for j in range(32):
                    rowd.append(p.dma("sq", lambda e, j=j: e.dma_start(
                        out=s1r_scr[0:1, j * 128:(j + 1) * 128],
                        in_=rcpf[:, j:j + 1]), deps=[t_rc], tag=f"s1r{j}"))
                btm = buf("sc")
                red2 = []
                for j in range(8):
                    t_l = p.dma("sq", lambda e, j=j: e.dma_start(
                        out=sc[:], in_=wproj[j * 128:(j + 1) * 128, :]),
                        deps=btm.wdeps(), tag=f"pjn{j}")
                    btm.wrote(t_l)
                    t_r1 = btm.read(p.ve(lambda e, j=j: e.tensor_reduce(
                        out=sclp[:, j:j + 1], in_=sc[:],
                        axis=mybir.AxisListType.X, op=ALU.add,
                        apply_absolute_value=True), deps=[t_l]))
                    t_c2 = p.ve(lambda e, j=j: e.tensor_scalar(
                        out=sclp[:, j:j + 1], in0=sclp[:, j:j + 1], scalar1=1.0 / INTER,
                        scalar2=1e-5, op0=ALU.mult, op1=ALU.max), deps=[t_r1])
                    red2.append(t_c2)
                t_rc2 = p.ve(lambda e: e.reciprocal(out=rcpp[:], in_=sclp[:]), deps=red2)
                toks.append(t_rc2)
                rowd2 = []
                for j in range(8):
                    rowd2.append(p.dma("sq", lambda e, j=j: e.dma_start(
                        out=s2r_scr[0:1, j * 128:(j + 1) * 128],
                        in_=rcpp[:, j:j + 1]), deps=[t_rc2], tag=f"s2r{j}"))
                rg = reg("OB", 0)
                war = rg.wdeps()
                t_g = p.te(lambda e: e.matmul(poB[0:RANK, 0:1], ones_row_f[0:1, 0:RANK],
                                              graw[:], start=True, stop=True),
                           deps=INIT + war, tag="gatebc")
                rg.wrote(t_g)
                t_ge = rg.read(p.ve(lambda e: e.tensor_copy(
                    out=gcol[:], in_=poB[0:RANK, 0:1]), deps=[t_g]))
                toks.append(t_ge)
                st["rowd"] = rowd
                st["rowd2"] = rowd2
                return toks

            def job_tern(si):
                bs1 = buf("s1row")
                bs2 = buf("s2row")
                cell = {}
                units = []

                def u_bc1(icc):
                    def go():
                        t_row1 = p.dma("sq", lambda e, icc=icc: e.dma_start(
                            out=s1row[:], in_=s1r_scr[0:1, icc * 512:(icc + 1) * 512]),
                            deps=bs1.wdeps(st["rowd"]), tag=f"s1row{icc}")
                        bs1.wrote(t_row1)
                        t_bc = mm_group("OB", 1, [
                            (lambda: ones_row_f[:], lambda: s1row[:], None)],
                            deps=[t_row1], tag=f"bc1_{icc}")
                        bs1.read(t_bc)
                        brb = buf("rb512")
                        t_be = reg("OB", 1).read(p.ve(lambda e: e.tensor_copy(
                            out=rb512[:], in_=poB[:, 512:1024]),
                            deps=brb.wdeps([t_bc])))
                        brb.wrote(t_be)
                        cell["be1"] = t_be
                        return []
                    return go

                def u_uts(icc):
                    def go():
                        brb = buf("rb512")
                        but = buf("uTf")
                        t_ul = p.dma("gq", lambda e, icc=icc: e.dma_start(
                            out=uTf[:], in_=uT[:, icc * 512:(icc + 1) * 512]),
                            deps=but.wdeps(), tag=f"uTl{icc}")
                        but.wrote(t_ul)
                        bub = buf("uTsb", icc % 2)
                        t_u = but.read(brb.read(p.ve(lambda e, icc=icc: e.tensor_tensor(
                            out=uTsb[icc % 2][:], in0=uTf[:],
                            in1=rb512[0:RANK, :], op=ALU.mult),
                            deps=bub.wdeps([cell["be1"], t_ul]))))
                        bub.wrote(t_u)
                        t_us = p.dma("gq", lambda e, icc=icc: e.dma_start(
                            out=uts_scr[:, icc * 512:(icc + 1) * 512],
                            in_=uTsb[icc % 2][:]),
                            deps=[t_u], tag=f"utst{icc}")
                        bub.read(t_us)
                        return [t_us]
                    return go

                def u_fct(icc, dt):
                    def go():
                        bpw = buf("pw", dt % 2)
                        t_l = p.dma("gq", lambda e: e.dma_start(
                            out=pwb[dt % 2][:],
                            in_=wfcT[dt * 128:(dt + 1) * 128,
                                     icc * 512:(icc + 1) * 512]),
                            deps=bpw.wdeps(), tag=f"fct{icc}_{dt}")
                        bpw.wrote(t_l)
                        brb = buf("rb512")
                        bpt = buf("stg", dt % 2)
                        t_m = brb.read(p.ve(lambda e: e.tensor_tensor(
                            out=pwb[dt % 2][:], in0=pwb[dt % 2][:],
                            in1=rb512[:], op=ALU.mult), deps=[t_l, cell["be1"]]))
                        t_rn = p.ve(lambda e: e.tensor_scalar(
                            out=pwb[dt % 2][:], in0=pwb[dt % 2][:], scalar1=MAGIC,
                            scalar2=MAGIC, op0=ALU.add, op1=ALU.subtract),
                            deps=[t_m])
                        t_cl = bpw.read(p.ve(lambda e: e.tensor_scalar(
                            out=stgb[dt % 2][:], in0=pwb[dt % 2][:], scalar1=1.0,
                            scalar2=-1.0, op0=ALU.min, op1=ALU.max),
                            deps=bpt.wdeps([t_rn])))
                        bpt.wrote(t_cl)
                        t_st2 = p.dma("gq", lambda e: e.dma_start(
                            out=w1s[1][:, icc * 4096 + dt * 512:
                                       icc * 4096 + (dt + 1) * 512],
                            in_=stgb[dt % 2][:]), deps=[t_cl],
                            tag=f"fcts{icc}_{dt}")
                        bpt.read(t_st2)
                        return [t_st2]
                    return go

                def u_bc2():
                    def go():
                        t_row2 = p.dma("sq", lambda e: e.dma_start(
                            out=s2row[:], in_=s2r_scr[0:1, 0:512]),
                            deps=bs2.wdeps(st["rowd2"]), tag=f"s2rowa{si}")
                        bs2.wrote(t_row2)
                        t_bc2 = mm_group("OB", 1, [
                            (lambda: ones_row_f[:], lambda: s2row[:], None)],
                            deps=[t_row2], tag=f"bc2a{si}")
                        bs2.read(t_bc2)
                        brb = buf("rb512")
                        t_be2 = reg("OB", 1).read(p.ve(lambda e: e.tensor_copy(
                            out=rb512[:], in_=poB[:, 512:1024]),
                            deps=brb.wdeps([t_bc2])))
                        brb.wrote(t_be2)
                        t_row3 = p.dma("sq", lambda e: e.dma_start(
                            out=s2row[:], in_=s2r_scr[0:1, 512:1024]),
                            deps=bs2.wdeps(), tag=f"s2rowb{si}")
                        bs2.wrote(t_row3)
                        t_bc3 = mm_group("OB", 0, [
                            (lambda: ones_row_f[:], lambda: s2row[:], None)],
                            deps=[t_row3], tag=f"bc2b{si}")
                        bs2.read(t_bc3)
                        brc = buf("rb512b")
                        t_be3 = reg("OB", 0).read(p.ve(lambda e: e.tensor_copy(
                            out=rb5122[:], in_=poB[:, 0:512]),
                            deps=brc.wdeps([t_bc3])))
                        brc.wrote(t_be3)
                        cell["be2"] = t_be2
                        cell["be3"] = t_be3
                        return []
                    return go

                def u_pjt(jt, hhalf, bi_):
                    def go():
                        brx = buf("rb512") if hhalf == 0 else buf("rb512b")
                        rbsel = rb512 if hhalf == 0 else rb5122
                        t_bex = cell["be2"] if hhalf == 0 else cell["be3"]
                        bpw = buf("pw", bi_)
                        t_l = p.dma("gq", lambda e: e.dma_start(
                            out=pwb[bi_][:],
                            in_=wprojT[jt * 128:(jt + 1) * 128,
                                       hhalf * 512:(hhalf + 1) * 512]),
                            deps=bpw.wdeps(), tag=f"pjt{jt}_{hhalf}")
                        bpw.wrote(t_l)
                        bpt = buf("stg", bi_)
                        t_m1 = brx.read(p.ve(lambda e: e.tensor_tensor(
                            out=pwb[bi_][:], in0=pwb[bi_][:],
                            in1=rbsel[:], op=ALU.mult), deps=[t_l, t_bex]))
                        t_r1 = p.ve(lambda e: e.tensor_scalar(
                            out=pwb[bi_][:], in0=pwb[bi_][:], scalar1=MAGIC,
                            scalar2=MAGIC, op0=ALU.add, op1=ALU.subtract),
                            deps=[t_m1])
                        t_c1 = bpw.read(p.ve(lambda e: e.tensor_scalar(
                            out=stgb[bi_][:], in0=pwb[bi_][:], scalar1=1.0,
                            scalar2=-1.0, op0=ALU.min, op1=ALU.max),
                            deps=bpt.wdeps([t_r1])))
                        bpt.wrote(t_c1)
                        t_st2 = p.dma("gq", lambda e: e.dma_start(
                            out=w2s[1].rearrange(
                                "p (ft it f) -> p ft it f", ft=8, it=32)[
                                :, hhalf * 4:(hhalf + 1) * 4, jt, :],
                            in_=stgb[bi_].ap().rearrange(
                                "p (ff f) -> p ff f", ff=4)),
                            deps=[t_c1], tag=f"pjts{jt}_{hhalf}")
                        bpt.read(t_st2)
                        return [t_st2]
                    return go

                for icc in (2 * si, 2 * si + 1):
                    units.append(u_bc1(icc))
                    units.append(u_uts(icc))
                    for dt in range(8):
                        units.append(u_fct(icc, dt))
                units.append(u_bc2())
                ji = 0
                for jt in range(8 * si, 8 * si + 8):
                    for hhalf in (0, 1):
                        units.append(u_pjt(jt, hhalf, ji % 2))
                        ji += 1
                return units

            return job_scales, job_tern


        def tap_dump(name, src_ap, deps):
            if name in tap_aps:
                return [p.dma("gq", lambda e: e.dma_start(out=tap_aps[name], in_=src_ap),
                              deps=deps, tag="tap_" + name)]
            return []

        final_stores = []
        all_taps = []
        prev_xsts = []
        ln1_toks = []
        kv1_stores = []
        for L in range(n_layers):
            xsrc = xin if L == 0 else x_scr
            xdst = (y if n_layers == 1 else x_scr) if L == 0 else y
            wprep = L0W_TOKS if L == 0 else prep1_list
            if L == 0 or not ln1_toks:
                ln_toks = []
                kv_stores = []
                for c in range(NCHUNK):
                    extra = [prev_xsts[c]] if (L == 1 and prev_xsts) else []
                    t_lnst, _ = phase_ln(L, c, xsrc, extra_deps=extra)
                    ln_toks.append(t_lnst)
                    kv_stores += phase_kv(L, c, ln_toks[c])
            else:
                ln_toks = ln1_toks
                kv_stores = kv1_stores
            tapd = tap_dump(f"xn{L}", xn_scr, ln_toks)
            tapd += tap_dump(f"k{L}", KS[L], kv_stores)
            tapd += tap_dump(f"vp{L}", VPS[L], kv_stores)
            all_taps += tapd
            prep_tern = None
            if L == 0 and n_layers == 2:
                job_scales, prep_tern = make_l1_prep()
                prep1_list.extend(job_scales())
            xsts = []
            for c in range(NCHUNK):
                par = c % 2
                bxn = buf("xnb", par)
                t_xn = p.dma("wq", lambda e, c=c, par=par: e.dma_start(
                    out=xnb[par][:], in_=xn_scr[:, c * 4096:(c + 1) * 4096]),
                    deps=bxn.wdeps([ln_toks[c]]), tag=f"L{L}c{c}:xn3")
                bxn.wrote(t_xn)
                s_toks = phase_attn(L, c, kv_stores + tapd + VPPAD, t_xn)
                if L == 1:
                    pump_fill(len(fillq))
                t_xst, comb = phase_ffn(
                    L, c, s_toks, t_xn, xsrc, xdst, list(wprep),
                    x_extra=([prev_xsts[c]] if (L == 1 and prev_xsts) else []))
                xsts.append(t_xst)
                if L == 0 and n_layers == 2:
                    t_st1, t_xn1 = phase_ln(1, c, None, x_res=comb,
                                            st_extra=[t_xn])
                    ln1_toks.append(t_st1)
                    kv1_stores += phase_kv(1, c, t_st1, xn_res=t_xn1)
                if prep_tern is not None:
                    fillq.extend(prep_tern(c))
            if L == 0 and n_layers == 2 and "x0" in tap_aps:
                t_tap = p.dma("gq", lambda e: e.dma_start(
                    out=tap_aps["x0"], in_=x_scr), deps=list(xsts), tag="tapx0")
                all_taps.append(t_tap)
            if L == 0 and n_layers == 2 and "tfcT" in tap_aps:
                all_taps.append(p.dma("gq", lambda e: e.dma_start(
                    out=tap_aps["tfcT"], in_=w1s[1]), deps=list(prep1_list), tag="taptf"))
            if L == 0 and n_layers == 2 and "tpjT" in tap_aps:
                all_taps.append(p.dma("gq", lambda e: e.dma_start(
                    out=tap_aps["tpjT"], in_=w2s[1]), deps=list(prep1_list), tag="taptp"))
            prev_xsts = xsts
            final_stores = xsts

        p.po(lambda e: e.nop(), deps=final_stores + all_taps, tag="fence")

        p.finalize()
        p.replay(nc, sems)
        nc.all_engine_barrier()
    return nc


_NC_CACHE = {}


def _get_nc(n_layers=2, taps=()):
    key = (n_layers, tuple(taps))
    if key not in _NC_CACHE:
        _NC_CACHE[key] = build_nc(n_layers, taps)
    return _NC_CACHE[key]


def make_in_maps(inputs):
    perm = _parity_perm()
    wqkv = np.asarray(inputs["L1_attn_w_qkv"], np.float32)
    wq1 = wqkv[0:D][perm]
    wk1 = wqkv[D:2 * D][perm]
    wv1 = wqkv[2 * D:3 * D]
    wqkvT = np.ascontiguousarray(np.concatenate([wq1, wk1, wv1], 0).T)
    T1, T2 = _rope_tables()
    R1T, R2T = _rot_mats()
    wfc = np.ascontiguousarray(np.asarray(inputs["L1_ffn_w_fc"], np.float32))
    wproj = np.ascontiguousarray(np.asarray(inputs["L1_ffn_w_proj"], np.float32))
    common = {
        "wqT0": np.ascontiguousarray(np.asarray(inputs["L0_attn_Wq"], np.float32).T),
        "wkT0": np.ascontiguousarray(np.asarray(inputs["L0_attn_Wk"], np.float32).T),
        "wvT0": np.ascontiguousarray(np.asarray(inputs["L0_attn_Wv"], np.float32).T),
        "w1": np.ascontiguousarray(np.asarray(inputs["L0_ffn_W1"], np.float32)),
        "b1": np.ascontiguousarray(np.asarray(inputs["L0_ffn_b1"], np.float32).reshape(32, 128).T),
        "w2": np.ascontiguousarray(np.asarray(inputs["L0_ffn_W2"], np.float32)),
        "b2": np.ascontiguousarray(np.asarray(inputs["L0_ffn_b2"], np.float32).reshape(8, 128).T),
        "wqkvT": wqkvT,
        "woutT": np.ascontiguousarray(np.asarray(inputs["L1_attn_w_out"], np.float32).T),
        "bout": np.ascontiguousarray(np.asarray(inputs["L1_attn_b_out"], np.float32).reshape(8, 128).T),
        "wfc": wfc,
        "wfcT": np.ascontiguousarray(wfc.T),
        "wproj": wproj,
        "wprojT": np.ascontiguousarray(wproj.T),
        "uT": np.ascontiguousarray(np.asarray(inputs["L1_ffn_u"], np.float32).T),
        "vT": np.ascontiguousarray(np.asarray(inputs["L1_ffn_v"], np.float32).T),
        "fbT": np.ascontiguousarray(np.asarray(inputs["L1_ffn_fast_b"], np.float32).T),
        "gate": np.asarray(inputs["L1_ffn_gate"], np.float32).reshape(1),
        "t1c": T1, "t2c": T2, "r1T": R1T, "r2T": R2T,
        "indm": _indmat(),
    }
    x = np.asarray(inputs["L0_attn_x"], np.float32)
    in_maps = []
    for i in range(8):
        m = dict(common)
        xt = x[i].reshape(NCHUNK, CH, 8, 128)
        m["x"] = np.ascontiguousarray(xt.transpose(3, 0, 2, 1).reshape(128, NCHUNK * 4096))
        in_maps.append(m)
    return in_maps


def kernel(**inputs):
    in_maps = make_in_maps(inputs)
    nc = _get_nc()
    res = run_bass_kernel_spmd(nc, in_maps, core_ids=list(range(8)))
    outs = []
    for r in res.results:
        yt = np.asarray(r["y"]).reshape(128, NCHUNK, 8, CH)
        outs.append(yt.transpose(1, 3, 2, 0).reshape(S, D))
    return np.stack(outs, 0).astype(np.float32)


# version 19
# speedup vs baseline: 1.1635x; 1.0568x over previous
import numpy as np
import ml_dtypes
import concourse.bass as bass
import concourse.mybir as mybir
from concourse.bass_utils import run_bass_kernel_spmd

F32 = mybir.dt.float32
BF16 = mybir.dt.bfloat16
AF = mybir.ActivationFunctionType
ALU = mybir.AluOpType

D = 1024
S = 2048
NCHUNK = 4
CH = S // NCHUNK
NH = 16
HD = 64
NPAIR = 8
NKT = 16
CSHIFT = 6.0
EPS = 1e-5
MAGIC = 12582912.0
INTER = 4096
RANK = 16


class Step:
    __slots__ = ("eng", "sem", "inc", "fn", "deps", "need", "val", "tag")

    def __init__(self, eng, sem, inc, fn, deps, tag):
        self.eng, self.sem, self.inc, self.fn = eng, sem, inc, fn
        self.deps = [d for d in deps if d is not None]
        self.need = False
        self.val = None
        self.tag = tag


class Prog:
    QENG = {"wq": "sync", "sq": "scalar", "gq": "gpsimd", "cq": "gpsimd"}

    def __init__(self):
        self.steps = []

    def _add(self, eng, sem, inc, fn, deps, tag):
        st = Step(eng, sem, inc, fn, deps, tag)
        for d in st.deps:
            d.need = True
        self.steps.append(st)
        return st

    def dma(self, q, fn, deps=(), tag=""):
        st = self._add(self.QENG[q], q, 16, fn, deps, tag)
        st.need = True
        return st

    def te(self, fn, deps=(), tag=""):
        return self._add("tensor", "te", 1, fn, deps, tag)

    def ve(self, fn, deps=(), tag=""):
        return self._add("vector", "ve", 1, fn, deps, tag)

    def ac(self, fn, deps=(), tag=""):
        return self._add("scalar", "ac", 1, fn, deps, tag)

    def po(self, fn, deps=(), tag=""):
        return self._add("gpsimd", "po", 1, fn, deps, tag)

    def finalize(self):
        cnt = {}
        for st in self.steps:
            if st.need:
                cnt[st.sem] = cnt.get(st.sem, 0) + st.inc
                st.val = cnt[st.sem]

    def replay(self, nc, sems):
        with nc.Block() as blk:
            for engname in ("sync", "vector", "scalar", "tensor", "gpsimd"):
                mysteps = [s for s in self.steps if s.eng == engname]

                def body(eng, mysteps=mysteps):
                    hwm = {}
                    for st in mysteps:
                        for d in st.deps:
                            if d.val is not None and hwm.get(d.sem, 0) < d.val:
                                eng.wait_ge(sems[d.sem], d.val)
                                hwm[d.sem] = d.val
                        inst = st.fn(eng)
                        if st.need:
                            inst.then_inc(sems[st.sem], st.inc)

                getattr(blk, engname)(body)


class Buf:

    def __init__(self):
        self.users = []
        self.writer = None

    def wdeps(self, extra=()):
        d = list(extra) + self.users
        self.users = []
        return d

    def wrote(self, step):
        self.writer = step
        return step

    def read(self, step):
        self.users.append(step)
        return step

    def peek(self):
        return list(self.users)


def _rope_tables():
    inv_freq = 1.0 / 10000.0 ** (np.arange(0, HD, 2, dtype=np.float32) / HD)
    t = np.arange(S, dtype=np.float32)
    fr = np.outer(t, inv_freq)
    cosT = np.cos(fr).T.astype(np.float32)
    sinT = np.sin(fr).T.astype(np.float32)
    T1 = np.concatenate([cosT, sinT, cosT, sinT], 0)
    T2 = np.concatenate([sinT, cosT, sinT, cosT], 0)
    return np.ascontiguousarray(T1), np.ascontiguousarray(T2)


def _rot_mats():
    R1 = np.zeros((128, 128), np.float32)
    R2 = np.zeros((128, 128), np.float32)
    for base in (0, 64):
        for r in range(32):
            R1[base + r, base + r] = 1.0
            R1[base + r, base + r + 32] = -1.0
            R2[base + 32 + r, base + r] = 1.0
            R2[base + 32 + r, base + 32 + r] = 1.0
    return np.ascontiguousarray(R1.T), np.ascontiguousarray(R2.T)


def _parity_perm():
    p = np.zeros(D, np.int64)
    for h in range(NH):
        b = h * HD
        p[b:b + 32] = b + np.arange(0, HD, 2)
        p[b + 32:b + HD] = b + np.arange(1, HD, 2)
    return p


def _indmat():
    ind = np.zeros((NH, D), np.float32)
    for h in range(NH):
        ind[h, h * HD:(h + 1) * HD] = 1.0
    return ind


def build_nc(n_layers=2, taps=()):
    nc = bass.Bass("TRN2", target_bir_lowering=False)

    def ein(name, shape):
        return nc.dram_tensor(name, list(shape), F32, kind="ExternalInput").ap()

    xin = ein("x", [128, NCHUNK * 4096])
    y = nc.dram_tensor("y", [128, NCHUNK * 4096], F32, kind="ExternalOutput").ap()

    wqT0 = ein("wqT0", [D, D]); wkT0 = ein("wkT0", [D, D]); wvT0 = ein("wvT0", [D, D])
    w1 = ein("w1", [D, INTER]); b1 = ein("b1", [128, 32])
    w2 = ein("w2", [INTER, D]); b2 = ein("b2", [128, 8])
    wqkvT = ein("wqkvT", [D, 3 * D])
    woutT = ein("woutT", [D, D]); bout = ein("bout", [128, 8])
    wfc = ein("wfc", [INTER, D]); wfcT = ein("wfcT", [D, INTER])
    wproj = ein("wproj", [D, INTER]); wprojT = ein("wprojT", [INTER, D])
    uT = ein("uT", [RANK, INTER]); vT = ein("vT", [D, RANK])
    fbT = ein("fbT", [RANK, RANK]); gate = ein("gate", [1])
    t1c = ein("t1c", [128, S]); t2c = ein("t2c", [128, S])
    r1T = ein("r1T", [128, 128]); r2T = ein("r2T", [128, 128])
    indmat = ein("indm", [NH, D])

    def scr(name, shape, dtype):
        return nc.dram_tensor(name, list(shape), dtype, kind="Internal").ap()

    x_scr = scr("x_scr", [128, NCHUNK * 4096], F32)
    xn_scr = scr("xn_scr", [128, NCHUNK * 4096], BF16)
    k_scr = scr("k_scr", [D, S], BF16)
    vp_scr = scr("vp_scr", [128, NPAIR * NKT * 132], BF16)
    k_scr1 = scr("k_scr1", [D, S], BF16)
    vp_scr1 = scr("vp_scr1", [128, NPAIR * NKT * 132], BF16)
    den_scr = scr("den_scr", [NH, CH], F32)
    w1s = [scr("w1bf", [128, 8 * 4096], BF16), scr("tfcT", [128, 8 * 4096], BF16)]
    w2s = [scr("w2bf", [128, 8 * 4096], BF16), scr("tpjT", [128, 8 * 4096], BF16)]
    s1r_scr = scr("s1r_scr", [1, INTER], F32)
    qrow_scr = scr("qrow_scr", [NH, CH], F32)
    uts_scr = scr("uts_scr", [RANK, INTER], BF16)
    s2r_scr = scr("s2r_scr", [1, D], F32)

    tap_aps = {}
    for name, shape in taps:
        tap_aps[name] = nc.dram_tensor("tap_" + name, list(shape), F32,
                                       kind="ExternalOutput").ap()

    with nc.cleanup_on_exit():
        A = nc.alloc_sbuf_tensor
        xnb = [A("xnb0", [128, 8 * CH], BF16), A("xnb1", [128, 8 * CH], BF16)]
        xcb = A("xcb", [128, 8 * CH], F32)
        khaug = [A("kha0", [65, S], BF16), A("kha1", [65, S], BF16)]
        vhb = [A("vh0", [128, NKT * 132], BF16), A("vh1", [128, NKT * 132], BF16)]
        qc = A("qc", [128, 8 * CH], BF16)
        qaug = A("qaug", [65, 16 * CH], BF16)
        eabb = [A("eab0", [128, 1024], BF16), A("eab1", [128, 1024], BF16)]
        sc = A("sc", [128, 8 * CH], F32)
        hc = A("hc", [128, 32 * 512], BF16)
        w1cb = [A("w1c0", [128, 4096], BF16), A("w1c1", [128, 4096], BF16)]
        w2cb = [A("w2c0", [128, 4096], BF16), A("w2c1", [128, 4096], BF16)]
        stgb = [A("stg0", [128, 512], BF16), A("stg1", [128, 512], BF16)]
        vstg = A("vstg", [128, 528], BF16)
        m1t = A("m1t", [128, 512], BF16)
        m2t = A("m2t", [128, 512], BF16)
        zt = A("zt", [128, 512], F32)
        ut = A("ut", [128, 512], F32)
        ttt = A("ttt", [128, 512], F32)
        t1b = A("t1b", [128, S], BF16)
        t2b = A("t2b", [128, S], BF16)
        r1b = A("r1b", [128, 128], BF16)
        r2b = A("r2b", [128, 128], BF16)
        indb = A("indb", [NH, D], F32)
        macc256 = A("macc256", [128, 256], F32)
        mcols = A("mcols", [128, 64], F32)
        negLB = [A("negLB0", [128, 1], F32), A("negLB1", [128, 1], F32)]
        zero_col = A("zero_col", [128, 1], F32)
        tiny_col = A("tiny_col", [128, 1], F32)
        ones_col_bf = A("ones_col_bf", [128, 1], BF16)
        ones_row_f = A("ones_row_f", [1, 128], F32)
        one_col = A("one_col", [128, 1], F32)
        negC_col = A("negC_col", [128, 1], F32)
        eps1 = A("eps1", [1, 1], F32)
        b1c = A("b1c", [128, 32], F32)
        b2c = A("b2c", [128, 8], F32)
        boutc = A("boutc", [128, 8], F32)
        mrow = A("mrow", [1, CH], F32)
        msqrow = A("msqrow", [1, CH], F32)
        rstdrow = A("rstdrow", [1, CH], F32)
        mrrow = A("mrrow", [1, CH], F32)
        den16 = A("den16", [NH, CH], F32)
        uTsb = [A("uTs0", [RANK, 512], BF16), A("uTs1", [RANK, 512], BF16)]
        uTf = A("uTf", [RANK, 512], BF16)
        vTb = A("vTb", [128, 8 * RANK], BF16)
        fbTb = A("fbTb", [RANK, RANK], BF16)
        gcol = A("gcol", [RANK, 1], F32)
        graw = A("graw", [1, 1], F32)
        z1b = A("z1b", [RANK, CH], BF16)
        z2g = A("z2g", [RANK, CH], BF16)
        sclf_sq = A("sclf_sq", [128, 32], F32)
        sclp = A("sclp", [128, 8], F32)
        rcpf = A("rcpf", [128, 32], F32)
        rcpp = A("rcpp", [128, 8], F32)
        s1row = mrow
        s2row = msqrow
        rb512 = A("rb512", [128, 512], F32)
        rb5122 = A("rb5122", [128, 512], F32)
        pwb = [A("pw0", [128, 512], F32), A("pw1", [128, 512], F32)]

        psA = nc.alloc_psum_tensor("psA", [128, 1024], F32)
        psB = nc.alloc_psum_tensor("psB", [128, 1024], F32)
        poA = nc.alloc_psum_tensor("poA", [128, 1024], F32)
        poB = nc.alloc_psum_tensor("poB", [128, 1024], F32)
        PS = {"A": psA, "B": psB, "OA": poA, "OB": poB}
        khp = [hc.ap()[:, 8192:10240], hc.ap()[:, 10240:12288]]

        sems = {k: nc.alloc_semaphore(k) for k in
                ("wq", "sq", "gq", "cq", "te", "ve", "ac", "po")}

        p = Prog()
        B = {}

        def buf(name, i=0):
            key = (name, i)
            if key not in B:
                B[key] = Buf()
            return B[key]

        def reg(name, half):
            return buf("ps" + name, half)

        KS = [k_scr, k_scr1]
        VPS = [vp_scr, vp_scr1]

        def psap(name, half, p0=0, p1=128):
            return PS[name][p0:p1, half * 512:half * 512 + 512]

        def mm_group(rname, half, mms, deps=(), tag="", p0=0, p1=128):
            r = reg(rname, half)
            war = r.wdeps()
            last = None
            n = len(mms)
            for i, (lf, rf, tp) in enumerate(mms):
                dd = (list(deps) + war) if i == 0 else []
                last = p.te(lambda e, lf=lf, rf=rf, tp=tp, st=(i == 0),
                            sp=(i == n - 1), p0=p0, p1=p1, half=half, rname=rname:
                            e.matmul(psap(rname, half, p0, p1), lf(), rf(),
                                     start=st, stop=sp, tile_position=tp),
                            deps=dd, tag=tag)
            r.wrote(last)
            return last

        INIT = []
        for fn in (lambda e: e.memset(ones_col_bf[:], 1.0),
                   lambda e: e.memset(ones_row_f[:], 1.0),
                   lambda e: e.memset(one_col[:], 1.0),
                   lambda e: e.memset(negC_col[:], -CSHIFT),
                   lambda e: e.memset(eps1[:], EPS)):
            INIT.append(p.ve(fn))
        INIT.append(p.dma("wq", lambda e: e.dma_start(out=indb[:], in_=indmat)))
        LB = [(150.0, 87.0 / 300.0, 1.0 / 32.0), (300.0, 87.0 / 600.0, 1.0 / 8.0)]
        for Li in (0, 1):
            Bc, lam, scl_ = LB[Li]
            INIT.append(p.ve(lambda e, Li=Li, v=-LB[Li][1] * LB[Li][0]:
                             e.memset(negLB[Li][:], v)))
        INIT.append(p.ve(lambda e: e.memset(zero_col[:], 0.0)))
        INIT.append(p.ve(lambda e: e.memset(tiny_col[:], 1e-38)))
        for b_ in range(2):
            INIT.append(p.ve(lambda e, b_=b_: e.memset(khaug[b_][64:65, :], 1.0)))
        INIT.append(p.ve(lambda e: e.memset(qaug[64:65, :], -640.0)))
        for dst, src in ((t1b, t1c), (t2b, t2c)):
            INIT.append(p.dma("gq", lambda e, d=dst, s=src: e.dma_start(out=d[:], in_=s)))
        for dst, src in ((r1b, r1T), (r2b, r2T), (fbTb, fbT)):
            INIT.append(p.dma("gq", lambda e, d=dst, s=src: e.dma_start(out=d[:], in_=s)))
        INIT.append(p.dma("gq", lambda e: e.dma_start(out=b1c[:], in_=b1)))
        INIT.append(p.dma("gq", lambda e: e.dma_start(out=b2c[:], in_=b2)))
        INIT.append(p.dma("gq", lambda e: e.dma_start(out=boutc[:], in_=bout)))
        INIT.append(p.dma("gq", lambda e: e.dma_start(
            out=vTb.ap().rearrange("p (t r) -> p t r", t=8),
            in_=vT.rearrange("(t p) r -> p t r", p=128))))
        INIT.append(p.dma("gq", lambda e: e.dma_start(
            out=graw[:], in_=gate.unsqueeze(0))))
        VPPAD = []
        INIT.append(p.ve(lambda e: e.memset(
            vstg.ap().rearrange("p (h z) -> p h z", h=8)[:, :, 64:66], 1.0)))

        L0W_TOKS = []
        for ic in range(8):
            L0W_TOKS.append(p.dma("cq", lambda e, ic=ic: e.dma_start(
                out=w1s[0][:, ic * 4096:(ic + 1) * 4096].rearrange(
                    "p (dt t) -> p dt t", dt=8),
                in_=w1.rearrange("(dt p) i -> p dt i", p=128)[:, :, ic * 512:(ic + 1) * 512]),
                tag=f"w1cast{ic}"))
        for ft in range(8):
            L0W_TOKS.append(p.dma("cq", lambda e, ft=ft: e.dma_start(
                out=w2s[0][:, ft * 4096:(ft + 1) * 4096].rearrange(
                    "p (it f) -> p it f", it=32),
                in_=w2.rearrange("(it p) f -> p it f", p=128)[:, :, ft * 128:(ft + 1) * 128]),
                tag=f"w2cast{ft}"))

        def load_wT_fchunk(whbm, col0, bufname, bi, deps=(), tag=""):
            bb = buf(bufname, bi)
            dst = (w1cb if bufname == "w1c" else w2cb)[bi]
            t = p.dma("gq", lambda e: e.dma_start(
                out=dst.ap().rearrange("p (dt t) -> p dt t", dt=8),
                in_=whbm[:, col0:col0 + 512].rearrange("(dt p) t -> p dt t", p=128)),
                deps=bb.wdeps(deps), tag=tag)
            bb.wrote(t)
            return t

        def load_x_chunk(xsrc, c, tag):
            return p.dma("gq", lambda e: e.dma_start(
                out=xcb[:], in_=xsrc[:, c * 4096:(c + 1) * 4096]),
                deps=buf("xcb").wdeps(), tag=tag)

        fillq = []
        prep1_list = []

        def pump_fill(n):
            while fillq and n > 0:
                prep1_list.extend(fillq.pop(0)())
                n -= 1

        def phase_ln(L, c, xsrc, extra_deps=(), x_res=None, st_extra=()):
            par = c % 2
            bx = buf("xcb")
            if x_res is None:
                t_x = load_x_chunk(xsrc, c, f"L{L}c{c}:x1")
                for d in extra_deps:
                    t_x.deps.append(d); d.need = True
                bx.wrote(t_x)
                xdeps = [t_x]
            else:
                xdeps = list(x_res)
            bq = buf("qc")
            t_cast = bx.read(p.ve(lambda e: e.tensor_copy(out=qc[:], in_=xcb[:]),
                                  deps=bq.wdeps(xdeps)))
            bq.wrote(t_cast)
            bh = buf("hc")
            t_sq = bq.read(p.ve(lambda e: e.tensor_mul(out=hc[:, 0:4096], in0=qc[:], in1=qc[:]),
                                deps=bh.wdeps([t_cast])))
            bh.wrote(t_sq)
            t_s1 = mm_group("A", 0, [
                (lambda: ones_col_bf[:, 0:1],
                 lambda dt=dt: qc[:, dt * CH:(dt + 1) * CH], None) for dt in range(8)],
                deps=[t_cast] + INIT, tag=f"L{L}c{c}:lnsum", p0=0, p1=1)
            bq.read(t_s1)
            t_s2 = mm_group("A", 1, [
                (lambda: ones_col_bf[:, 0:1],
                 lambda dt=dt: hc[:, dt * 512:(dt + 1) * 512], None) for dt in range(8)],
                deps=[t_sq], tag=f"L{L}c{c}:lnsq", p0=0, p1=1)
            bh.read(t_s2)
            bmr, bms = buf("s1row"), buf("s2row")
            brs, bmrr = buf("denrowA"), buf("denrowB")
            t_m = reg("A", 0).read(p.ve(lambda e: e.tensor_scalar_mul(
                out=mrow[:], in0=psA[0:1, 0:512], scalar1=1.0 / D),
                deps=[t_s1] + bmr.wdeps()))
            bmr.wrote(t_m)
            t_mq = reg("A", 1).read(p.ve(lambda e: e.tensor_scalar_mul(
                out=msqrow[:], in0=psA[0:1, 512:1024], scalar1=1.0 / D),
                deps=[t_s2] + bms.wdeps()))
            bms.wrote(t_mq)
            t_v1 = p.ve(lambda e: e.tensor_mul(out=rstdrow[:], in0=mrow[:], in1=mrow[:]),
                        deps=[t_m] + brs.wdeps())
            brs.wrote(t_v1)
            t_v2 = p.ve(lambda e: e.tensor_sub(out=msqrow[:], in0=msqrow[:], in1=rstdrow[:]),
                        deps=[t_v1, t_mq])
            t_l = p.ac(lambda e: e.activation(out=rstdrow[:], in_=msqrow[:], func=AF.Ln,
                                              bias=eps1[:, 0:1]), deps=[t_v2])
            bms.read(t_l)
            t_r = p.ac(lambda e: e.activation(out=rstdrow[:], in_=rstdrow[:], func=AF.Exp,
                                              scale=-0.5), deps=[t_l])
            t_mr = p.ve(lambda e: e.tensor_mul(out=mrrow[:], in0=mrow[:], in1=rstdrow[:]),
                        deps=[t_r, t_m] + bmrr.wdeps())
            bmrr.wrote(t_mr)
            bmr.read(t_mr)
            t_b1 = mm_group("B", 0, [(lambda: ones_row_f[:], lambda: rstdrow[:], None)],
                            deps=[t_r], tag="lnb1")
            brs.read(t_b1)
            t_b2 = mm_group("B", 1, [(lambda: ones_row_f[:], lambda: mrrow[:], None)],
                            deps=[t_mr], tag="lnb2")
            bmrr.read(t_b2)
            bsc = buf("sc")
            tt1 = []
            for dt in range(8):
                dd = [t_b1] + xdeps[-1:] + (bsc.wdeps() if dt == 0 else [])
                tt1.append(reg("B", 0).read(bx.read(p.ve(lambda e, dt=dt: e.tensor_tensor(
                    out=sc[:, dt * CH:(dt + 1) * CH], in0=xcb[:, dt * CH:(dt + 1) * CH],
                    in1=psB[:, 0:512], op=ALU.mult), deps=dd))))
            bsc.wrote(tt1[-1])
            bxn = buf("xnb", par)
            tt2 = []
            for dt in range(8):
                dd = [t_b2, tt1[dt]] + (bxn.wdeps() if dt == 0 else [])
                tt2.append(bsc.read(reg("B", 1).read(p.ve(lambda e, dt=dt: e.tensor_tensor(
                    out=xnb[par][:, dt * CH:(dt + 1) * CH],
                    in0=sc[:, dt * CH:(dt + 1) * CH],
                    in1=psB[:, 512:1024], op=ALU.subtract), deps=dd))))
            bxn.wrote(tt2[-1])
            t_st = p.dma("wq", lambda e: e.dma_start(
                out=xn_scr[:, c * 4096:(c + 1) * 4096], in_=xnb[par][:]),
                deps=tt2 + list(st_extra), tag=f"L{L}c{c}:xnst")
            bxn.read(t_st)
            return t_st, tt2[-1]

        def phase_kv(L, c, xn_st, xn_res=None):
            par = c % 2
            bxn = buf("xnb", par)
            if xn_res is None:
                t_xn = p.dma("wq", lambda e: e.dma_start(
                    out=xnb[par][:], in_=xn_scr[:, c * 4096:(c + 1) * 4096]),
                    deps=bxn.wdeps([xn_st]), tag=f"L{L}c{c}:xn2")
                bxn.wrote(t_xn)
            else:
                t_xn = xn_res
            wk = wkT0 if L == 0 else wqkvT[:, D:2 * D]
            wv = wvT0 if L == 0 else wqkvT[:, 2 * D:3 * D]
            stores = []
            for fc in range(2):
                bi = fc % 2
                t_w = load_wT_fchunk(wk, fc * 512, "w1c", bi, tag=f"L{L}c{c}:wk{fc}")
                for fi in range(4):
                    ft = fc * 4 + fi
                    hlf = ft % 2
                    t_mm = mm_group("A", hlf, [
                        (lambda dt=dt, fi=fi, bi=bi:
                         w1cb[bi][:, dt * 512 + fi * 128:dt * 512 + fi * 128 + 128],
                         lambda dt=dt: xnb[par][:, dt * CH:(dt + 1) * CH], None)
                        for dt in range(8)],
                        deps=[t_w, t_xn], tag=f"L{L}c{c}:kmm{ft}")
                    buf("w1c", bi).read(t_mm)
                    bxn.read(t_mm)
                    sb = ft % 2
                    bstg = buf("stg", sb)
                    if L == 0:
                        t_ev = reg("A", hlf).read(p.ve(
                            lambda e, sb=sb, hlf=hlf: e.tensor_copy(
                                out=stgb[sb][:], in_=psap("A", hlf)),
                            deps=bstg.wdeps([t_mm])))
                    else:
                        bm1, bm2 = buf("m1"), buf("m2")
                        t_m1 = reg("A", hlf).read(p.ve(lambda e, hlf=hlf, c=c: e.tensor_tensor(
                            out=m1t[:], in0=psap("A", hlf),
                            in1=t1b[:, c * CH:(c + 1) * CH], op=ALU.mult),
                            deps=bm1.wdeps([t_mm]) + INIT))
                        bm1.wrote(t_m1)
                        t_m2 = reg("A", hlf).read(p.ve(lambda e, hlf=hlf, c=c: e.tensor_tensor(
                            out=m2t[:], in0=psap("A", hlf),
                            in1=t2b[:, c * CH:(c + 1) * CH], op=ALU.mult),
                            deps=bm2.wdeps([t_mm])))
                        bm2.wrote(t_m2)
                        t_rot = mm_group("B", hlf, [
                            (lambda: r1b[:], lambda: m1t[:], None),
                            (lambda: r2b[:], lambda: m2t[:], None)],
                            deps=[t_m1, t_m2] + INIT, tag=f"L{L}c{c}:krot{ft}")
                        bm1.read(t_rot); bm2.read(t_rot)
                        t_ev = reg("B", hlf).read(p.ve(
                            lambda e, sb=sb, hlf=hlf: e.tensor_copy(
                                out=stgb[sb][:], in_=psap("B", hlf)),
                            deps=bstg.wdeps([t_rot])))
                    bstg.wrote(t_ev)
                    t_kst = p.dma("sq", lambda e, ft=ft, sb=sb, c=c, L=L: e.dma_start(
                        out=KS[L][ft * 128:(ft + 1) * 128, c * CH:(c + 1) * CH],
                        in_=stgb[sb][:]),
                        deps=[t_ev] + buf("kscr", L).wdeps(), tag=f"L{L}c{c}:kst{ft}")
                    bstg.read(t_kst)
                    stores.append(t_kst)
            for fc in range(2):
                bi = fc % 2
                t_w = load_wT_fchunk(wv, fc * 512, "w2c", bi, tag=f"L{L}c{c}:wv{fc}")
                for tt_ in range(4):
                    hlf = tt_ % 2
                    t_mm = mm_group("A", hlf, [
                        (lambda dt=dt, tt_=tt_:
                         xnb[par][:, dt * CH + tt_ * 128:dt * CH + tt_ * 128 + 128],
                         lambda dt=dt, bi=bi: w2cb[bi][:, dt * 512:(dt + 1) * 512], None)
                        for dt in range(8)],
                        deps=[t_w, t_xn], tag=f"L{L}c{c}:vmm{fc}_{tt_}")
                    buf("w2c", bi).read(t_mm)
                    bxn.read(t_mm)
                    bstg = buf("vstg")
                    t_ev = reg("A", hlf).read(p.ve(
                        lambda e, hlf=hlf: e.tensor_copy(
                            out=vstg.ap().rearrange("p (h z) -> p h z", h=8)[:, :, 0:64],
                            in_=psap("A", hlf)),
                        deps=bstg.wdeps([t_mm]) + INIT))
                    bstg.wrote(t_ev)
                    kti = c * 4 + tt_
                    t_vst = p.dma("sq", lambda e, kti=kti, fc=fc, L=L: e.dma_start(
                        out=VPS[L].rearrange("p (pr kt z) -> p pr kt z", pr=NPAIR, kt=NKT)[
                            :, fc * 4:(fc + 1) * 4, kti, :],
                        in_=vstg.ap().rearrange("p (pp z) -> p pp z", pp=4)),
                        deps=[t_ev] + buf("vpscr", L).wdeps(), tag=f"L{L}c{c}:vst{fc}_{tt_}")
                    bstg.read(t_vst)
                    stores.append(t_vst)
            return stores

        def phase_attn(L, c, kv_done, xn_tok):
            par = c % 2
            scale = (1.0 / 32.0) if L == 0 else (1.0 / 8.0)
            wqq = wqT0 if L == 0 else wqkvT[:, 0:D]
            bxn = buf("xnb", par)
            bqc = buf("qc")
            q_toks = []
            for fc in range(2):
                bi = fc % 2
                t_w = load_wT_fchunk(wqq, fc * 512, "w2c", bi, tag=f"L{L}c{c}:wq{fc}")
                for fi in range(4):
                    ft = fc * 4 + fi
                    hlf = ft % 2
                    t_mm = mm_group("A", hlf, [
                        (lambda dt=dt, fi=fi, bi=bi:
                         w2cb[bi][:, dt * 512 + fi * 128:dt * 512 + fi * 128 + 128],
                         lambda dt=dt: xnb[par][:, dt * CH:(dt + 1) * CH], None)
                        for dt in range(8)],
                        deps=[t_w, xn_tok], tag=f"L{L}c{c}:qmm{ft}")
                    buf("w2c", bi).read(t_mm)
                    bxn.read(t_mm)
                    wq_deps = bqc.wdeps() if (ft == 0 and L == 1) else []
                    bqa = buf("qaug")
                    wqa_deps = bqa.wdeps() if ft == 0 else []
                    if L == 0:
                        if f"q{L}c{c}" in tap_aps:
                            t_ev = reg("A", hlf).read(p.ve(
                                lambda e, ft=ft, hlf=hlf: e.tensor_copy(
                                    out=qc[:, ft * 512:(ft + 1) * 512],
                                    in_=psap("A", hlf)),
                                deps=[t_mm] + bqc.wdeps() if ft == 0 else [t_mm]))
                            bqc.wrote(t_ev)
                        else:
                            t_ev = None
                        t_a1 = reg("A", hlf).read(p.ve(lambda e, ft=ft, hlf=hlf: e.tensor_copy(
                            out=qaug[0:64, (2 * ft) * 512:(2 * ft + 1) * 512],
                            in_=psap("A", hlf, 0, 64)), deps=[t_mm] + wqa_deps))
                        t_a2 = reg("A", hlf).read(p.ve(lambda e, ft=ft, hlf=hlf: e.tensor_copy(
                            out=qaug[0:64, (2 * ft + 1) * 512:(2 * ft + 2) * 512],
                            in_=psap("A", hlf, 64, 128)), deps=[t_mm]))
                        bqa.wrote(t_a2)
                        q_toks.append(t_a1); q_toks.append(t_a2)
                    else:
                        bm1, bm2 = buf("m1"), buf("m2")
                        t_m1 = reg("A", hlf).read(p.ve(lambda e, hlf=hlf, c=c: e.tensor_tensor(
                            out=m1t[:], in0=psap("A", hlf),
                            in1=t1b[:, c * CH:(c + 1) * CH], op=ALU.mult),
                            deps=bm1.wdeps([t_mm]) + INIT))
                        bm1.wrote(t_m1)
                        t_m2 = reg("A", hlf).read(p.ve(lambda e, hlf=hlf, c=c: e.tensor_tensor(
                            out=m2t[:], in0=psap("A", hlf),
                            in1=t2b[:, c * CH:(c + 1) * CH], op=ALU.mult),
                            deps=bm2.wdeps([t_mm])))
                        bm2.wrote(t_m2)
                        t_rot = mm_group("B", hlf, [
                            (lambda: r1b[:], lambda: m1t[:], None),
                            (lambda: r2b[:], lambda: m2t[:], None)],
                            deps=[t_m1, t_m2], tag=f"L{L}c{c}:qrot{ft}")
                        bm1.read(t_rot); bm2.read(t_rot)
                        t_ev = reg("B", hlf).read(p.ve(lambda e, ft=ft, hlf=hlf: e.tensor_copy(
                            out=qc[:, ft * 512:(ft + 1) * 512], in_=psap("B", hlf)),
                            deps=[t_rot] + wq_deps))
                        t_a1 = reg("B", hlf).read(p.ve(lambda e, ft=ft, hlf=hlf: e.tensor_copy(
                            out=qaug[0:64, (2 * ft) * 512:(2 * ft + 1) * 512],
                            in_=psap("B", hlf, 0, 64)), deps=[t_rot] + wqa_deps))
                        t_a2 = reg("B", hlf).read(p.ve(lambda e, ft=ft, hlf=hlf: e.tensor_copy(
                            out=qaug[0:64, (2 * ft + 1) * 512:(2 * ft + 2) * 512],
                            in_=psap("B", hlf, 64, 128)), deps=[t_rot]))
                        bqa.wrote(t_a2)
                        q_toks.append(t_a1); q_toks.append(t_a2)
                    if t_ev is not None:
                        q_toks.append(t_ev)
            if L == 1 or f"q{L}c{c}" in tap_aps:
                bqc.wrote(q_toks[-1])
            if f"q{L}c{c}" in tap_aps:
                t_tapq = p.dma("gq", lambda e, L=L, c=c: e.dma_start(
                    out=tap_aps[f"q{L}c{c}"], in_=qc[:]), deps=list(q_toks), tag="tapq")
                bqc.read(t_tapq)
                q_toks = q_toks + [t_tapq]
            Bc, lam, _scl = LB[L]
            exp_toks = []
            qr_dmas = []
            bqrs = buf("qrow_scr")
            war_qrs = bqrs.wdeps() if L == 1 else []
            for pr in (range(NPAIR) if L == 1 else ()):
                pump_fill(3)
                pair_exp_toks = []
                kb = pr % 2
                bkhp = buf("khp", kb)
                t_khp = p.dma("wq", lambda e, pr=pr, kb=kb, L=L: e.dma_start(
                    out=khp[kb], in_=KS[L][pr * 128:(pr + 1) * 128, :]),
                    deps=bkhp.wdeps(kv_done) + buf("hc").peek(), tag=f"khpl{pr}")
                bkhp.wrote(t_khp)
                buf("kscr", L).read(t_khp)
                for qsub in range(4):
                    for kc in range(4):
                        sn = ("A", "B", "OA", "OB")[kc]
                        sten = PS[sn]
                        rs0, rs1 = reg(sn, 0), reg(sn, 1)
                        war_s = rs0.wdeps() + rs1.wdeps()
                        t_la = p.te(lambda e, kb=kb, kc=kc, pr=pr, qsub=qsub, sten=sten:
                                    e.matmul(sten[:, 0:512],
                                             qc[0:64, pr * 512 + qsub * 128:pr * 512 + qsub * 128 + 128],
                                             khp[kb][0:64, kc * 512:(kc + 1) * 512],
                                             start=True, stop=True, tile_position=(0, 0)),
                                    deps=[t_khp] + q_toks + war_s, tag=f"lA{pr}_{qsub}_{kc}")
                        t_lb = p.te(lambda e, kb=kb, kc=kc, pr=pr, qsub=qsub, sten=sten:
                                    e.matmul(sten[:, 512:1024],
                                             qc[64:128, pr * 512 + qsub * 128:pr * 512 + qsub * 128 + 128],
                                             khp[kb][64:128, kc * 512:(kc + 1) * 512],
                                             start=True, stop=True, tile_position=(64, 0)),
                                    deps=[], tag=f"lB{pr}_{qsub}_{kc}")
                        rs0.wrote(t_la); rs1.wrote(t_lb)
                        bkhp.read(t_la); bkhp.read(t_lb)
                        buf("hc").read(t_la); buf("hc").read(t_lb)
                        bqc.read(t_la); bqc.read(t_lb)
                        colA = (2 * pr) * 16 + qsub * 4 + kc
                        colB = (2 * pr + 1) * 16 + qsub * 4 + kc
                        if pr % 2 == 0:
                            t_ea = rs0.read(p.ac(lambda e, sten=sten, colA=colA, L=L:
                                                 e.activation(out=m1t[:], in_=sten[:, 0:512],
                                                              func=AF.Exp, bias=negLB[L][:, 0:1],
                                                              scale=lam * _scl,
                                                              accum_out=macc256[:, colA:colA + 1]),
                                                 deps=[t_la] + INIT, tag=f"eA{pr}_{qsub}_{kc}"))
                            t_eb2 = rs1.read(p.ac(lambda e, sten=sten, colB=colB, L=L:
                                                  e.activation(out=m1t[:], in_=sten[:, 512:1024],
                                                               func=AF.Exp, bias=negLB[L][:, 0:1],
                                                               scale=lam * _scl,
                                                               accum_out=macc256[:, colB:colB + 1]),
                                                  deps=[t_lb], tag=f"eB{pr}_{qsub}_{kc}"))
                        else:
                            t_ea = rs0.read(p.ve(lambda e, sten=sten, colA=colA:
                                                 e.reduce_max(out=macc256[:, colA:colA + 1],
                                                              in_=sten[:, 0:512],
                                                              axis=mybir.AxisListType.X),
                                                 deps=[t_la], tag=f"mA{pr}_{qsub}_{kc}"))
                            t_eb2 = rs1.read(p.ve(lambda e, sten=sten, colB=colB:
                                                  e.reduce_max(out=macc256[:, colB:colB + 1],
                                                               in_=sten[:, 512:1024],
                                                               axis=mybir.AxisListType.X),
                                                  deps=[t_lb], tag=f"mB{pr}_{qsub}_{kc}"))
                        exp_toks += [t_ea, t_eb2]
                        pair_exp_toks += [t_ea, t_eb2]
                g0 = 8 * pr
                mvp = macc256.ap().rearrange("p (g k) -> p k g", k=4)[:, :, g0:g0 + 8]
                mcp = mcols[:, g0:g0 + 8]
                cop = ALU.add if pr % 2 == 0 else ALU.max
                t_r1 = p.ve(lambda e, mvp=mvp, mcp=mcp, op=cop: e.tensor_tensor(
                    out=mcp, in0=mvp[:, 0, :], in1=mvp[:, 1, :], op=op),
                    deps=pair_exp_toks)
                t_r2 = p.ve(lambda e, mvp=mvp, mcp=mcp, op=cop: e.tensor_tensor(
                    out=mcp, in0=mcp, in1=mvp[:, 2, :], op=op), deps=[t_r1])
                combine_op = ALU.add if pr % 2 == 0 else ALU.max
                t_r3 = p.ve(lambda e, mvp=mvp, mcp=mcp, op=combine_op: e.tensor_tensor(
                    out=mcp, in0=mcp, in1=mvp[:, 3, :], op=op), deps=[t_r2])
                if pr % 2 == 0:
                    t_lnm = p.ac(lambda e, mcp=mcp: e.activation(
                        out=mcp, in_=mcp, func=AF.Ln, bias=tiny_col[:, 0:1]), deps=[t_r3])
                    t_aff = p.ve(lambda e, mcp=mcp, L=L: e.tensor_scalar(
                        out=mcp, in0=mcp, scalar1=-1.0 / (lam * _scl),
                        scalar2=-Bc / _scl, op0=ALU.mult, op1=ALU.add), deps=[t_lnm])
                else:
                    t_aff = p.ve(lambda e, mcp=mcp: e.tensor_scalar_mul(
                        out=mcp, in0=mcp, scalar1=-1.0), deps=[t_r3])
                for hh in (2 * pr, 2 * pr + 1):
                    for qsub in range(4):
                        qr_dmas.append(p.dma("sq", lambda e, hh=hh, qsub=qsub: e.dma_start(
                            out=qrow_scr[hh:hh + 1, qsub * 128:(qsub + 1) * 128],
                            in_=mcols[:, hh * 4 + qsub:hh * 4 + qsub + 1]),
                            deps=[t_aff] + (war_qrs if (hh == 0 and qsub == 0) else []),
                            tag=f"qr{hh}_{qsub}"))
            t_qrow = None
            if L == 1:
                bqa = buf("qaug")
                t_qrow = p.dma("gq", lambda e: e.dma_start(
                    out=qaug[64:65, :],
                    in_=qrow_scr.rearrange("h q -> (h q)").unsqueeze(0)),
                    deps=qr_dmas, tag="qrowl")
                bqa.read(t_qrow)
                bqrs.read(t_qrow)
            bqa = buf("qaug")
            den_dmas = []
            ev_toks = []
            for pr in range(NPAIR):
                pump_fill(4)
                kb = pr % 2
                bka, bkb_ = buf("kha", 0), buf("kha", 1)
                t_ka = p.dma("wq", lambda e, pr=pr, L=L: e.dma_start(
                    out=khaug[0][0:64, :],
                    in_=KS[L][(2 * pr) * 64:(2 * pr) * 64 + 64, :]),
                    deps=bka.wdeps(kv_done), tag=f"kal{pr}")
                bka.wrote(t_ka)
                buf("kscr", L).read(t_ka)
                t_kb2 = p.dma("wq", lambda e, pr=pr, L=L: e.dma_start(
                    out=khaug[1][0:64, :],
                    in_=KS[L][(2 * pr + 1) * 64:(2 * pr + 1) * 64 + 64, :]),
                    deps=bkb_.wdeps(kv_done), tag=f"kbl{pr}")
                bkb_.wrote(t_kb2)
                buf("kscr", L).read(t_kb2)
                bvh = buf("vh", kb)
                t_vh = p.dma("wq", lambda e, pr=pr, kb=kb, L=L: e.dma_start(
                    out=vhb[kb][:],
                    in_=VPS[L][:, pr * NKT * 132:(pr + 1) * NKT * 132]),
                    deps=bvh.wdeps(kv_done), tag=f"vhl{pr}")
                bvh.wrote(t_vh)
                buf("vpscr", L).read(t_vh)
                on = "OA" if pr % 2 == 0 else "OB"
                oten = PS[on]
                ro0, ro1 = reg(on, 0), reg(on, 1)
                war_o = ro0.wdeps() + ro1.wdeps()
                pv_last = None
                pend = None

                def emit_pv(kt, eb, t_e):
                    nonlocal pv_last
                    t_pa = p.te(lambda e, kb=kb, kt=kt, eb=eb, oten=oten: e.matmul(
                        oten[0:65, 0:512], vhb[kb][:, kt * 132:kt * 132 + 65],
                        eabb[eb][:, 0:512], start=(kt == 0), stop=(kt == NKT - 1)),
                        deps=[t_e, t_vh] + (war_o if kt == 0 else []), tag=f"pA{pr}_{kt}")
                    t_pb = p.te(lambda e, kb=kb, kt=kt, eb=eb, oten=oten: e.matmul(
                        oten[0:65, 512:1024], vhb[kb][:, kt * 132 + 66:kt * 132 + 131],
                        eabb[eb][:, 512:1024], start=(kt == 0), stop=(kt == NKT - 1)),
                        deps=[], tag=f"pB{pr}_{kt}")
                    buf("eab", eb).read(t_pa); buf("eab", eb).read(t_pb)
                    bvh.read(t_pa); bvh.read(t_pb)
                    pv_last = t_pb

                for kt in range(NKT):
                    sn = "A" if kt % 2 == 0 else "B"
                    sten = PS[sn]
                    rs0, rs1 = reg(sn, 0), reg(sn, 1)
                    war_s = rs0.wdeps() + rs1.wdeps()
                    t_sa = p.te(lambda e, kt=kt, pr=pr, sten=sten: e.matmul(
                        sten[:, 0:512], khaug[0][0:65, kt * 128:(kt + 1) * 128],
                        qaug[0:65, (2 * pr) * 512:(2 * pr + 1) * 512],
                        start=True, stop=True),
                        deps=[t_ka] + ([t_qrow] if t_qrow else []) + q_toks + war_s,
                        tag=f"sA{pr}_{kt}")
                    t_sb = p.te(lambda e, kt=kt, pr=pr, sten=sten: e.matmul(
                        sten[:, 512:1024], khaug[1][0:65, kt * 128:(kt + 1) * 128],
                        qaug[0:65, (2 * pr + 1) * 512:(2 * pr + 2) * 512],
                        start=True, stop=True),
                        deps=[t_kb2], tag=f"sB{pr}_{kt}")
                    rs0.wrote(t_sa); rs1.wrote(t_sb)
                    bka.read(t_sa); bkb_.read(t_sb)
                    bqa.read(t_sa); bqa.read(t_sb)
                    eb = kt % 2
                    beab = buf("eab", eb)
                    t_e = p.ac(lambda e, eb=eb, sten=sten: e.activation(
                        out=eabb[eb][:], in_=sten[:], func=AF.Exp,
                        bias=zero_col[:, 0:1], scale=scale),
                        deps=beab.wdeps([t_sa, t_sb]) + INIT, tag=f"exp{pr}_{kt}")
                    beab.wrote(t_e)
                    rs0.read(t_e); rs1.read(t_e)
                    if pend is not None:
                        emit_pv(*pend)
                    pend = (kt, eb, t_e)
                emit_pv(*pend)
                ro0.wrote(pv_last); ro1.wrote(pv_last)
                bdp = buf("denrowA"); bdp2 = buf("denrowB")
                t_dc = p.ve(lambda e, oten=oten: e.tensor_copy(
                    out=rstdrow[:], in_=oten[64:65, 0:512]),
                    deps=bdp.wdeps([pv_last]))
                t_dc2 = p.ve(lambda e, oten=oten: e.tensor_copy(
                    out=mrrow[:], in_=oten[64:65, 512:1024]),
                    deps=bdp2.wdeps([pv_last]))
                bdp.wrote(t_dc); bdp2.wrote(t_dc2)
                ro0.read(t_dc); ro1.read(t_dc)
                ro0.read(t_dc2); ro1.read(t_dc2)
                bdscr = buf("den_scr")
                t_dd = p.dma("sq", lambda e, pr=pr: e.dma_start(
                    out=den_scr[2 * pr:2 * pr + 1, :], in_=rstdrow[:]),
                    deps=[t_dc] + (bdscr.wdeps() if pr == 0 else []),
                    tag=f"dendA{pr}")
                t_dd2 = p.dma("sq", lambda e, pr=pr: e.dma_start(
                    out=den_scr[2 * pr + 1:2 * pr + 2, :], in_=mrrow[:]),
                    deps=[t_dc2], tag=f"dendB{pr}")
                bdp.read(t_dd); bdp2.read(t_dd2)
                den_dmas.append(t_dd); den_dmas.append(t_dd2)
                bsc = buf("sc")
                t_e1 = p.ve(lambda e, pr=pr, oten=oten: e.tensor_copy(
                    out=sc[0:64, pr * 512:(pr + 1) * 512], in_=oten[0:64, 0:512]),
                    deps=[pv_last] + (bsc.wdeps() if pr == 0 else []))
                t_e2 = p.ve(lambda e, pr=pr, oten=oten: e.tensor_copy(
                    out=sc[64:128, pr * 512:(pr + 1) * 512], in_=oten[0:64, 512:1024]),
                    deps=[pv_last])
                ro0.read(t_e1); ro0.read(t_e2)
                ro1.read(t_e1); ro1.read(t_e2)
                bsc.wrote(t_e2)
                ev_toks += [t_e1, t_e2]
            bd16 = buf("den16")
            t_d16 = p.dma("sq", lambda e: e.dma_start(out=den16[:], in_=den_scr),
                          deps=bd16.wdeps(den_dmas), tag="den16")
            buf("den_scr").read(t_d16)
            bd16.wrote(t_d16)
            tap_d = []
            if f"den{L}c{c}" in tap_aps:
                t_tapd = p.dma("gq", lambda e, L=L, c=c: e.dma_start(
                    out=tap_aps[f"den{L}c{c}"], in_=den_scr), deps=[t_d16], tag="tapden")
                tap_d = [t_tapd]
            t_ln = bd16.read(p.ac(lambda e: e.activation(
                out=den16[:], in_=den16[:], func=AF.Ln), deps=[t_d16] + tap_d))
            t_rd = bd16.read(p.ac(lambda e: e.activation(
                out=den16[:], in_=den16[:], func=AF.Exp, scale=-1.0), deps=[t_ln]))
            norm_toks = []
            for ft in range(8):
                hlf = ft % 2
                t_dv = mm_group("A", hlf, [
                    (lambda ft=ft: indb[:, ft * 128:(ft + 1) * 128],
                     lambda: den16[:], None)], deps=[t_rd], tag=f"divmm{ft}")
                if L == 0:
                    t_n = reg("A", hlf).read(p.ve(lambda e, ft=ft, hlf=hlf: e.tensor_tensor(
                        out=sc[:, ft * 512:(ft + 1) * 512],
                        in0=sc[:, ft * 512:(ft + 1) * 512],
                        in1=psap("A", hlf), op=ALU.mult),
                        deps=[t_dv, ev_toks[2 * ft], ev_toks[2 * ft + 1]]))
                else:
                    bhc = buf("hc")
                    t_n = reg("A", hlf).read(bhc.read(p.ve(
                        lambda e, ft=ft, hlf=hlf: e.tensor_tensor(
                            out=hc[:, ft * 512:(ft + 1) * 512],
                            in0=sc[:, ft * 512:(ft + 1) * 512],
                            in1=psap("A", hlf), op=ALU.mult),
                        deps=([t_dv, ev_toks[2 * ft], ev_toks[2 * ft + 1]] +
                              (bhc.wdeps() if ft == 0 else [])))))
                norm_toks.append(t_n)
            if L == 1:
                buf("hc").wrote(norm_toks[-1])
                proj_toks = []
                for fc in range(2):
                    bi = fc % 2
                    t_w = load_wT_fchunk(woutT, fc * 512, "w1c", bi, tag=f"wout{fc}")
                    for fi in range(4):
                        ft = fc * 4 + fi
                        hlf = ft % 2
                        t_mm = mm_group("B", hlf, [
                            (lambda dt=dt, fi=fi, bi=bi:
                             w1cb[bi][:, dt * 512 + fi * 128:dt * 512 + fi * 128 + 128],
                             lambda dt=dt: hc[:, dt * 512:(dt + 1) * 512], None)
                            for dt in range(8)],
                            deps=[t_w] + norm_toks, tag=f"pjmm{ft}")
                        buf("w1c", bi).read(t_mm)
                        buf("hc").read(t_mm)
                        t_pe = reg("B", hlf).read(p.ve(lambda e, ft=ft, hlf=hlf: e.tensor_scalar(
                            out=sc[:, ft * 512:(ft + 1) * 512], in0=psap("B", hlf),
                            scalar1=boutc[:, ft:ft + 1], scalar2=None, op0=ALU.add),
                            deps=[t_mm, norm_toks[ft]]))
                        proj_toks.append(t_pe)
                norm_toks = proj_toks
            if f"sc{L}c{c}" in tap_aps:
                t_tap = p.dma("gq", lambda e, L=L, c=c: e.dma_start(
                    out=tap_aps[f"sc{L}c{c}"], in_=sc[:]), deps=list(norm_toks),
                    tag="tapsc")
                buf("sc").read(t_tap)
                norm_toks = norm_toks + [t_tap]
            return norm_toks

        def phase_ffn(L, c, s_toks, xn_tok, xsrc, xdst, wprep_toks, x_extra=()):
            par = c % 2
            bxn = buf("xnb", par)
            bx = buf("xcb")
            t_x = load_x_chunk(xsrc, c, f"L{L}c{c}:x2")
            for d in x_extra:
                t_x.deps.append(d); d.need = True
            bx.wrote(t_x)
            lr_tok = None
            if L == 1:
                t_z1 = mm_group("OA", 0, [
                    (lambda dt=dt: vTb[:, dt * RANK:(dt + 1) * RANK],
                     lambda dt=dt: xnb[par][:, dt * CH:(dt + 1) * CH], None)
                    for dt in range(8)],
                    deps=[xn_tok] + INIT, tag="z1mm", p0=0, p1=RANK)
                bxn.read(t_z1)
                bz1 = buf("z1b")
                t_z1e = reg("OA", 0).read(p.ve(lambda e: e.tensor_copy(
                    out=z1b[:], in_=poA[0:RANK, 0:512]), deps=bz1.wdeps([t_z1])))
                bz1.wrote(t_z1e)
                t_z2 = mm_group("OB", 0, [(lambda: fbTb[:], lambda: z1b[:], None)],
                                deps=[t_z1e] + wprep_toks, tag="z2mm", p0=0, p1=RANK)
                bz1.read(t_z2)
                bz2 = buf("z2g")
                t_z2e = reg("OB", 0).read(p.ve(lambda e: e.tensor_scalar_mul(
                    out=z2g[:], in0=poB[0:RANK, 0:512], scalar1=gcol[:, 0:1]),
                    deps=bz2.wdeps([t_z2])))
                bz2.wrote(t_z2e)
                lr_tok = t_z2e
            h_toks = []
            bhc = buf("hc")
            war_h = bhc.wdeps()
            for ic in range(8):
                if L == 0:
                    pump_fill(3)
                bi = ic % 2
                bw = buf("w1c", bi)
                t_w = p.dma("wq", lambda e, ic=ic, bi=bi, L=L: e.dma_start(
                    out=w1cb[bi][:], in_=w1s[L][:, ic * 4096:(ic + 1) * 4096]),
                    deps=bw.wdeps(wprep_toks), tag=f"w1l{ic}")
                bw.wrote(t_w)
                t_ul = None
                if L == 1:
                    but = buf("uTsb", ic % 2)
                    t_ul = p.dma("wq", lambda e, ic=ic: e.dma_start(
                        out=uTsb[ic % 2][:], in_=uts_scr[:, ic * 512:(ic + 1) * 512]),
                        deps=but.wdeps(wprep_toks), tag=f"utl{ic}")
                    but.wrote(t_ul)
                for it in range(4):
                    ii = ic * 4 + it
                    hlf = it % 2
                    mms = [(lambda dt=dt, it=it, bi=bi:
                            w1cb[bi][:, dt * 512 + it * 128:dt * 512 + it * 128 + 128],
                            lambda dt=dt: xnb[par][:, dt * CH:(dt + 1) * CH], None)
                           for dt in range(8)]
                    if L == 1:
                        mms.append((lambda it=it, bu=ic % 2: uTsb[bu][:, it * 128:(it + 1) * 128],
                                    lambda: z2g[:], None))
                    t_mm = mm_group("A", hlf, mms,
                                    deps=[t_w, xn_tok] + ([lr_tok, t_ul] if lr_tok else []),
                                    tag=f"hmm{ii}")
                    if L == 1:
                        buf("uTsb", ic % 2).read(t_mm)
                    bw.read(t_mm)
                    bxn.read(t_mm)
                    if L == 1:
                        buf("z2g").read(t_mm)
                    if L == 0:
                        t_h = reg("A", hlf).read(p.ve(lambda e, ii=ii, hlf=hlf: e.tensor_scalar(
                            out=hc[:, ii * 512:(ii + 1) * 512], in0=psap("A", hlf),
                            scalar1=b1c[:, ii:ii + 1], scalar2=0.0,
                            op0=ALU.add, op1=ALU.max),
                            deps=[t_mm] + INIT + (war_h if ii == 0 else [])))
                    else:
                        bzt = buf("zt")
                        t_t = reg("A", hlf).read(p.ve(lambda e, hlf=hlf: e.tensor_scalar_max(
                            out=zt[:], in0=psap("A", hlf), scalar1=0.0),
                            deps=bzt.wdeps([t_mm]) + (war_h if ii == 0 else [])))
                        bzt.wrote(t_t)
                        t_h = bzt.read(p.ve(lambda e, ii=ii: e.scalar_tensor_tensor(
                            out=hc[:, ii * 512:(ii + 1) * 512], in0=zt[:],
                            scalar=sclf_sq[:, ii:ii + 1], in1=zt[:],
                            op0=ALU.mult, op1=ALU.mult), deps=[t_t] + wprep_toks))
                    h_toks.append(t_h)
            bhc.wrote(h_toks[-1])
            comb_toks = []
            for ft in range(8):
                bi = ft % 2
                bw = buf("w2c", bi)
                t_w = p.dma("wq", lambda e, ft=ft, bi=bi, L=L: e.dma_start(
                    out=w2cb[bi][:], in_=w2s[L][:, ft * 4096:(ft + 1) * 4096]),
                    deps=bw.wdeps(wprep_toks), tag=f"w2l{ft}")
                bw.wrote(t_w)
                hlf = ft % 2
                t_mm = mm_group("B", hlf, [
                    (lambda it=it, bi=bi: w2cb[bi][:, it * 128:(it + 1) * 128],
                     lambda it=it: hc[:, it * 512:(it + 1) * 512], None)
                    for it in range(32)],
                    deps=[t_w] + h_toks, tag=f"omm{ft}")
                bw.read(t_mm)
                bhc.read(t_mm)
                bz = buf("ztf")
                if L == 0:
                    t_z = reg("B", hlf).read(p.ve(lambda e, ft=ft, hlf=hlf: e.tensor_scalar_add(
                        out=zt[:], in0=psap("B", hlf), scalar1=b2c[:, ft:ft + 1]),
                        deps=bz.wdeps([t_mm]) + INIT))
                else:
                    t_z = reg("B", hlf).read(p.ve(lambda e, ft=ft, hlf=hlf: e.tensor_scalar_mul(
                        out=zt[:], in0=psap("B", hlf), scalar1=sclp[:, ft:ft + 1]),
                        deps=bz.wdeps([t_mm]) + wprep_toks))
                bz.wrote(t_z)
                bu = buf("ut")
                t_u = bz.read(p.ac(lambda e: e.activation(
                    out=ut[:], in_=zt[:], func=AF.Exp, scale=-1.0),
                    deps=bu.wdeps([t_z])))
                bu.wrote(t_u)
                bt_ = buf("ttt")
                t_t = bu.read(p.ac(lambda e: e.activation(
                    out=ttt[:], in_=ut[:], func=AF.Ln, bias=one_col[:, 0:1]),
                    deps=bt_.wdeps([t_u]) + INIT))
                bt_.wrote(t_t)
                t_w2 = bt_.read(p.ac(lambda e: e.activation(
                    out=ttt[:], in_=ttt[:], func=AF.Exp, scale=-1.0), deps=[t_t]))
                bt_.wrote(t_w2)
                bu2 = buf("ut")
                t_g = bz.read(bt_.read(p.ve(lambda e: e.tensor_mul(
                    out=ut[:], in0=zt[:], in1=ttt[:]), deps=bu2.wdeps([t_w2, t_z]))))
                bu2.wrote(t_g)
                t_sg = buf("sc").read(bu2.read(p.po(lambda e, ft=ft: e.tensor_mul(
                    out=zt[:], in0=sc[:, ft * 512:(ft + 1) * 512], in1=ut[:]),
                    deps=bz.wdeps([t_g] + s_toks))))
                bz.wrote(t_sg)
                t_cx = bz.read(bx.read(p.ve(lambda e, ft=ft: e.tensor_add(
                    out=xcb[:, ft * 512:(ft + 1) * 512],
                    in0=xcb[:, ft * 512:(ft + 1) * 512], in1=zt[:]),
                    deps=[t_sg, t_x])))
                comb_toks.append(t_cx)
            t_xst = p.dma("gq", lambda e, c=c: e.dma_start(
                out=xdst[:, c * 4096:(c + 1) * 4096], in_=xcb[:]),
                deps=comb_toks, tag=f"L{L}c{c}:xst")
            bx.read(t_xst)
            return t_xst, comb_toks

        def make_l1_prep():
            st = {}

            SROT = [(pwb[0], ("pw", 0)), (pwb[1], ("pw", 1)),
                    (rb512, ("rb512", 0)), (rb5122, ("rb512b", 0))]
            srot_i = [0]

            def snxt():
                dst, key = SROT[srot_i[0] % 4]
                srot_i[0] += 1
                return dst, buf(*key)

            def job_scales():
                st["red"] = []
                st["red2"] = []
                st["rowd"] = []
                st["rowd2"] = []
                units = []

                def u_wfc(rt):
                    def go():
                        ps = []
                        for h in range(2):
                            dst, bb = snxt()
                            t_l = p.dma("gq", lambda e, rt=rt, h=h, dst=dst:
                                        e.dma_start(out=dst[:],
                                                    in_=wfc[rt * 128:(rt + 1) * 128,
                                                            h * 512:(h + 1) * 512]),
                                        deps=bb.wdeps(), tag=f"scf{rt}_{h}")
                            bb.wrote(t_l)
                            ps.append((dst, bb, t_l))
                        (d0, b0, l0), (d1, b1, l1) = ps
                        t_r0 = b0.read(p.ve(lambda e, rt=rt, d0=d0: e.tensor_reduce(
                            out=mcols[:, rt % 4:rt % 4 + 1], in_=d0[:],
                            axis=mybir.AxisListType.X,
                            op=ALU.add, apply_absolute_value=True), deps=[l0]))
                        t_r1 = b1.read(p.ve(lambda e, rt=rt, d1=d1: e.tensor_reduce(
                            out=rcpf[:, rt:rt + 1], in_=d1[:],
                            axis=mybir.AxisListType.X,
                            op=ALU.add, apply_absolute_value=True), deps=[l1]))
                        t_ad = p.ve(lambda e, rt=rt: e.tensor_tensor(
                            out=rcpf[:, rt:rt + 1], in0=rcpf[:, rt:rt + 1],
                            in1=mcols[:, rt % 4:rt % 4 + 1], op=ALU.add),
                            deps=[t_r0, t_r1])
                        st["red"].append(t_ad)
                        return []
                    return go

                def u_wfc_fin():
                    def go():
                        t_s = p.ve(lambda e: e.tensor_scalar(
                            out=sclf_sq[:], in0=rcpf[:], scalar1=1.0 / D,
                            scalar2=1e-5, op0=ALU.mult, op1=ALU.max),
                            deps=list(st["red"]))
                        t_rc = p.ve(lambda e: e.reciprocal(
                            out=rcpf[:], in_=sclf_sq[:]), deps=[t_s])
                        t_sq = p.ve(lambda e: e.tensor_mul(
                            out=sclf_sq[:], in0=sclf_sq[:], in1=sclf_sq[:]),
                            deps=[t_rc])
                        st["t_rc"] = t_rc
                        return [t_sq]
                    return go

                def u_rowd(k):
                    def go():
                        for j in range(8 * k, 8 * k + 8):
                            st["rowd"].append(p.dma("sq", lambda e, j=j: e.dma_start(
                                out=s1r_scr[0:1, j * 128:(j + 1) * 128],
                                in_=rcpf[:, j:j + 1]), deps=[st["t_rc"]],
                                tag=f"s1r{j}"))
                        return []
                    return go

                def u_wpj(rt2, half):
                    def go():
                        rtoks = []
                        for j in range(4 * half, 4 * half + 4):
                            dst, bb = snxt()
                            t_l = p.dma("gq", lambda e, rt2=rt2, j=j, dst=dst:
                                        e.dma_start(out=dst[:],
                                                    in_=wproj[rt2 * 128:(rt2 + 1) * 128,
                                                              j * 512:(j + 1) * 512]),
                                        deps=bb.wdeps(), tag=f"scp{rt2}_{j}")
                            bb.wrote(t_l)
                            t_r = bb.read(p.ve(lambda e, j=j, dst=dst: e.tensor_reduce(
                                out=mcols[:, 8 + j:9 + j], in_=dst[:],
                                axis=mybir.AxisListType.X,
                                op=ALU.add, apply_absolute_value=True), deps=[t_l]))
                            rtoks.append(t_r)
                        if half == 1:
                            t_f = p.ve(lambda e, rt2=rt2: e.tensor_reduce(
                                out=sclp[:, rt2:rt2 + 1], in_=mcols[:, 8:16],
                                axis=mybir.AxisListType.X, op=ALU.add),
                                deps=rtoks + st.pop("wpjA", []))
                            st["red2"].append(t_f)
                        else:
                            st["wpjA"] = rtoks
                        return []
                    return go

                def u_wpj_fin():
                    def go():
                        t_c2 = p.ve(lambda e: e.tensor_scalar(
                            out=sclp[:], in0=sclp[:], scalar1=1.0 / INTER,
                            scalar2=1e-5, op0=ALU.mult, op1=ALU.max),
                            deps=list(st["red2"]))
                        t_rc2 = p.ve(lambda e: e.reciprocal(
                            out=rcpp[:], in_=sclp[:]), deps=[t_c2])
                        for j in range(8):
                            st["rowd2"].append(p.dma("sq", lambda e, j=j: e.dma_start(
                                out=s2r_scr[0:1, j * 128:(j + 1) * 128],
                                in_=rcpp[:, j:j + 1]), deps=[t_rc2], tag=f"s2r{j}"))
                        return [t_rc2]
                    return go

                def u_gate():
                    def go():
                        rg = reg("OB", 0)
                        war = rg.wdeps()
                        t_g = p.te(lambda e: e.matmul(
                            poB[0:RANK, 0:1], ones_row_f[0:1, 0:RANK],
                            graw[:], start=True, stop=True),
                            deps=INIT + war, tag="gatebc")
                        rg.wrote(t_g)
                        t_ge = rg.read(p.ve(lambda e: e.tensor_copy(
                            out=gcol[:], in_=poB[0:RANK, 0:1]), deps=[t_g]))
                        return [t_ge]
                    return go

                for rt in range(32):
                    units.append(u_wfc(rt))
                units.append(u_wfc_fin())
                for k in range(4):
                    units.append(u_rowd(k))
                for rt2 in range(8):
                    units.append(u_wpj(rt2, 0))
                    units.append(u_wpj(rt2, 1))
                units.append(u_wpj_fin())
                units.append(u_gate())
                return units

            def job_tern(si):
                bs1 = buf("s1row")
                bs2 = buf("s2row")
                cell = {}
                units = []

                def u_bc1(icc):
                    def go():
                        t_row1 = p.dma("sq", lambda e, icc=icc: e.dma_start(
                            out=s1row[:], in_=s1r_scr[0:1, icc * 512:(icc + 1) * 512]),
                            deps=bs1.wdeps(st["rowd"]), tag=f"s1row{icc}")
                        bs1.wrote(t_row1)
                        t_bc = mm_group("OB", 1, [
                            (lambda: ones_row_f[:], lambda: s1row[:], None)],
                            deps=[t_row1], tag=f"bc1_{icc}")
                        bs1.read(t_bc)
                        brb = buf("rb512")
                        t_be = reg("OB", 1).read(p.ve(lambda e: e.tensor_copy(
                            out=rb512[:], in_=poB[:, 512:1024]),
                            deps=brb.wdeps([t_bc])))
                        brb.wrote(t_be)
                        cell["be1"] = t_be
                        return []
                    return go

                def u_uts(icc):
                    def go():
                        brb = buf("rb512")
                        but = buf("uTf")
                        t_ul = p.dma("gq", lambda e, icc=icc: e.dma_start(
                            out=uTf[:], in_=uT[:, icc * 512:(icc + 1) * 512]),
                            deps=but.wdeps(), tag=f"uTl{icc}")
                        but.wrote(t_ul)
                        bub = buf("uTsb", icc % 2)
                        t_u = but.read(brb.read(p.ve(lambda e, icc=icc: e.tensor_tensor(
                            out=uTsb[icc % 2][:], in0=uTf[:],
                            in1=rb512[0:RANK, :], op=ALU.mult),
                            deps=bub.wdeps([cell["be1"], t_ul]))))
                        bub.wrote(t_u)
                        t_us = p.dma("gq", lambda e, icc=icc: e.dma_start(
                            out=uts_scr[:, icc * 512:(icc + 1) * 512],
                            in_=uTsb[icc % 2][:]),
                            deps=[t_u], tag=f"utst{icc}")
                        bub.read(t_us)
                        return [t_us]
                    return go

                def u_fct(icc, dt):
                    def go():
                        bpw = buf("pw", dt % 2)
                        t_l = p.dma("gq", lambda e: e.dma_start(
                            out=pwb[dt % 2][:],
                            in_=wfcT[dt * 128:(dt + 1) * 128,
                                     icc * 512:(icc + 1) * 512]),
                            deps=bpw.wdeps(), tag=f"fct{icc}_{dt}")
                        bpw.wrote(t_l)
                        brb = buf("rb512")
                        bpt = buf("stg", dt % 2)
                        t_m = brb.read(p.ve(lambda e: e.tensor_tensor(
                            out=pwb[dt % 2][:], in0=pwb[dt % 2][:],
                            in1=rb512[:], op=ALU.mult), deps=[t_l, cell["be1"]]))
                        t_rn = p.ve(lambda e: e.tensor_scalar(
                            out=pwb[dt % 2][:], in0=pwb[dt % 2][:], scalar1=MAGIC,
                            scalar2=MAGIC, op0=ALU.add, op1=ALU.subtract),
                            deps=[t_m])
                        t_cl = bpw.read(p.ve(lambda e: e.tensor_scalar(
                            out=stgb[dt % 2][:], in0=pwb[dt % 2][:], scalar1=1.0,
                            scalar2=-1.0, op0=ALU.min, op1=ALU.max),
                            deps=bpt.wdeps([t_rn])))
                        bpt.wrote(t_cl)
                        t_st2 = p.dma("gq", lambda e: e.dma_start(
                            out=w1s[1][:, icc * 4096 + dt * 512:
                                       icc * 4096 + (dt + 1) * 512],
                            in_=stgb[dt % 2][:]), deps=[t_cl],
                            tag=f"fcts{icc}_{dt}")
                        bpt.read(t_st2)
                        return [t_st2]
                    return go

                def u_bc2():
                    def go():
                        t_row2 = p.dma("sq", lambda e: e.dma_start(
                            out=s2row[:], in_=s2r_scr[0:1, 0:512]),
                            deps=bs2.wdeps(st["rowd2"]), tag=f"s2rowa{si}")
                        bs2.wrote(t_row2)
                        t_bc2 = mm_group("OB", 1, [
                            (lambda: ones_row_f[:], lambda: s2row[:], None)],
                            deps=[t_row2], tag=f"bc2a{si}")
                        bs2.read(t_bc2)
                        brb = buf("rb512")
                        t_be2 = reg("OB", 1).read(p.ve(lambda e: e.tensor_copy(
                            out=rb512[:], in_=poB[:, 512:1024]),
                            deps=brb.wdeps([t_bc2])))
                        brb.wrote(t_be2)
                        t_row3 = p.dma("sq", lambda e: e.dma_start(
                            out=s2row[:], in_=s2r_scr[0:1, 512:1024]),
                            deps=bs2.wdeps(), tag=f"s2rowb{si}")
                        bs2.wrote(t_row3)
                        t_bc3 = mm_group("OB", 0, [
                            (lambda: ones_row_f[:], lambda: s2row[:], None)],
                            deps=[t_row3], tag=f"bc2b{si}")
                        bs2.read(t_bc3)
                        brc = buf("rb512b")
                        t_be3 = reg("OB", 0).read(p.ve(lambda e: e.tensor_copy(
                            out=rb5122[:], in_=poB[:, 0:512]),
                            deps=brc.wdeps([t_bc3])))
                        brc.wrote(t_be3)
                        cell["be2"] = t_be2
                        cell["be3"] = t_be3
                        return []
                    return go

                def u_pjt(jt, hhalf, bi_):
                    def go():
                        brx = buf("rb512") if hhalf == 0 else buf("rb512b")
                        rbsel = rb512 if hhalf == 0 else rb5122
                        t_bex = cell["be2"] if hhalf == 0 else cell["be3"]
                        bpw = buf("pw", bi_)
                        t_l = p.dma("gq", lambda e: e.dma_start(
                            out=pwb[bi_][:],
                            in_=wprojT[jt * 128:(jt + 1) * 128,
                                       hhalf * 512:(hhalf + 1) * 512]),
                            deps=bpw.wdeps(), tag=f"pjt{jt}_{hhalf}")
                        bpw.wrote(t_l)
                        bpt = buf("stg", bi_)
                        t_m1 = brx.read(p.ve(lambda e: e.tensor_tensor(
                            out=pwb[bi_][:], in0=pwb[bi_][:],
                            in1=rbsel[:], op=ALU.mult), deps=[t_l, t_bex]))
                        t_r1 = p.ve(lambda e: e.tensor_scalar(
                            out=pwb[bi_][:], in0=pwb[bi_][:], scalar1=MAGIC,
                            scalar2=MAGIC, op0=ALU.add, op1=ALU.subtract),
                            deps=[t_m1])
                        t_c1 = bpw.read(p.ve(lambda e: e.tensor_scalar(
                            out=stgb[bi_][:], in0=pwb[bi_][:], scalar1=1.0,
                            scalar2=-1.0, op0=ALU.min, op1=ALU.max),
                            deps=bpt.wdeps([t_r1])))
                        bpt.wrote(t_c1)
                        t_st2 = p.dma("gq", lambda e: e.dma_start(
                            out=w2s[1].rearrange(
                                "p (ft it f) -> p ft it f", ft=8, it=32)[
                                :, hhalf * 4:(hhalf + 1) * 4, jt, :],
                            in_=stgb[bi_].ap().rearrange(
                                "p (ff f) -> p ff f", ff=4)),
                            deps=[t_c1], tag=f"pjts{jt}_{hhalf}")
                        bpt.read(t_st2)
                        return [t_st2]
                    return go

                for icc in (2 * si, 2 * si + 1):
                    units.append(u_bc1(icc))
                    units.append(u_uts(icc))
                    for dt in range(8):
                        units.append(u_fct(icc, dt))
                units.append(u_bc2())
                ji = 0
                for jt in range(8 * si, 8 * si + 8):
                    for hhalf in (0, 1):
                        units.append(u_pjt(jt, hhalf, ji % 2))
                        ji += 1
                return units

            return job_scales, job_tern


        def tap_dump(name, src_ap, deps):
            if name in tap_aps:
                return [p.dma("gq", lambda e: e.dma_start(out=tap_aps[name], in_=src_ap),
                              deps=deps, tag="tap_" + name)]
            return []

        final_stores = []
        all_taps = []
        prev_xsts = []
        ln1_toks = []
        kv1_stores = []
        for L in range(n_layers):
            xsrc = xin if L == 0 else x_scr
            xdst = (y if n_layers == 1 else x_scr) if L == 0 else y
            wprep = L0W_TOKS if L == 0 else prep1_list
            if L == 0 or not ln1_toks:
                ln_toks = []
                kv_stores = []
                for c in range(NCHUNK):
                    extra = [prev_xsts[c]] if (L == 1 and prev_xsts) else []
                    t_lnst, _ = phase_ln(L, c, xsrc, extra_deps=extra)
                    ln_toks.append(t_lnst)
                    kv_stores += phase_kv(L, c, ln_toks[c])
            else:
                ln_toks = ln1_toks
                kv_stores = kv1_stores
            tapd = tap_dump(f"xn{L}", xn_scr, ln_toks)
            tapd += tap_dump(f"k{L}", KS[L], kv_stores)
            tapd += tap_dump(f"vp{L}", VPS[L], kv_stores)
            all_taps += tapd
            prep_tern = None
            if L == 0 and n_layers == 2:
                job_scales, prep_tern = make_l1_prep()
                fillq.extend(job_scales())
            xsts = []
            for c in range(NCHUNK):
                par = c % 2
                bxn = buf("xnb", par)
                t_xn = p.dma("wq", lambda e, c=c, par=par: e.dma_start(
                    out=xnb[par][:], in_=xn_scr[:, c * 4096:(c + 1) * 4096]),
                    deps=bxn.wdeps([ln_toks[c]]), tag=f"L{L}c{c}:xn3")
                bxn.wrote(t_xn)
                s_toks = phase_attn(L, c, kv_stores + tapd + VPPAD, t_xn)
                if L == 1:
                    pump_fill(len(fillq))
                t_xst, comb = phase_ffn(
                    L, c, s_toks, t_xn, xsrc, xdst, list(wprep),
                    x_extra=([prev_xsts[c]] if (L == 1 and prev_xsts) else []))
                xsts.append(t_xst)
                if L == 0 and n_layers == 2:
                    t_st1, t_xn1 = phase_ln(1, c, None, x_res=comb,
                                            st_extra=[t_xn])
                    ln1_toks.append(t_st1)
                    kv1_stores += phase_kv(1, c, t_st1, xn_res=t_xn1)
                if prep_tern is not None:
                    fillq.extend(prep_tern(c))
            if L == 0 and n_layers == 2 and "x0" in tap_aps:
                t_tap = p.dma("gq", lambda e: e.dma_start(
                    out=tap_aps["x0"], in_=x_scr), deps=list(xsts), tag="tapx0")
                all_taps.append(t_tap)
            if L == 0 and n_layers == 2 and "tfcT" in tap_aps:
                all_taps.append(p.dma("gq", lambda e: e.dma_start(
                    out=tap_aps["tfcT"], in_=w1s[1]), deps=list(prep1_list), tag="taptf"))
            if L == 0 and n_layers == 2 and "tpjT" in tap_aps:
                all_taps.append(p.dma("gq", lambda e: e.dma_start(
                    out=tap_aps["tpjT"], in_=w2s[1]), deps=list(prep1_list), tag="taptp"))
            prev_xsts = xsts
            final_stores = xsts

        p.po(lambda e: e.nop(), deps=final_stores + all_taps, tag="fence")

        p.finalize()
        p.replay(nc, sems)
        nc.all_engine_barrier()
    return nc


_NC_CACHE = {}


def _get_nc(n_layers=2, taps=()):
    key = (n_layers, tuple(taps))
    if key not in _NC_CACHE:
        _NC_CACHE[key] = build_nc(n_layers, taps)
    return _NC_CACHE[key]


def make_in_maps(inputs):
    perm = _parity_perm()
    wqkv = np.asarray(inputs["L1_attn_w_qkv"], np.float32)
    wq1 = wqkv[0:D][perm]
    wk1 = wqkv[D:2 * D][perm]
    wv1 = wqkv[2 * D:3 * D]
    wqkvT = np.ascontiguousarray(np.concatenate([wq1, wk1, wv1], 0).T)
    T1, T2 = _rope_tables()
    R1T, R2T = _rot_mats()
    wfc = np.ascontiguousarray(np.asarray(inputs["L1_ffn_w_fc"], np.float32))
    wproj = np.ascontiguousarray(np.asarray(inputs["L1_ffn_w_proj"], np.float32))
    common = {
        "wqT0": np.ascontiguousarray(np.asarray(inputs["L0_attn_Wq"], np.float32).T),
        "wkT0": np.ascontiguousarray(np.asarray(inputs["L0_attn_Wk"], np.float32).T),
        "wvT0": np.ascontiguousarray(np.asarray(inputs["L0_attn_Wv"], np.float32).T),
        "w1": np.ascontiguousarray(np.asarray(inputs["L0_ffn_W1"], np.float32)),
        "b1": np.ascontiguousarray(np.asarray(inputs["L0_ffn_b1"], np.float32).reshape(32, 128).T),
        "w2": np.ascontiguousarray(np.asarray(inputs["L0_ffn_W2"], np.float32)),
        "b2": np.ascontiguousarray(np.asarray(inputs["L0_ffn_b2"], np.float32).reshape(8, 128).T),
        "wqkvT": wqkvT,
        "woutT": np.ascontiguousarray(np.asarray(inputs["L1_attn_w_out"], np.float32).T),
        "bout": np.ascontiguousarray(np.asarray(inputs["L1_attn_b_out"], np.float32).reshape(8, 128).T),
        "wfc": wfc,
        "wfcT": np.ascontiguousarray(wfc.T),
        "wproj": wproj,
        "wprojT": np.ascontiguousarray(wproj.T),
        "uT": np.ascontiguousarray(np.asarray(inputs["L1_ffn_u"], np.float32).T),
        "vT": np.ascontiguousarray(np.asarray(inputs["L1_ffn_v"], np.float32).T),
        "fbT": np.ascontiguousarray(np.asarray(inputs["L1_ffn_fast_b"], np.float32).T),
        "gate": np.asarray(inputs["L1_ffn_gate"], np.float32).reshape(1),
        "t1c": T1, "t2c": T2, "r1T": R1T, "r2T": R2T,
        "indm": _indmat(),
    }
    x = np.asarray(inputs["L0_attn_x"], np.float32)
    in_maps = []
    for i in range(8):
        m = dict(common)
        xt = x[i].reshape(NCHUNK, CH, 8, 128)
        m["x"] = np.ascontiguousarray(xt.transpose(3, 0, 2, 1).reshape(128, NCHUNK * 4096))
        in_maps.append(m)
    return in_maps


def kernel(**inputs):
    in_maps = make_in_maps(inputs)
    nc = _get_nc()
    res = run_bass_kernel_spmd(nc, in_maps, core_ids=list(range(8)))
    outs = []
    for r in res.results:
        yt = np.asarray(r["y"]).reshape(128, NCHUNK, 8, CH)
        outs.append(yt.transpose(1, 3, 2, 0).reshape(S, D))
    return np.stack(outs, 0).astype(np.float32)
